# version 06
# baseline (speedup 1.0000x reference)
import numpy as np
import ml_dtypes
from contextlib import ExitStack

import concourse.bass as bass
import concourse.mybir as mybir
from concourse.bass import IndirectOffsetOnAxis
from concourse.bass_utils import run_bass_kernel_spmd

F32 = mybir.dt.float32
BF16 = mybir.dt.bfloat16
I32 = mybir.dt.int32
AF = mybir.ActivationFunctionType
ALU = mybir.AluOpType
AX = mybir.AxisListType
BF = ml_dtypes.bfloat16

NCORES = 8
T = 1024
D = 4096
HD = 128
NH = 32
NKV = 8
KVD = NKV * HD
S = 4096
NEG = -1.0e30
EPS = 1e-5

cc = np.ascontiguousarray


def build_prep0(stage=3, nqk=40):
    nc = bass.Bass("TRN2", target_bir_lowering=False)
    X = nc.dram_tensor("x", [T, D], F32, kind="ExternalInput").ap()
    WQ = nc.dram_tensor("wq_t", [NH, D, HD], F32, kind="ExternalInput").ap()
    WK = nc.dram_tensor("wk_t", [NKV, D, HD], F32, kind="ExternalInput").ap()
    WV = nc.dram_tensor("wv", [D, KVD], F32, kind="ExternalInput").ap()
    CQ = nc.dram_tensor("cq", [HD, T], F32, kind="ExternalInput").ap()
    SQ = nc.dram_tensor("sq", [HD, T], F32, kind="ExternalInput").ap()
    CK = nc.dram_tensor("ck", [HD, T], F32, kind="ExternalInput").ap()
    SK = nc.dram_tensor("sk", [HD, T], F32, kind="ExternalInput").ap()
    EYE = nc.dram_tensor("eye", [128, 128], F32, kind="ExternalInput").ap()
    XNT = nc.dram_tensor("xnT", [D, T], BF16, kind="ExternalOutput").ap()
    QT = nc.dram_tensor("qT", [D, T], BF16, kind="ExternalOutput").ap()
    KT = nc.dram_tensor("kT", [KVD, T], BF16, kind="ExternalOutput").ap()
    VO = nc.dram_tensor("v", [T, KVD], BF16, kind="ExternalOutput").ap()

    qk_proj = ([("q", h) for h in range(NH)] + [("k", g) for g in range(NKV)])[:nqk]
    NQK = len(qk_proj)
    NVG = 32

    with ExitStack() as st:
        sb = lambda *a: st.enter_context(nc.sbuf_tensor(*a))
        ps = lambda *a: st.enter_context(nc.psum_tensor(*a))
        xt = [sb(f"xt{i}", [128, D], F32) for i in range(2)]
        junk = sb("junk", [128, D], BF16)
        stat = sb("stat", [128, 8], F32)
        xn = sb("xn", [128, D], BF16)
        xnT = sb("xnTs", [128, 32, T], BF16)
        eye = sb("eyeb", [128, 128], BF16)
        cqs = sb("cqs", [128, T], F32)
        sqs = sb("sqs", [128, T], F32)
        cks = sb("cks", [128, T], F32)
        sks = sb("sks", [128, T], F32)
        wbuf = [sb(f"wbuf{i}", [128, 32, 128], BF16) for i in range(2)]
        wvb = [sb(f"wvb{i}", [128, 32, 256], BF16) for i in range(2)]
        rt1 = sb("rt1", [128, 512], F32)
        rt2 = sb("rt2", [128, 512], F32)
        ro = [sb(f"ro{i}", [128, 512], BF16) for i in range(2)]
        vout = [sb(f"vout{i}", [128, 256], BF16) for i in range(2)]
        ptp = [ps(f"ptp{i}", [128, 128], BF16) for i in range(2)]
        pmm = [ps(f"pmm{i}", [128, 512], F32) for i in range(2)]
        lt = st.enter_context(nc.semaphore("lt"))
        lx = [st.enter_context(nc.semaphore(f"lx{i}")) for i in range(2)]
        ley = st.enter_context(nc.semaphore("ley"))
        lwq = [st.enter_context(nc.semaphore(f"lwq{i}")) for i in range(2)]
        lwv = [st.enter_context(nc.semaphore(f"lwv{i}")) for i in range(2)]
        sxn = st.enter_context(nc.semaphore("sxn"))
        sqk = [st.enter_context(nc.semaphore(f"sqk{i}")) for i in range(2)]
        svv = [st.enter_context(nc.semaphore(f"svv{i}")) for i in range(2)]
        s_ln = st.enter_context(nc.semaphore("s_ln"))
        v_ln = st.enter_context(nc.semaphore("v_ln"))
        tp = st.enter_context(nc.semaphore("tp"))
        tc = st.enter_context(nc.semaphore("tc"))
        mm = st.enter_context(nc.semaphore("mm"))
        ev = st.enter_context(nc.semaphore("ev"))
        so = st.enter_context(nc.semaphore("so"))
        block = st.enter_context(nc.Block())

        @block.sync
        def _(sy):
            sy.dma_start(out=cqs[:], in_=CQ).then_inc(lt, 16)
            sy.dma_start(out=sqs[:], in_=SQ).then_inc(lt, 16)
            sy.dma_start(out=cks[:], in_=CK).then_inc(lt, 16)
            sy.dma_start(out=sks[:], in_=SK).then_inc(lt, 16)
            for i in range(8):
                if i >= 2:
                    sy.wait_ge(s_ln, 4 * i - 4)
                    sy.wait_ge(v_ln, 7 * i - 13)
                sy.dma_start(out=xt[i % 2][:], in_=X[128 * i:128 * (i + 1), :]).then_inc(lx[i % 2], 16)
            sy.wait_ge(tc, 256)
            for j in range(32):
                sy.dma_start(out=XNT[128 * j:128 * (j + 1), :], in_=xnT[:, j, :]).then_inc(sxn, 16)
            if stage < 2:
                sy.wait_ge(so, 16 * 32)
                return
            for G in range(2 * NQK):
                pi, th = G // 2, G % 2
                kind, idx = qk_proj[pi]
                sy.wait_ge(ev, G + 1)
                dst = QT if kind == "q" else KT
                sy.dma_start(out=dst[128 * idx:128 * (idx + 1), 512 * th:512 * (th + 1)],
                             in_=ro[G % 2][:]).then_inc(sqk[G % 2], 16)
            if stage < 3:
                sy.wait_ge(so, 16 * (32 + 2 * NQK))
                return
            for h in range(NVG):
                hf, tt = h // 8, h % 8
                sy.wait_ge(ev, 2 * NQK + h + 1)
                sy.dma_start(out=VO[128 * tt:128 * (tt + 1), 256 * hf:256 * (hf + 1)],
                             in_=vout[h % 2][:]).then_inc(svv[h % 2], 16)
            sy.wait_ge(sxn, 16 * 32)
            sy.wait_ge(sqk[0], 16 * NQK)
            sy.wait_ge(sqk[1], 16 * NQK)
            sy.wait_ge(svv[0], 16 * (NVG // 2))
            sy.wait_ge(svv[1], 16 * (NVG // 2))

        @block.gpsimd
        def _(g):
            g.dma_start(out=eye[:], in_=EYE).then_inc(ley, 16)
            if stage < 2:
                return
            for pi, (kind, idx) in enumerate(qk_proj):
                if pi >= 2:
                    g.wait_ge(mm, 2 * pi - 2)
                src = WQ[idx] if kind == "q" else WK[idx]
                g.dma_start(out=wbuf[pi % 2][:],
                            in_=src.rearrange("(j p) n -> p j n", p=128)).then_inc(lwq[pi % 2], 16)
            if stage < 3:
                return
            for hf in range(4):
                if hf >= 2:
                    g.wait_ge(mm, 2 * NQK + 8 * (hf - 1))
                g.dma_start(out=wvb[hf % 2][:],
                            in_=WV[:, 256 * hf:256 * (hf + 1)].rearrange("(j p) n -> p j n", p=128)
                            ).then_inc(lwv[hf % 2], 16)

        @block.scalar
        def _(s):
            for i in range(8):
                s.wait_ge(lx[i % 2], 16 * (i // 2 + 1))
                s.activation(junk[:], xt[i % 2][:], AF.Square,
                             accum_out=stat[:, 1:2]).then_inc(s_ln, 1)
                s.wait_ge(v_ln, 7 * i + 5)
                s.activation(stat[:, 3:4], stat[:, 4:5], AF.Ln).then_inc(s_ln, 1)
                s.wait_ge(v_ln, 7 * i + 6)
                s.activation(stat[:, 5:6], stat[:, 4:5], AF.Exp).then_inc(s_ln, 1)
                s.wait_ge(v_ln, 7 * i + 7)
                s.wait_ge(s_ln, 4 * i + 3)
                if i > 0:
                    s.wait_ge(tp, 32 * i)
                s.activation(xn[:], xt[i % 2][:], AF.Identity, scale=stat[:, 5:6],
                             bias=stat[:, 6:7]).then_inc(s_ln, 1)
            if stage < 3:
                return
            for h in range(NVG):
                s.wait_ge(mm, 2 * NQK + h + 1)
                if h >= 2:
                    s.wait_ge(svv[h % 2], 16 * (h // 2))
                s.activation(vout[h % 2][:], pmm[(2 * NQK + h) % 2][:, 0:256], AF.Copy).then_inc(ev, 1)

        @block.vector
        def _(v):
            for i in range(8):
                v.wait_ge(s_ln, 4 * i + 1)
                v.wait_ge(lx[i % 2], 16 * (i // 2 + 1))
                v.reduce_sum(stat[:, 0:1], xt[i % 2][:], axis=AX.X).then_inc(v_ln, 1)
                v.wait_ge(v_ln, 7 * i + 1)
                v.tensor_scalar(stat[:, 2:3], stat[:, 0:1], 1.0 / D, None, ALU.mult).then_inc(v_ln, 1)
                v.wait_ge(v_ln, 7 * i + 2)
                v.tensor_tensor(stat[:, 7:8], stat[:, 2:3], stat[:, 2:3], ALU.mult).then_inc(v_ln, 1)
                v.wait_ge(v_ln, 7 * i + 3)
                v.scalar_tensor_tensor(stat[:, 3:4], stat[:, 1:2], 1.0 / D, stat[:, 7:8],
                                       ALU.mult, ALU.subtract).then_inc(v_ln, 1)
                v.wait_ge(v_ln, 7 * i + 4)
                v.tensor_scalar(stat[:, 4:5], stat[:, 3:4], EPS, None,
                                ALU.add).then_inc(v_ln, 1)
                v.wait_ge(s_ln, 4 * i + 2)
                v.tensor_scalar(stat[:, 4:5], stat[:, 3:4], -0.5, None,
                                ALU.mult).then_inc(v_ln, 1)
                v.wait_ge(s_ln, 4 * i + 3)
                v.scalar_tensor_tensor(stat[:, 6:7], stat[:, 2:3], -1.0, stat[:, 5:6],
                                       ALU.mult, ALU.mult).then_inc(v_ln, 1)
                for j in range(32):
                    n = 32 * i + j
                    v.wait_ge(tp, n + 1)
                    v.tensor_copy(xnT[:, j, 128 * i:128 * (i + 1)], ptp[n % 2][:]).then_inc(tc, 1)
            if stage < 2:
                return
            v.wait_ge(lt, 64)
            for G in range(2 * NQK):
                pi, th = G // 2, G % 2
                kind, _ = qk_proj[pi]
                Ct = cqs if kind == "q" else cks
                St = sqs if kind == "q" else sks
                sl = slice(512 * th, 512 * (th + 1))
                p = pmm[G % 2]
                v.wait_ge(mm, G + 1)
                if G >= 2:
                    v.wait_ge(sqk[G % 2], 16 * (G // 2))
                if G >= 1:
                    v.wait_ge(ev, G)
                vb = 56 + 3 * G
                v.tensor_tensor(rt1[:], p[:], Ct[:, sl], ALU.mult).then_inc(v_ln, 1)
                v.tensor_tensor(rt2[0:64, :], p[64:128, :], St[0:64, sl], ALU.mult).then_inc(v_ln, 1)
                v.tensor_tensor(rt2[64:128, :], p[0:64, :], St[64:128, sl], ALU.mult).then_inc(v_ln, 1)
                v.wait_ge(v_ln, vb + 3)
                v.tensor_tensor(ro[G % 2][:], rt1[:], rt2[:], ALU.add).then_inc(ev, 1)

        @block.tensor
        def _(t):
            t.wait_ge(ley, 16)
            for i in range(8):
                t.wait_ge(s_ln, 4 * i + 4)
                for j in range(32):
                    n = 32 * i + j
                    if n >= 2:
                        t.wait_ge(tc, n - 1)
                    t.transpose(ptp[n % 2][:], xn[:, 128 * j:128 * (j + 1)], eye[:]).then_inc(tp, 1)
            t.wait_ge(tc, 256)
            if stage < 2:
                return
            for G in range(2 * NQK):
                pi, th = G // 2, G % 2
                if th == 0:
                    t.wait_ge(lwq[pi % 2], 16 * (pi // 2 + 1))
                if G >= 2:
                    t.wait_ge(ev, G - 1)
                for j in range(32):
                    inst = t.matmul(pmm[G % 2][:], wbuf[pi % 2][:, j, :],
                                    xnT[:, j, 512 * th:512 * (th + 1)],
                                    start=(j == 0), stop=(j == 31))
                inst.then_inc(mm, 1)
            if stage < 3:
                return
            for h in range(NVG):
                hf, tt = h // 8, h % 8
                G = 2 * NQK + h
                if tt == 0:
                    t.wait_ge(lwv[hf % 2], 16 * (hf // 2 + 1))
                t.wait_ge(ev, G - 1)
                for j in range(32):
                    inst = t.matmul(pmm[G % 2][:, 0:256], xnT[:, j, 128 * tt:128 * (tt + 1)],
                                    wvb[hf % 2][:, j, :], start=(j == 0), stop=(j == 31))
                inst.then_inc(mm, 1)
    return nc


def host_prep0_inputs(x_full2d, wq, wk, wv):
    perm_d = np.concatenate([np.arange(0, HD, 2), np.arange(1, HD, 2)])
    colp_q = np.concatenate([HD * h + perm_d for h in range(NH)])
    colp_k = np.concatenate([HD * g + perm_d for g in range(NKV)])
    wq_t = cc(wq[:, colp_q].reshape(D, NH, HD).transpose(1, 0, 2))
    wk_t = cc(wk[:, colp_k].reshape(D, NKV, HD).transpose(1, 0, 2))
    wv_c = cc(wv)
    eye = np.eye(128, dtype=np.float32)
    inv = 10000.0 ** (-np.arange(64) / 64.0)
    maps = []
    for c in range(NCORES):
        pos = np.arange(1024 * (c % 4), 1024 * (c % 4) + 1024, dtype=np.float64)
        ang = pos[:, None] * inv[None, :]
        cos = np.cos(ang).astype(np.float32)
        sin = np.sin(ang).astype(np.float32)
        Ct = cc(np.concatenate([cos, cos], 1).T)
        St = cc(np.concatenate([-sin, sin], 1).T)
        sc = np.float32(1.0 / np.sqrt(HD))
        maps.append(dict(
            x=cc(x_full2d[T * c:T * (c + 1)]),
            wq_t=wq_t, wk_t=wk_t, wv=wv_c,
            cq=Ct * sc, sq=St * sc, ck=Ct, sk=St, eye=eye,
        ))
    return maps




def build_attn0():
    nc = bass.Bass("TRN2", target_bir_lowering=False)
    QTF = nc.dram_tensor("qTf", [128, 8, S], BF16, kind="ExternalInput").ap()
    KTF = nc.dram_tensor("kTf", [128, 2, S], BF16, kind="ExternalInput").ap()
    VA = nc.dram_tensor("vA", [128, 32, 2, 129], BF16, kind="ExternalInput").ap()
    MSK = nc.dram_tensor("msk", [128, 4, 512], F32, kind="ExternalInput").ap()
    CTX = nc.dram_tensor("ctx", [128, 32, 8, 128], BF16, kind="ExternalOutput").ap()

    tiles = []
    for h in range(8):
        for qc in range(8):
            nkt = 4 * qc + 4
            for kt in range(nkt):
                tiles.append((h, qc, kt, nkt, h * 8 + qc))
    NT = len(tiles)
    cumtiles = {}
    n = 0
    for h in range(8):
        for qc in range(8):
            n += 4 * qc + 4
            cumtiles[h * 8 + qc] = n
    masked_idx = {}
    mi = 0
    for n_, (h, qc, kt, nkt, g) in enumerate(tiles):
        if kt >= nkt - 4:
            masked_idx[n_] = mi
            mi += 1
    NMSK = mi

    with ExitStack() as st:
        sb = lambda *a: st.enter_context(nc.sbuf_tensor(*a))
        ps = lambda *a: st.enter_context(nc.psum_tensor(*a))
        qT = sb("qTs", [128, 8, S], BF16)
        kT = sb("kTs", [128, 2, S], BF16)
        vA = sb("vAs", [128, 32, 2, 129], BF16)
        msk = sb("msks", [128, 4, 512], F32)
        expb = [sb(f"expb{i}", [128, 512], BF16) for i in range(2)]
        sxp = [sb(f"sxp{i}", [128, 512], F32) for i in range(2)]
        rcp = sb("rcp", [128, 4], F32)
        ctx = sb("ctxs", [128, 32, 8, 128], BF16)
        scp = [ps(f"scp{i}", [128, 512], F32) for i in range(2)]
        ctp = [ps(f"ctp{i}", [128, 129], F32) for i in range(4)]
        lds = st.enter_context(nc.semaphore("lds"))
        mm = st.enter_context(nc.semaphore("mm"))
        avm = st.enter_context(nc.semaphore("avm"))
        exv = st.enter_context(nc.semaphore("exv"))
        vx = st.enter_context(nc.semaphore("vx"))
        vrc = st.enter_context(nc.semaphore("vrc"))
        ctev = st.enter_context(nc.semaphore("ctev"))
        so = st.enter_context(nc.semaphore("so"))
        block = st.enter_context(nc.Block())

        @block.sync
        def _(sy):
            sy.dma_start(out=qT[:], in_=QTF).then_inc(lds, 16)
            sy.dma_start(out=kT[:], in_=KTF).then_inc(lds, 16)
            sy.dma_start(out=vA[:], in_=VA).then_inc(lds, 16)
            sy.dma_start(out=msk[:], in_=MSK).then_inc(lds, 16)
            sy.wait_ge(ctev, 64)
            sy.dma_start(out=CTX, in_=ctx[:]).then_inc(so, 16)
            sy.wait_ge(so, 16)

        @block.tensor
        def _(t):
            t.wait_ge(lds, 64)
            for n_, (h, qc, kt, nkt, g) in enumerate(tiles):
                if n_ >= 2:
                    t.wait_ge(exv, n_ - 1)
                t.matmul(scp[n_ % 2][:], kT[:, h // 4, 128 * kt:128 * (kt + 1)],
                         qT[:, h, 512 * qc:512 * (qc + 1)],
                         start=True, stop=True).then_inc(mm, 1)
                if n_ >= 1:
                    _emit_av(t, tiles[n_ - 1], n_ - 1, expb, vA, ctp, exv, ctev, avm)
            _emit_av(t, tiles[NT - 1], NT - 1, expb, vA, ctp, exv, ctev, avm)

        @block.vector
        def _(v):
            nv = 0
            for n_, (h, qc, kt, nkt, g) in enumerate(tiles):
                if kt >= nkt - 4:
                    r = kt - 4 * qc
                    v.wait_ge(mm, n_ + 1)
                    if n_ >= 2:
                        v.wait_ge(exv, n_ - 1)
                    v.tensor_tensor(sxp[n_ % 2][:], scp[n_ % 2][:], msk[:, r, :],
                                    ALU.add).then_inc(vx, 1)
                if kt == nkt - 1:
                    v.wait_ge(avm, cumtiles[g])
                    if g >= 1:
                        v.wait_ge(ctev, g)
                    for qs4 in range(4):
                        v.reciprocal(rcp[:, qs4:qs4 + 1],
                                     ctp[qs4][:, 128:129]).then_inc(vrc, 1)

        @block.scalar
        def _(s):
            for n_, (h, qc, kt, nkt, g) in enumerate(tiles):
                if n_ >= 2:
                    s.wait_ge(avm, n_ - 1)
                if kt >= nkt - 4:
                    s.wait_ge(vx, masked_idx[n_] + 1)
                    s.activation(expb[n_ % 2][:], sxp[n_ % 2][:], AF.Exp).then_inc(exv, 1)
                else:
                    s.wait_ge(mm, n_ + 1)
                    s.activation(expb[n_ % 2][:], scp[n_ % 2][:], AF.Exp).then_inc(exv, 1)
                if kt == nkt - 1:
                    s.wait_ge(vrc, 4 * (g + 1))
                    qt_global = 4 * qc
                    for qs4 in range(4):
                        inst = s.activation(ctx[:, qt_global + qs4, h, :],
                                            ctp[qs4][:, 0:128], AF.Copy,
                                            scale=rcp[:, qs4:qs4 + 1])
                    inst.then_inc(ctev, 1)

    return nc


def _emit_av(t, tile, m, expb, vA, ctp, exv, ctev, avm):
    h, qc, kt, nkt, g = tile
    t.wait_ge(exv, m + 1)
    if kt == 0 and g >= 1:
        t.wait_ge(ctev, g)
    for qs4 in range(4):
        inst = t.matmul(ctp[qs4][:], expb[m % 2][:, 128 * qs4:128 * (qs4 + 1)],
                        vA[:, kt, h // 4, :], start=(kt == 0), stop=(kt == nkt - 1))
    inst.then_inc(avm, 1)


def host_attn0_inputs(qT_cores, kT_cores, v_cores):
    maps = []
    qT_b = [np.concatenate(qT_cores[4 * b:4 * b + 4], axis=1) for b in range(2)]
    kT_b = [np.concatenate(kT_cores[4 * b:4 * b + 4], axis=1) for b in range(2)]
    v_b = [np.concatenate(v_cores[4 * b:4 * b + 4], axis=0) for b in range(2)]
    kl = np.arange(128)[:, None]
    ql = np.arange(512)[None, :]
    m4 = np.stack([np.where(128 * r + kl > ql, NEG, 0.0) for r in range(4)], 0).astype(np.float32)
    msk = cc(m4.transpose(1, 0, 2))
    for i in range(NCORES):
        b, m = i // 4, i % 4
        qTf = cc(qT_b[b][128 * 8 * m:128 * 8 * (m + 1)].reshape(8, 128, S).transpose(1, 0, 2))
        kTf = cc(kT_b[b][256 * m:256 * (m + 1)].reshape(2, 128, S).transpose(1, 0, 2))
        vA = np.ones((128, 32, 2, 129), dtype=BF)
        for gl in range(2):
            vg = v_b[b][:, 128 * (2 * m + gl):128 * (2 * m + gl + 1)]
            vA[:, :, gl, 0:128] = vg.reshape(32, 128, 128).transpose(1, 0, 2)
        maps.append(dict(qTf=qTf, kTf=kTf, vA=cc(vA), msk=msk))
    return maps


def host_attn0_ctx(results):
    ctx_b = [np.zeros((4096, 4096), dtype=BF) for _ in range(2)]
    for i in range(NCORES):
        b, m = i // 4, i % 4
        arr = np.asarray(results[i]["ctx"])
        ctx_b[b][:, 1024 * m:1024 * (m + 1)] = arr.transpose(1, 0, 2, 3).reshape(4096, 1024)
    return ctx_b


ARO = [0, 4, 1, 5, 2, 6, 3, 7]


def lperm_global(ncol_percore):
    return np.array([8 * (ncol_percore * c + col) + p
                     for c in range(8) for p in range(8) for col in range(ncol_percore)])


def build_dequant():
    nc = bass.Bass("TRN2", target_bir_lowering=False)
    GQ = nc.dram_tensor("gq", [4096, 172], I32, kind="ExternalInput").ap()
    GS = nc.dram_tensor("gs", [32, 1376], F32, kind="ExternalInput").ap()
    GZ = nc.dram_tensor("gz", [32, 172], I32, kind="ExternalInput").ap()
    UQ = nc.dram_tensor("uq", [4096, 172], I32, kind="ExternalInput").ap()
    US = nc.dram_tensor("us", [32, 1376], F32, kind="ExternalInput").ap()
    UZ = nc.dram_tensor("uz", [32, 172], I32, kind="ExternalInput").ap()
    DQ = nc.dram_tensor("dq", [11008, 64], I32, kind="ExternalInput").ap()
    DSE = nc.dram_tensor("dse", [11008, 512], F32, kind="ExternalInput").ap()
    DZE = nc.dram_tensor("dze", [11008, 64], I32, kind="ExternalInput").ap()
    ONE = nc.dram_tensor("one", [1, 128], F32, kind="ExternalInput").ap()
    WG = nc.dram_tensor("Wg", [4096, 1376], BF16, kind="ExternalOutput").ap()
    WU = nc.dram_tensor("Wu", [4096, 1376], BF16, kind="ExternalOutput").ap()
    WD = nc.dram_tensor("Wd", [11008, 512], BF16, kind="ExternalOutput").ap()

    mats = [("g", GQ, GS, GZ, WG), ("u", UQ, US, UZ, WU)]
    units = [(mi, g) for mi in range(2) for g in range(32)]
    NU = len(units)
    ND = 86
    C, N = 172, 1376

    with ExitStack() as st:
        sb = lambda *a: st.enter_context(nc.sbuf_tensor(*a))
        ps = lambda *a: st.enter_context(nc.psum_tensor(*a))
        qw_sb = [sb("gqs", [128, 32, 172], I32), sb("uqs", [128, 32, 172], I32),
                 sb("dqs", [128, 86, 64], I32)]
        ones = sb("ones", [1, 128], F32)
        zqr = [sb(f"zqr{i}", [1, 172], I32) for i in range(2)]
        srow = [sb(f"srow{i}", [1, 1376], F32) for i in range(2)]
        znib = sb("znib", [1, 1376], I32)
        zb = sb("zb", [1, 1376], F32)
        nibi = sb("nibi", [128, 1376], I32)
        nibb = sb("nibb", [128, 1376], BF16)
        tb = sb("tb", [128, 1376], F32)
        wout = [sb(f"wout{i}", [128, 1376], BF16) for i in range(2)]
        dzt = [sb(f"dzt{i}", [128, 64], I32) for i in range(2)]
        dst = [sb(f"dst{i}", [128, 512], F32) for i in range(2)]
        znibt = sb("znibt", [128, 512], I32)
        zft = sb("zft", [128, 512], F32)
        nibi2 = sb("nibi2", [128, 512], I32)
        nibb2 = sb("nibb2", [128, 512], BF16)
        tb2 = sb("tb2", [128, 512], F32)
        wod = [sb(f"wod{i}", [128, 512], BF16) for i in range(2)]
        psS = ps("psS", [128, 1376], F32)
        psZ = ps("psZ", [128, 1376], F32)
        ldg = st.enter_context(nc.semaphore("ldg"))
        lzr = [st.enter_context(nc.semaphore(f"lzr{i}")) for i in range(2)]
        srw = [st.enter_context(nc.semaphore(f"srw{i}")) for i in range(2)]
        dldp = [st.enter_context(nc.semaphore(f"dldp{i}")) for i in range(2)]
        sdw = [st.enter_context(nc.semaphore(f"sdw{i}")) for i in range(2)]
        zn = st.enter_context(nc.semaphore("zn"))
        zc = st.enter_context(nc.semaphore("zc"))
        bc = st.enter_context(nc.semaphore("bc"))
        wn = st.enter_context(nc.semaphore("wn"))
        vsf = st.enter_context(nc.semaphore("vsf"))
        wv = st.enter_context(nc.semaphore("wv"))
        so = st.enter_context(nc.semaphore("so"))
        dld = st.enter_context(nc.semaphore("dld"))
        dzn = st.enter_context(nc.semaphore("dzn"))
        dzc = st.enter_context(nc.semaphore("dzc"))
        dwn = st.enter_context(nc.semaphore("dwn"))
        dvs = st.enter_context(nc.semaphore("dvs"))
        dwv = st.enter_context(nc.semaphore("dwv"))
        block = st.enter_context(nc.Block())

        @block.gpsimd
        def _(g_):
            g_.dma_start(out=qw_sb[0][:], in_=GQ.rearrange("(g p) c -> p g c", p=128)).then_inc(ldg, 16)
            g_.dma_start(out=qw_sb[1][:], in_=UQ.rearrange("(g p) c -> p g c", p=128)).then_inc(ldg, 16)
            g_.dma_start(out=qw_sb[2][:], in_=DQ.rearrange("(g p) c -> p g c", p=128)).then_inc(ldg, 16)
            g_.dma_start(out=ones[:], in_=ONE).then_inc(ldg, 16)

        @block.sync
        def _(sy):
            def store(u2):
                mi2, g2 = units[u2]
                sy.wait_ge(wv, u2 + 1)
                sy.dma_start(out=mats[mi2][4][128 * g2:128 * (g2 + 1), :],
                             in_=wout[u2 % 2][:]).then_inc(srw[u2 % 2], 16)
            for u, (mi, g) in enumerate(units):
                ZZ, SS = mats[mi][3], mats[mi][2]
                if u >= 2:
                    sy.wait_ge(zn, u - 1)
                    sy.wait_ge(bc, u - 1)
                sy.dma_start(out=zqr[u % 2][:], in_=ZZ[g:g + 1, :]).then_inc(lzr[u % 2], 16)
                sy.dma_start(out=srow[u % 2][:], in_=SS[g:g + 1, :]).then_inc(lzr[u % 2], 16)
                if u >= 2:
                    store(u - 2)
            store(NU - 2)
            store(NU - 1)
            def dstore(u2):
                sy.wait_ge(dwv, u2 + 1)
                sy.dma_start(out=WD[128 * u2:128 * (u2 + 1), :],
                             in_=wod[u2 % 2][:]).then_inc(sdw[u2 % 2], 16)
            for ud in range(ND):
                if ud >= 2:
                    sy.wait_ge(dzn, ud - 1)
                    sy.wait_ge(dwv, ud - 1)
                sy.dma_start(out=dzt[ud % 2][:], in_=DZE[128 * ud:128 * (ud + 1), :]).then_inc(dldp[ud % 2], 16)
                sy.dma_start(out=dst[ud % 2][:], in_=DSE[128 * ud:128 * (ud + 1), :]).then_inc(dldp[ud % 2], 16)
                if ud >= 2:
                    dstore(ud - 2)
            dstore(ND - 2)
            dstore(ND - 1)
            sy.wait_ge(srw[0], 16 * (NU // 2))
            sy.wait_ge(srw[1], 16 * (NU // 2))
            sy.wait_ge(sdw[0], 16 * 43)
            sy.wait_ge(sdw[1], 16 * 43)

        @block.vector
        def _(v):
            v.wait_ge(ldg, 64)
            for u, (mi, g) in enumerate(units):
                v.wait_ge(lzr[u % 2], 32 * (u // 2 + 1))
                if u >= 1:
                    v.wait_ge(zc, 2 * (u - 1) + 1)
                for p in range(8):
                    inst = v.tensor_scalar(znib[:, p * C:(p + 1) * C], zqr[u % 2][:],
                                           4 * ARO[p], 15, ALU.logical_shift_right, ALU.bitwise_and)
                inst.then_inc(zn, 1)
                if u >= 1:
                    v.wait_ge(zc, 2 * (u - 1) + 2)
                for p in range(8):
                    inst = v.tensor_scalar(nibi[:, p * C:(p + 1) * C], qw_sb[mi][:, g, :],
                                           4 * ARO[p], 15, ALU.logical_shift_right, ALU.bitwise_and)
                inst.then_inc(wn, 1)
                v.wait_ge(bc, u + 1)
                v.wait_ge(zc, 2 * u + 2)
                if u >= 1:
                    v.wait_ge(wv, u)
                v.tensor_tensor(tb[:], nibb[:], psZ[:], ALU.subtract).then_inc(vsf, 1)
                v.wait_ge(vsf, u + 1)
                if u >= 2:
                    v.wait_ge(srw[u % 2], 16 * (u // 2))
                v.tensor_tensor(wout[u % 2][:], tb[:], psS[:], ALU.mult).then_inc(wv, 1)
            for ud in range(ND):
                v.wait_ge(dldp[ud % 2], 32 * (ud // 2 + 1))
                if ud >= 1:
                    v.wait_ge(dzc, 2 * (ud - 1) + 1)
                for p in range(8):
                    inst = v.tensor_scalar(znibt[:, p * 64:(p + 1) * 64], dzt[ud % 2][:],
                                           4 * ARO[p], 15, ALU.logical_shift_right, ALU.bitwise_and)
                inst.then_inc(dzn, 1)
                if ud >= 1:
                    v.wait_ge(dzc, 2 * (ud - 1) + 2)
                for p in range(8):
                    inst = v.tensor_scalar(nibi2[:, p * 64:(p + 1) * 64], qw_sb[2][:, ud, :],
                                           4 * ARO[p], 15, ALU.logical_shift_right, ALU.bitwise_and)
                inst.then_inc(dwn, 1)
                v.wait_ge(dzc, 2 * ud + 2)
                if ud >= 1:
                    v.wait_ge(dwv, ud)
                v.tensor_tensor(tb2[:], nibb2[:], zft[:], ALU.subtract).then_inc(dvs, 1)
                v.wait_ge(dvs, ud + 1)
                if ud >= 2:
                    v.wait_ge(sdw[ud % 2], 16 * (ud // 2))
                v.tensor_tensor(wod[ud % 2][:], tb2[:], dst[ud % 2][:], ALU.mult).then_inc(dwv, 1)

        @block.scalar
        def _(s):
            for u, (mi, g) in enumerate(units):
                s.wait_ge(zn, u + 1)
                if u >= 1:
                    s.wait_ge(bc, u)
                s.activation(zb[:], znib[:], AF.Copy).then_inc(zc, 1)
                s.wait_ge(wn, u + 1)
                if u >= 1:
                    s.wait_ge(vsf, u)
                s.activation(nibb[:], nibi[:], AF.Copy).then_inc(zc, 1)
            for ud in range(ND):
                s.wait_ge(dzn, ud + 1)
                if ud >= 1:
                    s.wait_ge(dvs, ud)
                s.activation(zft[:], znibt[:], AF.Copy).then_inc(dzc, 1)
                s.wait_ge(dwn, ud + 1)
                if ud >= 1:
                    s.wait_ge(dvs, ud)
                s.activation(nibb2[:], nibi2[:], AF.Copy).then_inc(dzc, 1)

        @block.tensor
        def _(t):
            t.wait_ge(ldg, 64)
            for u, (mi, g) in enumerate(units):
                t.wait_ge(zc, 2 * u + 1)
                t.wait_ge(lzr[u % 2], 32 * (u // 2 + 1))
                if u >= 1:
                    t.wait_ge(wv, u)
                for ci, cw in [(0, 512), (1, 512), (2, 352)]:
                    t.matmul(psS[:, 512 * ci:512 * ci + cw], ones[:],
                             srow[u % 2][:, 512 * ci:512 * ci + cw], start=True, stop=True)
                for ci, cw in [(0, 512), (1, 512), (2, 352)]:
                    inst = t.matmul(psZ[:, 512 * ci:512 * ci + cw], ones[:],
                                    zb[:, 512 * ci:512 * ci + cw], start=True, stop=True)
                inst.then_inc(bc, 1)
    return nc


def host_dequant_inputs(gq, gs, gz, uq, us, uz, dq, ds, dz):
    lp_gate = lperm_global(172)
    lp_down = lperm_global(64)
    dq_perm = dq[lp_gate, :]
    ds_exp = np.repeat(ds, 128, axis=0)[lp_gate, :]
    dz_exp = np.repeat(dz, 128, axis=0)[lp_gate, :]
    one = np.ones((1, 128), np.float32)
    maps = []
    for c in range(8):
        maps.append(dict(
            gq=cc(gq[:, 172 * c:172 * (c + 1)]),
            gs=cc(gs[:, lp_gate[1376 * c:1376 * (c + 1)]]),
            gz=cc(gz[:, 172 * c:172 * (c + 1)]),
            uq=cc(uq[:, 172 * c:172 * (c + 1)]),
            us=cc(us[:, lp_gate[1376 * c:1376 * (c + 1)]]),
            uz=cc(uz[:, 172 * c:172 * (c + 1)]),
            dq=cc(dq_perm[:, 64 * c:64 * (c + 1)]),
            dse=cc(ds_exp[:, lp_down[512 * c:512 * (c + 1)]]),
            dze=cc(dz_exp[:, 64 * c:64 * (c + 1)]),
            one=one,
        ))
    return maps, lp_gate, lp_down


def build_ffn0():
    nc = bass.Bass("TRN2", target_bir_lowering=False)
    XNT = nc.dram_tensor("xnT", [D, T], BF16, kind="ExternalInput").ap()
    WGS = nc.dram_tensor("wgs", [86, D, 128], BF16, kind="ExternalInput").ap()
    WUS = nc.dram_tensor("wus", [86, D, 128], BF16, kind="ExternalInput").ap()
    HT = nc.dram_tensor("hT", [11008, T], BF16, kind="ExternalOutput").ap()

    NG = 86 * 2
    with ExitStack() as st:
        sb = lambda *a: st.enter_context(nc.sbuf_tensor(*a))
        ps = lambda *a: st.enter_context(nc.psum_tensor(*a))
        xnT = sb("xnTs", [128, 32, T], BF16)
        wg = [sb(f"wg{i}", [128, 32, 128], BF16) for i in range(2)]
        wu = [sb(f"wu{i}", [128, 32, 128], BF16) for i in range(2)]
        sg = [sb(f"sg{i}", [128, 512], F32) for i in range(2)]
        ht = [sb(f"ht{i}", [128, 512], BF16) for i in range(2)]
        psG = [ps(f"psG{i}", [128, 512], F32) for i in range(2)]
        psU = [ps(f"psU{i}", [128, 512], F32) for i in range(2)]
        lds = st.enter_context(nc.semaphore("lds"))
        lgw = [st.enter_context(nc.semaphore(f"lgw{i}")) for i in range(2)]
        luw = [st.enter_context(nc.semaphore(f"luw{i}")) for i in range(2)]
        sht = [st.enter_context(nc.semaphore(f"sht{i}")) for i in range(2)]
        mmg = st.enter_context(nc.semaphore("mmg"))
        sgs = st.enter_context(nc.semaphore("sgs"))
        hv = st.enter_context(nc.semaphore("hv"))
        so = st.enter_context(nc.semaphore("so"))
        block = st.enter_context(nc.Block())

        @block.sync
        def _(sy):
            sy.dma_start(out=xnT[:], in_=XNT.rearrange("(j p) t -> p j t", p=128)).then_inc(lds, 16)
            for gi in range(NG):
                nt, th = gi // 2, gi % 2
                sy.wait_ge(hv, gi + 1)
                sy.dma_start(out=HT[128 * nt:128 * (nt + 1), 512 * th:512 * (th + 1)],
                             in_=ht[gi % 2][:]).then_inc(sht[gi % 2], 16)
            sy.wait_ge(sht[0], 16 * (NG // 2))
            sy.wait_ge(sht[1], 16 * (NG // 2))

        @block.gpsimd
        def _(g):
            for nt in range(86):
                if nt >= 2:
                    g.wait_ge(mmg, 4 * (nt - 1))
                g.dma_start(out=wg[nt % 2][:], in_=WGS[nt].rearrange("(j p) n -> p j n", p=128)).then_inc(lgw[nt % 2], 16)
                g.dma_start(out=wu[nt % 2][:], in_=WUS[nt].rearrange("(j p) n -> p j n", p=128)).then_inc(luw[nt % 2], 16)

        @block.tensor
        def _(t):
            t.wait_ge(lds, 16)
            for gi in range(NG):
                nt, th = gi // 2, gi % 2
                if th == 0:
                    t.wait_ge(lgw[nt % 2], 16 * (nt // 2 + 1))
                    t.wait_ge(luw[nt % 2], 16 * (nt // 2 + 1))
                if gi >= 2:
                    t.wait_ge(sgs, gi - 1)
                    t.wait_ge(hv, gi - 1)
                sl = slice(512 * th, 512 * (th + 1))
                for j in range(32):
                    inst = t.matmul(psG[gi % 2][:], wg[nt % 2][:, j, :], xnT[:, j, sl],
                                    start=(j == 0), stop=(j == 31))
                inst.then_inc(mmg, 1)
                for j in range(32):
                    inst = t.matmul(psU[gi % 2][:], wu[nt % 2][:, j, :], xnT[:, j, sl],
                                    start=(j == 0), stop=(j == 31))
                inst.then_inc(mmg, 1)

        @block.scalar
        def _(s):
            for gi in range(NG):
                s.wait_ge(mmg, 2 * gi + 1)
                if gi >= 2:
                    s.wait_ge(hv, gi - 1)
                s.activation(sg[gi % 2][:], psG[gi % 2][:], AF.Silu).then_inc(sgs, 1)

        @block.vector
        def _(v):
            for gi in range(NG):
                v.wait_ge(mmg, 2 * gi + 2)
                v.wait_ge(sgs, gi + 1)
                if gi >= 2:
                    v.wait_ge(sht[gi % 2], 16 * (gi // 2))
                v.tensor_tensor(ht[gi % 2][:], sg[gi % 2][:], psU[gi % 2][:], ALU.mult).then_inc(hv, 1)
    return nc


def build_downwo():
    nc = bass.Bass("TRN2", target_bir_lowering=False)
    HT = nc.dram_tensor("hT", [11008, T], BF16, kind="ExternalInput").ap()
    WDS = nc.dram_tensor("wds", [32, 11008, 128], BF16, kind="ExternalInput").ap()
    CXT = nc.dram_tensor("cxT", [D, T], BF16, kind="ExternalInput").ap()
    WOS = nc.dram_tensor("wos", [32, D, 128], F32, kind="ExternalInput").ap()
    XTP = nc.dram_tensor("xTp", [D, T], F32, kind="ExternalInput").ap()
    ONE = nc.dram_tensor("one", [1, 128], F32, kind="ExternalInput").ap()
    ONEC = nc.dram_tensor("onec", [128, 1], F32, kind="ExternalInput").ap()
    X1T = nc.dram_tensor("x1T", [D, T], F32, kind="ExternalOutput").ap()
    XN1T = nc.dram_tensor("xn1T", [D, T], BF16, kind="ExternalOutput").ap()

    with ExitStack() as st:
        sb = lambda *a: st.enter_context(nc.sbuf_tensor(*a))
        ps = lambda *a: st.enter_context(nc.psum_tensor(*a))
        hth = sb("hth", [128, 86, 256], BF16)
        cxh = sb("cxh", [128, 32, 256], BF16)
        g0h = sb("g0h", [128, 32, 256], BF16)
        x1bh = sb("x1bh", [128, 32, 256], BF16)
        wd = [sb(f"wd{i}", [128, 86, 128], BF16) for i in range(2)]
        wo = [sb(f"wo{i}", [128, 32, 128], BF16) for i in range(2)]
        xt = [sb(f"xt{i}", [128, 256], F32) for i in range(2)]
        s0 = [sb(f"s0{i}", [128, 256], F32) for i in range(2)]
        tsb = sb("tsb", [128, 256], F32)
        x1s = [sb(f"x1s{i}", [128, 256], F32) for i in range(2)]
        x1q = [sb(f"x1q{i}", [128, 256], BF16) for i in range(2)]
        xn1 = [sb(f"xn1{i}", [128, 256], BF16) for i in range(2)]
        onesc = sb("onesc", [128, 1], BF16)
        oner = sb("oner", [1, 128], F32)
        mrow = sb("mrow", [1, 256], F32)
        t1row = sb("t1row", [1, 256], F32)
        vrow = sb("vrow", [1, 256], F32)
        hrow = sb("hrow", [1, 256], F32)
        rstdrow = sb("rstdrow", [1, 256], F32)
        nbrow = sb("nbrow", [1, 256], F32)
        psD = ps("psD", [128, 256], F32)
        psS = ps("psS2", [128, 256], F32)
        psSum = ps("psSum", [1, 256], F32)
        psSq = ps("psSq", [1, 256], F32)
        psR = ps("psR", [128, 256], F32)
        psN = ps("psN", [128, 256], F32)
        lo = st.enter_context(nc.semaphore("lo"))
        lh = st.enter_context(nc.semaphore("lh"))
        lc = st.enter_context(nc.semaphore("lc"))
        lon = st.enter_context(nc.semaphore("lon"))
        lwd = [st.enter_context(nc.semaphore(f"lwd{i}")) for i in range(2)]
        lwo = [st.enter_context(nc.semaphore(f"lwo{i}")) for i in range(2)]
        lxp = [st.enter_context(nc.semaphore(f"lxp{i}")) for i in range(2)]
        sx1 = [st.enter_context(nc.semaphore(f"sx1{i}")) for i in range(2)]
        sxn1 = [st.enter_context(nc.semaphore(f"sxn1{i}")) for i in range(2)]
        mmd = st.enter_context(nc.semaphore("mmd"))
        mms = st.enter_context(nc.semaphore("mms"))
        mmq = st.enter_context(nc.semaphore("mmq"))
        gsl = st.enter_context(nc.semaphore("gsl"))
        sc0 = st.enter_context(nc.semaphore("sc0"))
        sxc = st.enter_context(nc.semaphore("sxc"))
        vv = st.enter_context(nc.semaphore("vv"))
        vw = st.enter_context(nc.semaphore("vw"))
        sln = st.enter_context(nc.semaphore("sln"))
        bcm = st.enter_context(nc.semaphore("bcm"))
        vn = st.enter_context(nc.semaphore("vn"))
        so = st.enter_context(nc.semaphore("so"))
        block = st.enter_context(nc.Block())

        def x1_store_count(u):
            hf2, ot2 = divmod(u, 32)
            return 64 * hf2 + ot2 + 1

        def xn1_store_count(u):
            hf2, ot2 = divmod(u, 32)
            return 64 * hf2 + 32 + ot2 + 1

        @block.sync
        def _(sy):
            sy.dma_start(out=oner[:], in_=ONE).then_inc(lo, 16)
            for hf in range(4):
                if hf >= 1:
                    sy.wait_ge(mmd, 32 * hf)
                sy.dma_start(out=hth[:], in_=HT.rearrange("(j p) t -> p j t", p=128)[:, :, 256 * hf:256 * (hf + 1)]).then_inc(lh, 16)
                if hf >= 1:
                    sy.wait_ge(mms, 32 * hf)
                sy.dma_start(out=cxh[:], in_=CXT.rearrange("(j p) t -> p j t", p=128)[:, :, 256 * hf:256 * (hf + 1)]).then_inc(lc, 16)
                for ot in range(32):
                    u = 32 * hf + ot
                    if u >= 2:
                        sy.wait_ge(vv, 2 * (u - 2) + 2)
                    sy.dma_start(out=xt[ot % 2][:], in_=XTP[128 * ot:128 * (ot + 1), 256 * hf:256 * (hf + 1)]).then_inc(lxp[u % 2], 16)
                    if ot >= 2:
                        sy.dma_start(out=X1T[128 * (ot - 2):128 * (ot - 1), 256 * hf:256 * (hf + 1)],
                                     in_=x1s[ot % 2][:]).then_inc(sx1[ot % 2], 16)
                for ot in [30, 31]:
                    sy.wait_ge(vv, 2 * (32 * hf + ot) + 2)
                    sy.dma_start(out=X1T[128 * ot:128 * (ot + 1), 256 * hf:256 * (hf + 1)],
                                 in_=x1s[ot % 2][:]).then_inc(sx1[ot % 2], 16)
                for ot in range(32):
                    sy.wait_ge(vn, 2 * (32 * hf + ot) + 2)
                    sy.dma_start(out=XN1T[128 * ot:128 * (ot + 1), 256 * hf:256 * (hf + 1)],
                                 in_=xn1[ot % 2][:]).then_inc(sxn1[ot % 2], 16)
            for par in range(2):
                sy.wait_ge(sx1[par], 16 * 64)
                sy.wait_ge(sxn1[par], 16 * 64)

        @block.gpsimd
        def _(g):
            g.dma_start(out=onesc[:], in_=ONEC).then_inc(lon, 16)
            for hf in range(4):
                for ot in range(32):
                    u = 32 * hf + ot
                    if u >= 2:
                        g.wait_ge(mmd, u - 1)
                    g.dma_start(out=wd[ot % 2][:], in_=WDS[ot].rearrange("(j p) n -> p j n", p=128)).then_inc(lwd[ot % 2], 16)
                for ot in range(32):
                    u = 32 * hf + ot
                    if u >= 2:
                        g.wait_ge(mms, u - 1)
                    g.dma_start(out=wo[ot % 2][:], in_=WOS[ot].rearrange("(j p) n -> p j n", p=128)).then_inc(lwo[ot % 2], 16)

        @block.tensor
        def _(t):
            for hf in range(4):
                t.wait_ge(lh, 16 * (hf + 1))
                for ot in range(32):
                    u = 32 * hf + ot
                    t.wait_ge(lwd[ot % 2], 16 * (u // 2 + 1))
                    if u >= 1:
                        t.wait_ge(gsl, u)
                    for j in range(86):
                        inst = t.matmul(psD[:], wd[ot % 2][:, j, :], hth[:, j, :],
                                        start=(j == 0), stop=(j == 85))
                    inst.then_inc(mmd, 1)
                t.wait_ge(lc, 16 * (hf + 1))
                if hf >= 1:
                    t.wait_ge(vw, 6 * hf)
                    t.wait_ge(vn, 64 * hf)
                for ot in range(34):
                    if ot < 32:
                        u = 32 * hf + ot
                        t.wait_ge(lwo[ot % 2], 16 * (u // 2 + 1))
                        if u >= 1:
                            t.wait_ge(sc0, u)
                        for j in range(32):
                            inst = t.matmul(psS[:], wo[ot % 2][:, j, :], cxh[:, j, :],
                                            start=(j == 0), stop=(j == 31))
                        inst.then_inc(mms, 1)
                    if ot >= 2:
                        o2 = ot - 2
                        u2 = 32 * hf + o2
                        t.wait_ge(sxc, 2 * u2 + 2)
                        if hf == 0 and o2 == 0:
                            t.wait_ge(lon, 16)
                        t.matmul(psSum[:], onesc[:], x1bh[:, o2, :],
                                 start=(o2 == 0), stop=(o2 == 31))
                        inst = t.matmul(psSq[:], onesc[:], x1q[o2 % 2][:],
                                        start=(o2 == 0), stop=(o2 == 31))
                        inst.then_inc(mmq, 1)
                t.wait_ge(sln, 2 * hf + 2)
                t.wait_ge(vw, 6 * hf + 6)
                if hf == 0:
                    t.wait_ge(lo, 16)
                t.matmul(psR[:], oner[:], rstdrow[:], start=True, stop=True)
                inst = t.matmul(psN[:], oner[:], nbrow[:], start=True, stop=True)
                inst.then_inc(bcm, 1)

        @block.scalar
        def _(s):
            for hf in range(4):
                for ot in range(32):
                    u = 32 * hf + ot
                    s.wait_ge(mmd, u + 1)
                    if hf >= 1 and ot == 0:
                        s.wait_ge(vv, 64 * hf)
                    s.activation(g0h[:, ot, :], psD[:], AF.Silu).then_inc(gsl, 1)
                for ot in range(32):
                    u = 32 * hf + ot
                    s.wait_ge(mms, u + 1)
                    if u >= 2:
                        s.wait_ge(vv, 2 * (u - 2) + 1)
                    s.activation(s0[ot % 2][:], psS[:], AF.Copy).then_inc(sc0, 1)
                    s.wait_ge(vv, 2 * u + 2)
                    if hf >= 1 and ot == 0:
                        s.wait_ge(vn, 64 * hf)
                    if u >= 2:
                        s.wait_ge(mmq, u - 1)
                    s.activation(x1bh[:, ot, :], x1s[ot % 2][:], AF.Copy)
                    inst = s.activation(x1q[ot % 2][:], x1s[ot % 2][:], AF.Square)
                    inst.then_inc(sxc, 2)
                s.wait_ge(vw, 6 * hf + 4)
                s.activation(vrow[:], hrow[:], AF.Ln).then_inc(sln, 1)
                s.wait_ge(vw, 6 * hf + 5)
                s.activation(rstdrow[:], hrow[:], AF.Exp).then_inc(sln, 1)

        @block.vector
        def _(v):
            for hf in range(4):
                for ot in range(32):
                    u = 32 * hf + ot
                    v.wait_ge(gsl, u + 1)
                    v.wait_ge(sc0, u + 1)
                    v.wait_ge(lxp[u % 2], 16 * (u // 2 + 1))
                    if hf >= 1 and ot == 0:
                        v.wait_ge(vn, 64 * hf)
                    if u >= 1:
                        v.wait_ge(vv, 2 * u)
                    v.tensor_tensor(tsb[:], g0h[:, ot, :], s0[ot % 2][:], ALU.mult).then_inc(vv, 1)
                    v.wait_ge(vv, 2 * u + 1)
                    if u >= 2:
                        v.wait_ge(sx1[u % 2], 16 * ((u - 2) // 2 + 1) if False else 16 * (u // 2))
                    v.tensor_tensor(x1s[ot % 2][:], tsb[:], xt[ot % 2][:], ALU.add).then_inc(vv, 1)
                v.wait_ge(mmq, 32 * hf + 32)
                vb = 6 * hf
                v.tensor_scalar(mrow[:], psSum[:], 1.0 / D, None, ALU.mult).then_inc(vw, 1)
                v.wait_ge(vw, vb + 1)
                v.tensor_tensor(t1row[:], mrow[:], mrow[:], ALU.mult).then_inc(vw, 1)
                v.wait_ge(vw, vb + 2)
                v.scalar_tensor_tensor(vrow[:], psSq[:], 1.0 / D, t1row[:], ALU.mult, ALU.subtract).then_inc(vw, 1)
                v.wait_ge(vw, vb + 3)
                v.tensor_scalar(hrow[:], vrow[:], EPS, None, ALU.add).then_inc(vw, 1)
                v.wait_ge(sln, 2 * hf + 1)
                v.tensor_scalar(hrow[:], vrow[:], -0.5, None, ALU.mult).then_inc(vw, 1)
                v.wait_ge(sln, 2 * hf + 2)
                v.scalar_tensor_tensor(nbrow[:], mrow[:], -1.0, rstdrow[:], ALU.mult, ALU.mult).then_inc(vw, 1)
                v.wait_ge(vv, 64 * hf + 64)
                v.wait_ge(bcm, hf + 1)
                v.wait_ge(sxc, 2 * (32 * hf + 31) + 2)
                for ot in range(32):
                    u = 32 * hf + ot
                    if ot >= 1:
                        v.wait_ge(vn, 2 * u)
                    v.tensor_tensor(tsb[:], x1bh[:, ot, :], psR[:], ALU.mult).then_inc(vn, 1)
                    v.wait_ge(vn, 2 * u + 1)
                    if u >= 2:
                        v.wait_ge(sxn1[u % 2], 16 * (u // 2))
                    v.tensor_tensor(xn1[ot % 2][:], tsb[:], psN[:], ALU.add).then_inc(vn, 1)
    return nc


def build_attn1(stage=2, nkv=NKV):
    nc = bass.Bass("TRN2", target_bir_lowering=False)
    XNT = nc.dram_tensor("xn1T", [D, T], BF16, kind="ExternalInput").ap()
    WQS = nc.dram_tensor("wqs", [32, D, 128], F32, kind="ExternalInput").ap()
    C1 = nc.dram_tensor("c1", [HD, T], F32, kind="ExternalInput").ap()
    S1 = nc.dram_tensor("s1", [HD, T], F32, kind="ExternalInput").ap()
    KC = nc.dram_tensor("kc", [512, NKV, 16, HD], F32, kind="ExternalInput").ap()
    VC = nc.dram_tensor("vc", [512, NKV, 16, HD], F32, kind="ExternalInput").ap()
    BTI = nc.dram_tensor("bti", [128, 2], I32, kind="ExternalInput").ap()
    EYE = nc.dram_tensor("eye", [128, 128], F32, kind="ExternalInput").ap()
    CTX1 = nc.dram_tensor("ctx1", [NKV, 128, 32, 128], BF16, kind="ExternalOutput").ap()

    with ExitStack() as st:
        sb = lambda *a: st.enter_context(nc.sbuf_tensor(*a))
        ps = lambda *a: st.enter_context(nc.psum_tensor(*a))
        xnT = sb("xnTs", [128, 32, T], BF16)
        q1T = sb("q1Ts", [128, 32, T], BF16)
        wq = [sb(f"wq{i}", [128, 32, 128], BF16) for i in range(2)]
        c1s = sb("c1s", [128, T], F32)
        s1s = sb("s1s", [128, T], F32)
        rt1 = sb("rt1", [128, 512], F32)
        rt2 = sb("rt2", [128, 512], F32)
        eyeb = sb("eyeb", [128, 128], BF16)
        idx = sb("idx", [128, 2], I32)
        kgB = sb("kgB", [128, 2, 16, 128], BF16)
        vstage = sb("vstage", [128, 2048], BF16)
        vgB = sb("vgB", [128, 2, 16, 129], BF16)
        kT1 = sb("kT1", [128, 32, 128], BF16)
        expb = [sb(f"expb{i}", [128, 512], BF16) for i in range(2)]
        rcp = sb("rcp", [128, 4], F32)
        cxg = [sb(f"cxg{i}", [128, 32, 128], BF16) for i in range(2)]
        pmm = [ps(f"pmm{i}", [128, 512], F32) for i in range(2)]
        ptp = ps("ptp", [128, 128], BF16)
        ctp = [ps(f"ctp{i}", [128, 129], F32) for i in range(4)]
        lds = st.enter_context(nc.semaphore("lds"))
        lxn = st.enter_context(nc.semaphore("lxn"))
        ley = st.enter_context(nc.semaphore("ley"))
        lwq = [st.enter_context(nc.semaphore(f"lwq{i}")) for i in range(2)]
        kf = [st.enter_context(nc.semaphore(f"kf{i}")) for i in range(2)]
        vfp = [st.enter_context(nc.semaphore(f"vfp{i}")) for i in range(2)]
        sf = [st.enter_context(nc.semaphore(f"sf{i}")) for i in range(2)]
        msd = st.enter_context(nc.semaphore("msd"))
        mmp = st.enter_context(nc.semaphore("mmp"))
        evp = st.enter_context(nc.semaphore("evp"))
        vvp = st.enter_context(nc.semaphore("vvp"))
        vcp = st.enter_context(nc.semaphore("vcp"))
        tpf = st.enter_context(nc.semaphore("tpf"))
        tcf = st.enter_context(nc.semaphore("tcf"))
        mmf = st.enter_context(nc.semaphore("mmf"))
        exf = st.enter_context(nc.semaphore("exf"))
        avf = st.enter_context(nc.semaphore("avf"))
        vrc = st.enter_context(nc.semaphore("vrc"))
        cte = st.enter_context(nc.semaphore("cte"))
        sof = st.enter_context(nc.semaphore("sof"))
        block = st.enter_context(nc.Block())

        def av_tile(t, n):
            g, ti = divmod(n, 256)
            qh, th, kt = ti // 64, (ti // 32) % 2, ti % 32
            half, j = kt // 16, kt % 16
            t.wait_ge(exf, n + 1)
            t.wait_ge(vcp, 2 * g + half + 1)
            if kt == 0:
                t.wait_ge(msd, 1)
                gidx = n // 32
                if gidx >= 1:
                    t.wait_ge(cte, gidx)
            for qs4 in range(4):
                inst = t.matmul(ctp[qs4][:], expb[n % 2][:, 128 * qs4:128 * (qs4 + 1)],
                                vgB[:, half, j, :], start=(kt == 0), stop=(kt == 31))
            inst.then_inc(avf, 1)

        @block.sync
        def _(sy):
            sy.dma_start(out=xnT[:], in_=XNT.rearrange("(j p) t -> p j t", p=128)).then_inc(lxn, 16)
            sy.dma_start(out=c1s[:], in_=C1).then_inc(lds, 16)
            sy.dma_start(out=s1s[:], in_=S1).then_inc(lds, 16)
            sy.dma_start(out=idx[:], in_=BTI).then_inc(lds, 16)
            if stage < 2:
                sy.wait_ge(evp, 64)
                sy.dma_start(out=CTX1.rearrange("g p a b -> p (g a b)", p=128) if False else CTX1.rearrange("g p a b -> p g (a b)"),
                             in_=q1T[:].rearrange("p a t -> p (a t)").rearrange("p (g x) -> p g x", g=8)).then_inc(sof, 16)
                sy.wait_ge(sof, 16)
                return
            for g in range(nkv):
                sy.wait_ge(cte, 8 * (g + 1))
                sy.dma_start(out=CTX1[g].rearrange("p a b -> p (a b)"),
                             in_=cxg[g % 2][:].rearrange("p a b -> p (a b)")).then_inc(sf[g % 2], 16)
            for par in range(2):
                sy.wait_ge(sf[par], 16 * len([g for g in range(nkv) if g % 2 == par]))

        @block.gpsimd
        def _(g_):
            g_.dma_start(out=eyeb[:], in_=EYE).then_inc(ley, 16)
            g_.memset(vgB[:, :, :, 128:129], 1.0).then_inc(msd, 1)
            for pi in range(32):
                if pi >= 2:
                    g_.wait_ge(mmp, 2 * pi - 2)
                g_.dma_start(out=wq[pi % 2][:], in_=WQS[pi].rearrange("(j p) n -> p j n", p=128)).then_inc(lwq[pi % 2], 16)
            if stage < 2:
                return
            g_.wait_ge(lds, 48)
            kcf = KC.rearrange("b h j d -> b (h j d)")
            vcf = VC.rearrange("b h j d -> b (h j d)")
            for g in range(nkv):
                if g >= 1:
                    g_.wait_ge(tpf, 32 * g)
                for half in range(2):
                    g_.indirect_dma_start(
                        out=kgB[:, half, :, :].rearrange("p a b -> p (a b)"), out_offset=None,
                        in_=kcf, in_offset=IndirectOffsetOnAxis(idx[:, half:half + 1], axis=0),
                        element_offset=g * 2048).then_inc(kf[half], 16)
                    if g >= 1 or half >= 1:
                        g_.wait_ge(vcp, 2 * g + half)
                    g_.indirect_dma_start(
                        out=vstage[:], out_offset=None,
                        in_=vcf, in_offset=IndirectOffsetOnAxis(idx[:, half:half + 1], axis=0),
                        element_offset=g * 2048).then_inc(vfp[half], 16)

        @block.tensor
        def _(t):
            t.wait_ge(lxn, 16)
            for G in range(64):
                pi, th = G // 2, G % 2
                if th == 0:
                    t.wait_ge(lwq[pi % 2], 16 * (pi // 2 + 1))
                if G >= 2:
                    t.wait_ge(evp, G - 1)
                for j in range(32):
                    inst = t.matmul(pmm[G % 2][:], wq[pi % 2][:, j, :],
                                    xnT[:, j, 512 * th:512 * (th + 1)],
                                    start=(j == 0), stop=(j == 31))
                inst.then_inc(mmp, 1)
            if stage < 2:
                return
            pending = None
            for g in range(nkv):
                if pending is not None:
                    av_tile(t, pending)
                    pending = None
                if g == 0:
                    t.wait_ge(ley, 16)
                for kt in range(32):
                    n_t = 32 * g + kt
                    t.wait_ge(kf[kt // 16], 16 * (g + 1))
                    if n_t >= 1:
                        t.wait_ge(tcf, n_t)
                    t.transpose(ptp[:], kgB[:, kt // 16, kt % 16, :], eyeb[:]).then_inc(tpf, 1)
                for ti in range(256):
                    n = 256 * g + ti
                    qh, th, kt = ti // 64, (ti // 32) % 2, ti % 32
                    h = 4 * g + qh
                    if ti == 0:
                        t.wait_ge(tcf, 32 * (g + 1))
                        t.wait_ge(evp, 64)
                    if n >= 2:
                        t.wait_ge(exf, n - 1)
                    t.matmul(pmm[n % 2][:], kT1[:, kt, :], q1T[:, h, 512 * th:512 * (th + 1)],
                             start=True, stop=True).then_inc(mmf, 1)
                    if ti >= 1:
                        av_tile(t, n - 1)
                pending = 256 * (g + 1) - 1
            av_tile(t, pending)

        @block.scalar
        def _(s):
            if stage < 2:
                return
            for n in range(256 * nkv):
                if n >= 2:
                    s.wait_ge(avf, n - 1)
                s.wait_ge(mmf, n + 1)
                s.activation(expb[n % 2][:], pmm[n % 2][:], AF.Exp).then_inc(exf, 1)
                ti = n % 256
                if ti % 32 == 31:
                    gidx = n // 32
                    g, qh, th = n // 256, ti // 64, (ti // 32) % 2
                    s.wait_ge(vrc, 4 * (gidx + 1))
                    if qh == 0 and th == 0 and g >= 2:
                        s.wait_ge(sf[g % 2], 16 * (g // 2))
                    for qs4 in range(4):
                        inst = s.activation(cxg[g % 2][:, qh * 8 + th * 4 + qs4, :],
                                            ctp[qs4][:, 0:128], AF.Copy,
                                            scale=rcp[:, qs4:qs4 + 1])
                    inst.then_inc(cte, 1)

        @block.vector
        def _(v):
            v.wait_ge(lds, 48)
            for G in range(64):
                pi, th = G // 2, G % 2
                sl = slice(512 * th, 512 * (th + 1))
                p = pmm[G % 2]
                v.wait_ge(mmp, G + 1)
                if G >= 1:
                    v.wait_ge(evp, G)
                vb = 3 * G
                v.tensor_tensor(rt1[:], p[:], c1s[:, sl], ALU.mult).then_inc(vvp, 1)
                v.tensor_tensor(rt2[0:64, :], p[64:128, :], s1s[0:64, sl], ALU.mult).then_inc(vvp, 1)
                v.tensor_tensor(rt2[64:128, :], p[0:64, :], s1s[64:128, sl], ALU.mult).then_inc(vvp, 1)
                v.wait_ge(vvp, vb + 3)
                v.tensor_tensor(q1T[:, pi, sl], rt1[:], rt2[:], ALU.add).then_inc(evp, 1)
            if stage < 2:
                return
            for g in range(nkv):
                for kt in range(32):
                    n_t = 32 * g + kt
                    v.wait_ge(tpf, n_t + 1)
                    if kt == 0 and g >= 1:
                        v.wait_ge(mmf, 256 * g)
                    v.tensor_copy(kT1[:, kt, :], ptp[:]).then_inc(tcf, 1)
                for half in range(2):
                    v.wait_ge(vfp[half], 16 * (g + 1))
                    if g >= 1:
                        v.wait_ge(avf, 256 * g)
                    v.tensor_copy(vgB[:, half, :, 0:128],
                                  vstage[:].rearrange("p (a b) -> p a b", a=16)).then_inc(vcp, 1)
                for gidx in range(8 * g, 8 * (g + 1)):
                    v.wait_ge(avf, 32 * (gidx + 1))
                    if gidx >= 1:
                        v.wait_ge(cte, gidx)
                    for qs4 in range(4):
                        v.reciprocal(rcp[:, qs4:qs4 + 1], ctp[qs4][:, 128:129]).then_inc(vrc, 1)
    return nc


def build_final():
    nc = bass.Bass("TRN2", target_bir_lowering=False)
    XNT = nc.dram_tensor("xn1T", [D, T], BF16, kind="ExternalInput").ap()
    CXT = nc.dram_tensor("cx1T", [D, T], BF16, kind="ExternalInput").ap()
    X1T = nc.dram_tensor("x1T", [D, T], F32, kind="ExternalInput").ap()
    WOS = nc.dram_tensor("wos", [32, D, 128], F32, kind="ExternalInput").ap()
    WBS = nc.dram_tensor("wbs", [32, D, 128], F32, kind="ExternalInput").ap()
    BB = nc.dram_tensor("bb", [128, 32], F32, kind="ExternalInput").ap()
    WG8 = nc.dram_tensor("wg8", [D, 8], F32, kind="ExternalInput").ap()
    AT = nc.dram_tensor("at", [D, 128], F32, kind="ExternalInput").ap()
    BT8 = nc.dram_tensor("bt8", [128, D], F32, kind="ExternalInput").ap()
    E16 = nc.dram_tensor("e16", [8, 128], F32, kind="ExternalInput").ap()
    EYE = nc.dram_tensor("eye", [128, 128], F32, kind="ExternalInput").ap()
    OUT = nc.dram_tensor("outT", [D, T], F32, kind="ExternalOutput").ap()

    with ExitStack() as st:
        sb = lambda *a: st.enter_context(nc.sbuf_tensor(*a))
        ps = lambda *a: st.enter_context(nc.psum_tensor(*a))
        xnT = sb("xnTs", [128, 32, T], BF16)
        cxT = sb("cxTs", [128, 32, T], BF16)
        wo = [sb(f"wo{i}", [128, 32, 128], BF16) for i in range(2)]
        wb = [sb("wb0", [128, 32, 128], BF16)]
        x1t = [sb(f"x1t{i}", [128, 512], F32) for i in range(2)]
        bb = sb("bbs", [128, 32], F32)
        wg8 = sb("wg8s", [128, 32, 8], BF16)
        at = sb("ats", [128, 32, 128], BF16)
        bt8 = sb("bt8s", [128, D], BF16)
        e16 = sb("e16s", [8, 128], BF16)
        eyeb = sb("eyeb", [128, 128], BF16)
        eyef = sb("eyef", [8, 8], F32)
        lg = sb("lg", [8, T], F32)
        lgT = sb("lgT", [128, 8, 8], F32)
        mc = sb("mc", [128, 8], F32)
        msk1 = sb("msk1", [128, 8], F32)
        msk2 = sb("msk2", [128, 8], F32)
        l2c = sb("l2c", [128, 8], F32)
        r1 = sb("r1", [128, 8], F32)
        r2 = sb("r2", [128, 8], F32)
        routeb = sb("routeb", [128, 8], BF16)
        routeT = sb("routeT", [8, T], BF16)
        routeX = sb("routeX", [128, T], BF16)
        hw = sb("hws", [128, T], BF16)
        g1 = [sb(f"g1{i}", [128, 512], F32) for i in range(2)]
        s1 = [sb(f"s1{i}", [128, 512], F32) for i in range(2)]
        tt2 = sb("tt2", [128, 512], F32)
        ob = [sb(f"ob{i}", [128, 512], F32) for i in range(2)]
        psL = ps("psL", [8, T], F32)
        psT = ps("psT", [128, 8], F32)
        psTb = ps("psTb", [8, 128], BF16)
        psRX = ps("psRX", [128, 512], F32)
        psH = ps("psH", [128, 512], F32)
        psR2 = ps("psR2", [128, 512], F32)
        psS3 = ps("psS3", [128, 512], F32)
        lds = st.enter_context(nc.semaphore("lds"))
        lgf = st.enter_context(nc.semaphore("lgf"))
        lwb = st.enter_context(nc.semaphore("lwb"))
        lwo = [st.enter_context(nc.semaphore(f"lwo{i}")) for i in range(2)]
        lx1 = [st.enter_context(nc.semaphore(f"lx1{i}")) for i in range(2)]
        sow2 = [st.enter_context(nc.semaphore(f"sow2{i}")) for i in range(2)]
        mml = st.enter_context(nc.semaphore("mml"))
        vga = st.enter_context(nc.semaphore("vga"))
        sga = st.enter_context(nc.semaphore("sga"))
        tra = st.enter_context(nc.semaphore("tra"))
        mmx = st.enter_context(nc.semaphore("mmx"))
        mmh = st.enter_context(nc.semaphore("mmh"))
        vhw = st.enter_context(nc.semaphore("vhw"))
        mmr = st.enter_context(nc.semaphore("mmr"))
        mmo = st.enter_context(nc.semaphore("mmo"))
        sgl = st.enter_context(nc.semaphore("sgl"))
        ss1 = st.enter_context(nc.semaphore("ss1"))
        vvo = st.enter_context(nc.semaphore("vvo"))
        so = st.enter_context(nc.semaphore("so"))
        block = st.enter_context(nc.Block())

        VGA_CH = lambda tc: 9 + 9 * tc

        @block.sync
        def _(sy):
            sy.dma_start(out=xnT[:], in_=XNT.rearrange("(j p) t -> p j t", p=128)).then_inc(lds, 16)
            sy.dma_start(out=cxT[:], in_=CXT.rearrange("(j p) t -> p j t", p=128)).then_inc(lds, 16)
            sy.dma_start(out=bb[:], in_=BB).then_inc(lds, 16)
            sy.dma_start(out=eyef[:], in_=EYE[0:8, 0:8]).then_inc(lds, 16)
            for u in range(64):
                ot, th = divmod(u, 2)
                if u >= 2:
                    sy.wait_ge(vvo, 2 * (u - 2) + 2)
                sy.dma_start(out=x1t[u % 2][:], in_=X1T[128 * ot:128 * (ot + 1), 512 * th:512 * (th + 1)]).then_inc(lx1[u % 2], 16)
                if u >= 2:
                    u2 = u - 2
                    ot2, th2 = divmod(u2, 2)
                    sy.wait_ge(vvo, 2 * u2 + 2)
                    sy.dma_start(out=OUT[128 * ot2:128 * (ot2 + 1), 512 * th2:512 * (th2 + 1)],
                                 in_=ob[u2 % 2][:]).then_inc(sow2[u2 % 2], 16)
            for u2 in [62, 63]:
                ot2, th2 = divmod(u2, 2)
                sy.wait_ge(vvo, 2 * u2 + 2)
                sy.dma_start(out=OUT[128 * ot2:128 * (ot2 + 1), 512 * th2:512 * (th2 + 1)],
                             in_=ob[u2 % 2][:]).then_inc(sow2[u2 % 2], 16)
            for i in range(2):
                sy.wait_ge(sow2[i], 16 * 32)

        @block.gpsimd
        def _(g):
            g.dma_start(out=wg8[:], in_=WG8.rearrange("(j p) e -> p j e", p=128)).then_inc(lgf, 16)
            g.dma_start(out=at[:], in_=AT.rearrange("(j p) n -> p j n", p=128)).then_inc(lgf, 16)
            g.dma_start(out=bt8[:], in_=BT8).then_inc(lgf, 16)
            g.dma_start(out=e16[:], in_=E16).then_inc(lgf, 16)
            g.dma_start(out=eyeb[:], in_=EYE).then_inc(lgf, 16)
            for ot in range(32):
                if ot >= 1:
                    g.wait_ge(mmr, 2 * ot)
                g.dma_start(out=wb[0][:], in_=WBS[ot].rearrange("(j p) n -> p j n", p=128)).then_inc(lwb, 16)
                if ot >= 2:
                    g.wait_ge(mmo, 2 * ot - 2)
                g.dma_start(out=wo[ot % 2][:], in_=WOS[ot].rearrange("(j p) n -> p j n", p=128)).then_inc(lwo[ot % 2], 16)

        @block.tensor
        def _(t):
            t.wait_ge(lds, 64)
            t.wait_ge(lgf, 16 * 5)
            for th in range(2):
                for j in range(32):
                    inst = t.matmul(psL[:, 512 * th:512 * (th + 1)], wg8[:, j, :],
                                    xnT[:, j, 512 * th:512 * (th + 1)],
                                    start=(j == 0), stop=(j == 31))
                inst.then_inc(mml, 1)
            for tc in range(8):
                t.wait_ge(vga, 1)
                if tc >= 1:
                    t.wait_ge(vga, 1 + tc)
                t.transpose(psT[:, 0:8], lg[:, 128 * tc:128 * (tc + 1)], eyef[:]).then_inc(tra, 1)
            for tc in range(8):
                t.wait_ge(vga, VGA_CH(tc) + 8)
                if tc >= 1:
                    t.wait_ge(vga, VGA_CH(tc - 1) + 9)
                t.transpose(psTb[:], routeb[:], eyeb[:]).then_inc(tra, 1)
            for th in range(2):
                t.wait_ge(vga, 81 + th)
                t.matmul(psRX[:], e16[:], routeT[:, 512 * th:512 * (th + 1)],
                         start=True, stop=True).then_inc(mmx, 1)
            for th in range(2):
                t.wait_ge(vga, 83)
                if th >= 1:
                    t.wait_ge(vhw, th)
                for j in range(32):
                    inst = t.matmul(psH[:], at[:, j, :], xnT[:, j, 512 * th:512 * (th + 1)],
                                    start=(j == 0), stop=(j == 31))
                inst.then_inc(mmh, 1)
            t.wait_ge(vhw, 2)
            for ot in range(32):
                t.wait_ge(lwb, 16 * (ot + 1))
                for th in range(2):
                    u = 2 * ot + th
                    if u >= 1:
                        t.wait_ge(sgl, u)
                    for j in range(32):
                        t.matmul(psR2[:], wb[0][:, j, :], xnT[:, j, 512 * th:512 * (th + 1)],
                                 start=(j == 0), stop=False)
                    inst = t.matmul(psR2[:], bt8[:, 128 * ot:128 * (ot + 1)],
                                    hw[:, 512 * th:512 * (th + 1)], start=False, stop=True)
                    inst.then_inc(mmr, 1)
                t.wait_ge(lwo[ot % 2], 16 * (ot // 2 + 1))
                for th in range(2):
                    u = 2 * ot + th
                    if u >= 1:
                        t.wait_ge(ss1, u)
                    for j in range(32):
                        inst = t.matmul(psS3[:], wo[ot % 2][:, j, :], cxT[:, j, 512 * th:512 * (th + 1)],
                                        start=(j == 0), stop=(j == 31))
                    inst.then_inc(mmo, 1)

        @block.scalar
        def _(s):
            for tc in range(8):
                vb = VGA_CH(tc)
                s.wait_ge(vga, vb + 6)
                s.activation(mc[:, 3:4], mc[:, 2:3], AF.Sigmoid).then_inc(sga, 1)
                s.wait_ge(sga, 3 * tc + 1)
                s.activation(r1[:], msk1[:], AF.Copy, scale=mc[:, 3:4]).then_inc(sga, 1)
                s.wait_ge(vga, vb + 7)
                s.activation(r2[:], msk2[:], AF.Copy, scale=mc[:, 4:5]).then_inc(sga, 1)
            for ot in range(32):
                for th in range(2):
                    u = 2 * ot + th
                    s.wait_ge(mmr, u + 1)
                    if u >= 2:
                        s.wait_ge(vvo, 2 * (u - 2) + 1)
                    s.activation(g1[th][:], psR2[:], AF.Silu, bias=bb[:, ot:ot + 1]).then_inc(sgl, 1)
                    s.wait_ge(mmo, u + 1)
                    if u >= 2:
                        s.wait_ge(vvo, 2 * (u - 2) + 1)
                    s.activation(s1[th][:], psS3[:], AF.Copy).then_inc(ss1, 1)

        @block.vector
        def _(v):
            v.wait_ge(mml, 2)
            v.tensor_copy(lg[:], psL[:]).then_inc(vga, 1)
            for tc in range(8):
                v.wait_ge(tra, tc + 1)
                v.tensor_copy(lgT[:, tc, :], psT[:, 0:8]).then_inc(vga, 1)
            for tc in range(8):
                vb = VGA_CH(tc)
                x = lgT[:, tc, :]
                if tc >= 1:
                    v.wait_ge(sga, 3 * tc)
                v.reduce_max(mc[:, 0:1], x, axis=AX.X).then_inc(vga, 1)
                v.wait_ge(vga, vb + 1)
                v.tensor_scalar(msk1[:], x, mc[:, 0:1], None, ALU.is_ge).then_inc(vga, 1)
                v.wait_ge(vga, vb + 2)
                v.scalar_tensor_tensor(l2c[:], msk1[:], NEG, x, ALU.mult, ALU.add).then_inc(vga, 1)
                v.wait_ge(vga, vb + 3)
                v.reduce_max(mc[:, 1:2], l2c[:], axis=AX.X).then_inc(vga, 1)
                v.wait_ge(vga, vb + 4)
                v.tensor_scalar(msk2[:], l2c[:], mc[:, 1:2], None, ALU.is_ge).then_inc(vga, 1)
                v.tensor_tensor(mc[:, 2:3], mc[:, 0:1], mc[:, 1:2], ALU.subtract).then_inc(vga, 1)
                v.wait_ge(sga, 3 * tc + 1)
                v.tensor_scalar(mc[:, 4:5], mc[:, 3:4], -1.0, 1.0, ALU.mult, ALU.add).then_inc(vga, 1)
                v.wait_ge(sga, 3 * tc + 3)
                if tc >= 1:
                    v.wait_ge(tra, 8 + tc)
                v.tensor_tensor(routeb[:], r1[:], r2[:], ALU.add).then_inc(vga, 1)
                v.wait_ge(tra, 8 + tc + 1)
                v.tensor_copy(routeT[:, 128 * tc:128 * (tc + 1)], psTb[:]).then_inc(vga, 1)
            for th in range(2):
                v.wait_ge(mmx, th + 1)
                v.tensor_copy(routeX[:, 512 * th:512 * (th + 1)], psRX[:]).then_inc(vga, 1)
            for th in range(2):
                v.wait_ge(mmh, th + 1)
                v.tensor_tensor(hw[:, 512 * th:512 * (th + 1)], psH[:],
                                routeX[:, 512 * th:512 * (th + 1)], ALU.mult).then_inc(vhw, 1)
            for u in range(64):
                ot, th = divmod(u, 2)
                v.wait_ge(sgl, u + 1)
                v.wait_ge(ss1, u + 1)
                v.wait_ge(lx1[u % 2], 16 * (u // 2 + 1))
                if u >= 1:
                    v.wait_ge(vvo, 2 * u)
                v.tensor_tensor(tt2[:], g1[th][:], s1[th][:], ALU.mult).then_inc(vvo, 1)
                v.wait_ge(vvo, 2 * u + 1)
                if u >= 2:
                    v.wait_ge(sow2[u % 2], 16 * (u // 2))
                v.tensor_tensor(ob[u % 2][:], tt2[:], x1t[u % 2][:],
                                ALU.add).then_inc(vvo, 1)
    return nc


def kernel_unfused(L0_attn_x, L0_attn_wq, L0_attn_wk, L0_attn_wv, L0_attn_wo,
           L0_ffn_gate_qweight, L0_ffn_gate_scales, L0_ffn_gate_qzeros,
           L0_ffn_up_qweight, L0_ffn_up_scales, L0_ffn_up_qzeros,
           L0_ffn_down_qweight, L0_ffn_down_scales, L0_ffn_down_qzeros,
           L1_attn_k_cache, L1_attn_v_cache, L1_attn_Wq, L1_attn_Wo,
           L1_attn_block_table, L1_ffn_W_base, L1_ffn_b_base, L1_ffn_W_gate,
           L1_ffn_A, L1_ffn_B):
    f32 = np.float32
    cores = list(range(NCORES))
    x2 = np.asarray(L0_attn_x, dtype=f32).reshape(8192, D)
    eye = np.eye(128, dtype=f32)

    mapsA = host_prep0_inputs(x2, np.asarray(L0_attn_wq, f32), np.asarray(L0_attn_wk, f32),
                              np.asarray(L0_attn_wv, f32))
    resA = run_bass_kernel_spmd(build_prep0(), mapsA, core_ids=cores).results
    xnT_c = [np.asarray(resA[c]["xnT"]) for c in range(NCORES)]

    mapsB = host_attn0_inputs([np.asarray(resA[c]["qT"]) for c in range(NCORES)],
                              [np.asarray(resA[c]["kT"]) for c in range(NCORES)],
                              [np.asarray(resA[c]["v"]) for c in range(NCORES)])
    resB = run_bass_kernel_spmd(build_attn0(), mapsB, core_ids=cores).results
    ctx_b = host_attn0_ctx(resB)

    mapsC, lp_gate, lp_down = host_dequant_inputs(
        np.asarray(L0_ffn_gate_qweight), np.asarray(L0_ffn_gate_scales, f32), np.asarray(L0_ffn_gate_qzeros),
        np.asarray(L0_ffn_up_qweight), np.asarray(L0_ffn_up_scales, f32), np.asarray(L0_ffn_up_qzeros),
        np.asarray(L0_ffn_down_qweight), np.asarray(L0_ffn_down_scales, f32), np.asarray(L0_ffn_down_qzeros))
    resC = run_bass_kernel_spmd(build_dequant(), mapsC, core_ids=cores).results
    Wg_full = np.concatenate([np.asarray(resC[c]["Wg"]) for c in range(NCORES)], axis=1)
    Wu_full = np.concatenate([np.asarray(resC[c]["Wu"]) for c in range(NCORES)], axis=1)
    Wd_full = np.concatenate([np.asarray(resC[c]["Wd"]) for c in range(NCORES)], axis=1)

    wgs = cc(Wg_full.reshape(D, 86, 128).transpose(1, 0, 2))
    wus = cc(Wu_full.reshape(D, 86, 128).transpose(1, 0, 2))
    mapsD = [dict(xnT=xnT_c[c], wgs=wgs, wus=wus) for c in range(NCORES)]
    resD = run_bass_kernel_spmd(build_ffn0(), mapsD, core_ids=cores).results

    wds = cc(Wd_full.reshape(11008, 32, 128).transpose(1, 0, 2))
    wo_perm = np.asarray(L0_attn_wo, f32)[:, lp_down]
    wosE = cc(wo_perm.reshape(D, 32, 128).transpose(1, 0, 2))
    one_r = np.ones((1, 128), f32)
    one_c = np.ones((128, 1), f32)
    mapsE = []
    for c in range(NCORES):
        b, m = c // 4, c % 4
        ctxT = cc(ctx_b[b][1024 * m:1024 * (m + 1), :].T)
        xTp = cc(x2[1024 * c:1024 * (c + 1)].T[lp_down])
        mapsE.append(dict(hT=np.asarray(resD[c]["hT"]), wds=wds, cxT=ctxT,
                          wos=wosE, xTp=xTp, one=one_r, onec=one_c))
    resE = run_bass_kernel_spmd(build_downwo(), mapsE, core_ids=cores).results
    inv_perm = np.argsort(lp_down)
    x1T_c = [cc(np.asarray(resE[c]["x1T"])[inv_perm]) for c in range(NCORES)]
    xn1T_c = [cc(np.asarray(resE[c]["xn1T"])[inv_perm]) for c in range(NCORES)]

    wqsF = cc(np.asarray(L1_attn_Wq, f32).T.reshape(D, 32, 128).transpose(1, 0, 2))
    kcF = np.asarray(L1_attn_k_cache, f32)
    vcF = np.asarray(L1_attn_v_cache, f32)
    bti = cc(np.asarray(L1_attn_block_table, np.int32).reshape(2, 128).T.copy())
    inv = 10000.0 ** (-np.arange(64) / 64.0)
    sc128 = f32(1.0 / np.sqrt(HD))
    mapsF = []
    for c in range(NCORES):
        pos = np.arange(1024 * (c % 4), 1024 * (c % 4) + 1024, dtype=np.float64)
        ang = pos[:, None] * inv[None, :]
        Ct = cc(np.concatenate([np.cos(ang), np.cos(ang)], 1).T.astype(f32)) * sc128
        St = cc(np.concatenate([-np.sin(ang), np.sin(ang)], 1).T.astype(f32)) * sc128
        mapsF.append(dict(xn1T=xn1T_c[c], wqs=wqsF, c1=Ct, s1=St,
                          kc=kcF, vc=vcF, bti=bti, eye=eye))
    resF = run_bass_kernel_spmd(build_attn1(), mapsF, core_ids=cores).results

    wosG = cc(np.asarray(L1_attn_Wo, f32).T.reshape(D, 32, 128).transpose(1, 0, 2))
    wbsG = cc(np.asarray(L1_ffn_W_base, f32).T.reshape(D, 32, 128).transpose(1, 0, 2))
    bbG = cc(np.asarray(L1_ffn_b_base, f32).reshape(32, 128).T)
    wg8G = cc(np.asarray(L1_ffn_W_gate, f32).T)
    atG = cc(np.asarray(L1_ffn_A, f32).transpose(2, 0, 1).reshape(D, 128))
    bt8G = cc(np.asarray(L1_ffn_B, f32).transpose(0, 2, 1).reshape(128, D))
    e16 = np.zeros((8, 128), f32)
    for e in range(8):
        e16[e, 16 * e:16 * (e + 1)] = 1.0
    mapsG = []
    for c in range(NCORES):
        arr = np.asarray(resF[c]["ctx1"])
        a6 = arr.reshape(8, 128, 4, 2, 4, 128)
        ctx1 = a6.transpose(3, 4, 1, 0, 2, 5).reshape(1024, D)
        mapsG.append(dict(xn1T=xn1T_c[c], cx1T=cc(ctx1.T), x1T=x1T_c[c],
                          wos=wosG, wbs=wbsG, bb=bbG, wg8=wg8G, at=atG,
                          bt8=bt8G, e16=e16, eye=eye))
    resG = run_bass_kernel_spmd(build_final(), mapsG, core_ids=cores).results

    out = np.empty((8192, D), dtype=f32)
    for c in range(NCORES):
        out[1024 * c:1024 * (c + 1)] = np.asarray(resG[c]["outT"]).T
    return out.reshape(2, 4096, 4096)



def build_mega(upto=99, dbg=()):
    nc = bass.Bass("TRN2", target_bir_lowering=False)
    _dbgd = {}
    def _mark_dbg(ap, nm):
        _dbgd[nm] = ap
        return ap
    XRO = nc.dram_tensor("xro", [S, D], F32, kind="ExternalInput").ap()
    XTP = nc.dram_tensor("xTp", [D, T], F32, kind="ExternalInput").ap()
    WQ = nc.dram_tensor("wq_t", [NH, D, HD], F32, kind="ExternalInput").ap()
    WK = nc.dram_tensor("wk_t", [NKV, D, HD], F32, kind="ExternalInput").ap()
    WV = nc.dram_tensor("wv", [D, KVD], F32, kind="ExternalInput").ap()
    CQ1 = nc.dram_tensor("cq1", [HD, T], F32, kind="ExternalInput").ap()
    SQ1 = nc.dram_tensor("sq1", [HD, T], F32, kind="ExternalInput").ap()
    CKF = nc.dram_tensor("ckf", [HD, S], F32, kind="ExternalInput").ap()
    SKF = nc.dram_tensor("skf", [HD, S], F32, kind="ExternalInput").ap()
    BIA = nc.dram_tensor("bia", [128, 64], F32, kind="ExternalInput").ap()
    M4 = nc.dram_tensor("m4", [128, 4, 512], F32, kind="ExternalInput").ap()
    EYE = nc.dram_tensor("eye", [128, 128], F32, kind="ExternalInput").ap()
    ONE = nc.dram_tensor("one", [1, 128], F32, kind="ExternalInput").ap()
    ONEC = nc.dram_tensor("onec", [128, 1], F32, kind="ExternalInput").ap()
    WOP = nc.dram_tensor("wop", [32, D, 128], F32, kind="ExternalInput").ap()
    GQ = nc.dram_tensor("gq", [D, 1376], I32, kind="ExternalInput").ap()
    GS = nc.dram_tensor("gs", [32, 11008], F32, kind="ExternalInput").ap()
    GZ = nc.dram_tensor("gz", [32, 1376], I32, kind="ExternalInput").ap()
    UQ = nc.dram_tensor("uq", [D, 1376], I32, kind="ExternalInput").ap()
    US = nc.dram_tensor("us", [32, 11008], F32, kind="ExternalInput").ap()
    UZ = nc.dram_tensor("uz", [32, 1376], I32, kind="ExternalInput").ap()
    DQ = nc.dram_tensor("dq", [11008, 512], I32, kind="ExternalInput").ap()
    DS = nc.dram_tensor("ds", [86, D], F32, kind="ExternalInput").ap()
    DZ = nc.dram_tensor("dz", [86, 512], I32, kind="ExternalInput").ap()
    WQ1 = nc.dram_tensor("wq1", [32, D, 128], F32, kind="ExternalInput").ap()
    C1T = nc.dram_tensor("c1t", [HD, T], F32, kind="ExternalInput").ap()
    S1T = nc.dram_tensor("s1t", [HD, T], F32, kind="ExternalInput").ap()
    KC = nc.dram_tensor("kc", [512, NKV, 16, HD], F32, kind="ExternalInput").ap()
    VC = nc.dram_tensor("vc", [512, NKV, 16, HD], F32, kind="ExternalInput").ap()
    BTI = nc.dram_tensor("bti", [128, 2], I32, kind="ExternalInput").ap()
    WO1 = nc.dram_tensor("wo1", [32, D, 128], F32, kind="ExternalInput").ap()
    WB1 = nc.dram_tensor("wb1", [32, D, 128], F32, kind="ExternalInput").ap()
    BB = nc.dram_tensor("bb", [128, 32], F32, kind="ExternalInput").ap()
    WG8 = nc.dram_tensor("wg8", [D, 8], F32, kind="ExternalInput").ap()
    AT = nc.dram_tensor("at", [D, 128], F32, kind="ExternalInput").ap()
    BT8 = nc.dram_tensor("bt8", [128, D], F32, kind="ExternalInput").ap()
    E16 = nc.dram_tensor("e16", [8, 128], F32, kind="ExternalInput").ap()
    CSEL = nc.dram_tensor("csel", [128, 256], BF16, kind="ExternalInput").ap()
    OUT = nc.dram_tensor("outT", [D, T], F32, kind="ExternalOutput").ap()
    def scr(nm, shape, dt):
        kw = dict(kind="ExternalOutput") if nm in dbg else {}
        return nc.dram_tensor(nm, shape, dt, **kw).ap()
    XNS = scr("xnT_s", [D, S], BF16)
    KTS = scr("kT_s", [KVD, S], BF16)
    VS = scr("v_s", [S, KVD], BF16)
    QTS = scr("qT_s", [D, T], BF16)
    WGSc = scr("Wg_s", [D, 11008], BF16)
    WUSc = scr("Wu_s", [D, 11008], BF16)
    WDSc = scr("Wd_s", [11008, D], BF16)
    HTS = scr("hT_s", [11008, T], BF16)
    X1S = scr("x1T_s", [D, T], F32)
    XN1S = scr("xn1T_s", [D, T], BF16)
    CX1S = scr("cx1T_s", [D, T], BF16)
    ZSG = scr("zsg_s", [32, 11008], BF16)
    ZSU = scr("zsu_s", [32, 11008], BF16)
    ZSD = scr("zsd_s", [86, 4096], BF16)
    CXS = scr("cx_s", [T, D], BF16)
    CTS = scr("ctxT_s", [D, T], BF16)

    outer = ExitStack()
    with outer:
        osb = lambda *a: outer.enter_context(nc.sbuf_tensor(*a))
        _cst = [outer]
        sem = lambda n: _cst[0].enter_context(nc.semaphore(n))
        sem2 = lambda n: [_cst[0].enter_context(nc.semaphore(f"{n}{i}")) for i in range(2)]

        def _phreset():
            nc.reset(previous_sbuf_base=nc.sbuf_base, previous_sbuf_top=nc.sbuf_top,
                     previous_psum_base=nc.psum_base, previous_psum_top=nc.psum_top,
                     previous_local_dram_base=nc.local_dram_base,
                     previous_shared_dram_base=nc.shared_dram_base)

        eyeb = osb("eyeb", [128, 128], BF16)
        oner = osb("oner", [1, 128], F32)
        onecs = osb("onecs", [128, 1], BF16)

        lcn = nc.alloc_semaphore(name="lcn")
        lcg = nc.alloc_semaphore(name="lcg")
        with nc.named_scope("C0"), nc.Block() as b:
            @b.gpsimd
            def _(g):
                g.dma_start(out=eyeb[:], in_=EYE).then_inc(lcg, 16)
                g.dma_start(out=onecs[:], in_=ONEC).then_inc(lcg, 16)
            @b.sync
            def _(sy):
                sy.dma_start(out=oner[:], in_=ONE).then_inc(lcn, 16)
                sy.wait_ge(lcn, 16)
                sy.wait_ge(lcg, 32)

        with ExitStack() as st0:
            sb = lambda *a: st0.enter_context(nc.sbuf_tensor(*a))
            _cst[0] = st0
            xt = [sb(f"p0xt{i}", [128, D], F32) for i in range(2)]
            junk = sb("p0junk", [128, D], BF16)
            stat = sb("p0stat", [128, 8], F32)
            xn = sb("p0xn", [128, D], BF16)
            xstg = [sb(f"p0stg{i}", [128, 32, 128], BF16) for i in range(2)]
            ptp = [st0.enter_context(nc.psum_tensor(f"p0ptp{i}", [128, 128], BF16)) for i in range(2)]
            lx = sem2("p0lx")
            s_ln = sem("p0s")
            v_ln = sem("p0v")
            tp = sem("p0tp")
            tc = sem("p0tc")
            stg = sem2("p0st")
            with nc.named_scope("P0"), nc.Block() as b:
                @b.sync
                def _(sy):
                    for i in range(32):
                        if i >= 2:
                            sy.wait_ge(s_ln, 4 * i - 4)
                            sy.wait_ge(v_ln, 7 * i - 13)
                        sy.dma_start(out=xt[i % 2][:], in_=XRO[128 * i:128 * (i + 1), :]).then_inc(lx[i % 2], 16)
                        if i >= 2:
                            sy.wait_ge(tc, 32 * (i - 1))
                            sy.dma_start(out=XNS.rearrange("(j p) t -> p j t", p=128)[:, :, 128 * (i - 2):128 * (i - 1)],
                                         in_=xstg[i % 2][:]).then_inc(stg[i % 2], 16)
                    for i in [30, 31]:
                        sy.wait_ge(tc, 32 * (i + 1))
                        sy.dma_start(out=XNS.rearrange("(j p) t -> p j t", p=128)[:, :, 128 * i:128 * (i + 1)],
                                     in_=xstg[i % 2][:]).then_inc(stg[i % 2], 16)
                    for par in range(2):
                        sy.wait_ge(stg[par], 16 * 16)
                @b.scalar
                def _(s):
                    for i in range(32):
                        s.wait_ge(lx[i % 2], 16 * (i // 2 + 1))
                        s.activation(junk[:], xt[i % 2][:], AF.Square,
                                     accum_out=stat[:, 1:2]).then_inc(s_ln, 1)
                        s.wait_ge(v_ln, 7 * i + 5)
                        s.activation(stat[:, 3:4], stat[:, 4:5], AF.Ln).then_inc(s_ln, 1)
                        s.wait_ge(v_ln, 7 * i + 6)
                        s.activation(stat[:, 5:6], stat[:, 4:5], AF.Exp).then_inc(s_ln, 1)
                        s.wait_ge(v_ln, 7 * i + 7)
                        s.wait_ge(s_ln, 4 * i + 3)
                        if i > 0:
                            s.wait_ge(tp, 32 * i)
                        s.activation(xn[:], xt[i % 2][:], AF.Identity, scale=stat[:, 5:6],
                                     bias=stat[:, 6:7]).then_inc(s_ln, 1)
                @b.vector
                def _(v):
                    for i in range(32):
                        v.wait_ge(s_ln, 4 * i + 1)
                        v.wait_ge(lx[i % 2], 16 * (i // 2 + 1))
                        v.reduce_sum(stat[:, 0:1], xt[i % 2][:], axis=AX.X).then_inc(v_ln, 1)
                        v.wait_ge(v_ln, 7 * i + 1)
                        v.tensor_scalar(stat[:, 2:3], stat[:, 0:1], 1.0 / D, None, ALU.mult).then_inc(v_ln, 1)
                        v.wait_ge(v_ln, 7 * i + 2)
                        v.tensor_tensor(stat[:, 7:8], stat[:, 2:3], stat[:, 2:3], ALU.mult).then_inc(v_ln, 1)
                        v.wait_ge(v_ln, 7 * i + 3)
                        v.scalar_tensor_tensor(stat[:, 3:4], stat[:, 1:2], 1.0 / D, stat[:, 7:8],
                                               ALU.mult, ALU.subtract).then_inc(v_ln, 1)
                        v.wait_ge(v_ln, 7 * i + 4)
                        v.tensor_scalar(stat[:, 4:5], stat[:, 3:4], EPS, None, ALU.add).then_inc(v_ln, 1)
                        v.wait_ge(s_ln, 4 * i + 2)
                        v.tensor_scalar(stat[:, 4:5], stat[:, 3:4], -0.5, None, ALU.mult).then_inc(v_ln, 1)
                        v.wait_ge(s_ln, 4 * i + 3)
                        v.scalar_tensor_tensor(stat[:, 6:7], stat[:, 2:3], -1.0, stat[:, 5:6],
                                               ALU.mult, ALU.mult).then_inc(v_ln, 1)
                        for j in range(32):
                            n = 32 * i + j
                            v.wait_ge(tp, n + 1)
                            if i >= 2 and j == 0:
                                v.wait_ge(stg[i % 2], 16 * (i // 2))
                            v.tensor_copy(xstg[i % 2][:, j, :], ptp[n % 2][:]).then_inc(tc, 1)
                @b.tensor
                def _(t):
                    for i in range(32):
                        t.wait_ge(s_ln, 4 * i + 4)
                        for j in range(32):
                            n = 32 * i + j
                            if n >= 2:
                                t.wait_ge(tc, n - 1)
                            t.transpose(ptp[n % 2][:], xn[:, 128 * j:128 * (j + 1)], eyeb[:]).then_inc(tp, 1)

        if upto <= 1:
            return nc

        with ExitStack() as st1:
            sb = lambda *a: st1.enter_context(nc.sbuf_tensor(*a))
            _cst[0] = st1
            wks = sb("p1wks", [128, 8, 32, 128], BF16)
            xc = [sb(f"p1xc{i}", [128, 32, 512], BF16) for i in range(2)]
            ckfs = sb("ckfs", [128, S], F32)
            skfs = sb("skfs", [128, S], F32)
            rt1 = sb("p1rt1", [128, 512], F32)
            rt2 = sb("p1rt2", [128, 512], F32)
            ko = [sb(f"p1ko{i}", [128, 512], BF16) for i in range(2)]
            pk = [st1.enter_context(nc.psum_tensor(f"p1pk{i}", [128, 512], F32)) for i in range(2)]
            lwk = sem("p1lwk")
            ltb = sem("p1ltb")
            lxc = sem2("p1lxc")
            mmk = sem("p1mmk")
            vvk = sem("p1vvk")
            evk = sem("p1evk")
            ssk = sem2("p1ssk")
            with nc.named_scope("P1k"), nc.Block() as b:
                @b.gpsimd
                def _(g):
                    for gg in range(8):
                        g.dma_start(out=wks[:, gg, :, :],
                                    in_=WK[gg].rearrange("(j p) n -> p j n", p=128)).then_inc(lwk, 16)
                    for tcx in range(8):
                        if tcx >= 2:
                            g.wait_ge(mmk, 8 * (tcx - 1))
                        g.dma_start(out=xc[tcx % 2][:],
                                    in_=XNS.rearrange("(j p) t -> p j t", p=128)[:, :, 512 * tcx:512 * (tcx + 1)]).then_inc(lxc[tcx % 2], 16)
                @b.sync
                def _(sy):
                    sy.dma_start(out=ckfs[:], in_=CKF).then_inc(ltb, 16)
                    sy.dma_start(out=skfs[:], in_=SKF).then_inc(ltb, 16)
                    for G in range(64):
                        tcx, gg = G // 8, G % 8
                        sy.wait_ge(evk, G + 1)
                        sy.dma_start(out=KTS[128 * gg:128 * (gg + 1), 512 * tcx:512 * (tcx + 1)],
                                     in_=ko[G % 2][:]).then_inc(ssk[G % 2], 16)
                    for par in range(2):
                        sy.wait_ge(ssk[par], 16 * 32)
                @b.tensor
                def _(t):
                    t.wait_ge(lwk, 16 * 8)
                    for G in range(64):
                        tcx, gg = G // 8, G % 8
                        if gg == 0:
                            t.wait_ge(lxc[tcx % 2], 16 * (tcx // 2 + 1))
                        if G >= 2:
                            t.wait_ge(evk, G - 1)
                        for j in range(32):
                            inst = t.matmul(pk[G % 2][:], wks[:, gg, j, :], xc[tcx % 2][:, j, :],
                                            start=(j == 0), stop=(j == 31))
                        inst.then_inc(mmk, 1)
                @b.vector
                def _(v):
                    v.wait_ge(ltb, 32)
                    for G in range(64):
                        tcx, gg = G // 8, G % 8
                        sl = slice(512 * tcx, 512 * (tcx + 1))
                        p = pk[G % 2]
                        v.wait_ge(mmk, G + 1)
                        if G >= 1:
                            v.wait_ge(evk, G)
                        if G >= 2:
                            v.wait_ge(ssk[G % 2], 16 * (G // 2))
                        vb = 3 * G
                        v.tensor_tensor(rt1[:], p[:], ckfs[:, sl], ALU.mult).then_inc(vvk, 1)
                        v.tensor_tensor(rt2[0:64, :], p[64:128, :], skfs[0:64, sl], ALU.mult).then_inc(vvk, 1)
                        v.tensor_tensor(rt2[64:128, :], p[0:64, :], skfs[64:128, sl], ALU.mult).then_inc(vvk, 1)
                        v.wait_ge(vvk, vb + 3)
                        v.tensor_tensor(ko[G % 2][:], rt1[:], rt2[:], ALU.add).then_inc(evk, 1)

        if upto <= 2:
            return nc

        with ExitStack() as st1v:
            sb = lambda *a: st1v.enter_context(nc.sbuf_tensor(*a))
            _cst[0] = st1v
            wvs = sb("p1wvs", [128, 32, 1024], BF16)
            xc = [sb(f"p1vxc{i}", [128, 32, 512], BF16) for i in range(2)]
            vo = [sb(f"p1vo{i}", [128, 512], BF16) for i in range(2)]
            pv = [st1v.enter_context(nc.psum_tensor(f"p1pv{i}", [128, 512], F32)) for i in range(2)]
            lwv = sem("p1lwv")
            lxv = sem2("p1lxv")
            mmv = sem("p1mmv")
            evv = sem("p1evv")
            ssv = sem2("p1ssv")
            with nc.named_scope("P1v"), nc.Block() as b:
                @b.gpsimd
                def _(g):
                    g.dma_start(out=wvs[:], in_=WV.rearrange("(j p) n -> p j n", p=128)).then_inc(lwv, 16)
                    for tcx in range(8):
                        if tcx >= 2:
                            g.wait_ge(mmv, 8 * (tcx - 1))
                        g.dma_start(out=xc[tcx % 2][:],
                                    in_=XNS.rearrange("(j p) t -> p j t", p=128)[:, :, 512 * tcx:512 * (tcx + 1)]).then_inc(lxv[tcx % 2], 16)
                @b.sync
                def _(sy):
                    for G in range(64):
                        tcx, tt, vh = G // 8, (G // 2) % 4, G % 2
                        sy.wait_ge(evv, G + 1)
                        sy.dma_start(out=VS[128 * (4 * tcx + tt):128 * (4 * tcx + tt + 1), 512 * vh:512 * (vh + 1)],
                                     in_=vo[G % 2][:]).then_inc(ssv[G % 2], 16)
                    for par in range(2):
                        sy.wait_ge(ssv[par], 16 * 32)
                @b.tensor
                def _(t):
                    t.wait_ge(lwv, 16)
                    for G in range(64):
                        tcx, tt, vh = G // 8, (G // 2) % 4, G % 2
                        if tt == 0 and vh == 0:
                            t.wait_ge(lxv[tcx % 2], 16 * (tcx // 2 + 1))
                        if G >= 2:
                            t.wait_ge(evv, G - 1)
                        for j in range(32):
                            inst = t.matmul(pv[G % 2][:], xc[tcx % 2][:, j, 128 * tt:128 * (tt + 1)],
                                            wvs[:, j, 512 * vh:512 * (vh + 1)],
                                            start=(j == 0), stop=(j == 31))
                        inst.then_inc(mmv, 1)
                @b.scalar
                def _(s):
                    for G in range(64):
                        s.wait_ge(mmv, G + 1)
                        if G >= 2:
                            s.wait_ge(ssv[G % 2], 16 * (G // 2))
                        s.activation(vo[G % 2][:], pv[G % 2][:], AF.Copy).then_inc(evv, 1)

        if upto <= 3:
            return nc

        with ExitStack() as st1q:
            sb = lambda *a: st1q.enter_context(nc.sbuf_tensor(*a))
            _cst[0] = st1q
            xq = sb("p1xq", [128, 32, 1024], BF16)
            wqb = [sb(f"p1wqb{i}", [128, 32, 128], BF16) for i in range(2)]
            cq1s = sb("cq1s", [128, T], F32)
            sq1s = sb("sq1s", [128, T], F32)
            rt1 = sb("p1qrt1", [128, 512], F32)
            rt2 = sb("p1qrt2", [128, 512], F32)
            qo = [sb(f"p1qo{i}", [128, 512], BF16) for i in range(2)]
            pq = [st1q.enter_context(nc.psum_tensor(f"p1pq{i}", [128, 512], F32)) for i in range(2)]
            lxq = sem("p1lxq")
            ltq = sem("p1ltq")
            lwq = sem2("p1lwq")
            mmq = sem("p1mmq")
            vvq = sem("p1vvq")
            evq = sem("p1evq")
            ssq = sem2("p1ssq")
            with nc.named_scope("P1q"), nc.Block() as b:
                @b.gpsimd
                def _(g):
                    for pi in range(32):
                        if pi >= 2:
                            g.wait_ge(mmq, 2 * pi - 2)
                        g.dma_start(out=wqb[pi % 2][:],
                                    in_=WQ[pi].rearrange("(j p) n -> p j n", p=128)).then_inc(lwq[pi % 2], 16)
                @b.sync
                def _(sy):
                    sy.dma_start(out=xq[:], in_=XNS.rearrange("(j p) t -> p j t", p=128)[:, :, 0:1024]).then_inc(lxq, 16)
                    sy.dma_start(out=cq1s[:], in_=CQ1).then_inc(ltq, 16)
                    sy.dma_start(out=sq1s[:], in_=SQ1).then_inc(ltq, 16)
                    for G in range(64):
                        pi, th = G // 2, G % 2
                        sy.wait_ge(evq, G + 1)
                        sy.dma_start(out=QTS[128 * pi:128 * (pi + 1), 512 * th:512 * (th + 1)],
                                     in_=qo[G % 2][:]).then_inc(ssq[G % 2], 16)
                    for par in range(2):
                        sy.wait_ge(ssq[par], 16 * 32)
                @b.tensor
                def _(t):
                    t.wait_ge(lxq, 16)
                    for G in range(64):
                        pi, th = G // 2, G % 2
                        if th == 0:
                            t.wait_ge(lwq[pi % 2], 16 * (pi // 2 + 1))
                        if G >= 2:
                            t.wait_ge(evq, G - 1)
                        for j in range(32):
                            inst = t.matmul(pq[G % 2][:], wqb[pi % 2][:, j, :],
                                            xq[:, j, 512 * th:512 * (th + 1)],
                                            start=(j == 0), stop=(j == 31))
                        inst.then_inc(mmq, 1)
                @b.vector
                def _(v):
                    v.wait_ge(ltq, 32)
                    for G in range(64):
                        pi, th = G // 2, G % 2
                        sl = slice(512 * th, 512 * (th + 1))
                        p = pq[G % 2]
                        v.wait_ge(mmq, G + 1)
                        if G >= 1:
                            v.wait_ge(evq, G)
                        if G >= 2:
                            v.wait_ge(ssq[G % 2], 16 * (G // 2))
                        vb = 3 * G
                        v.tensor_tensor(rt1[:], p[:], cq1s[:, sl], ALU.mult).then_inc(vvq, 1)
                        v.tensor_tensor(rt2[0:64, :], p[64:128, :], sq1s[0:64, sl], ALU.mult).then_inc(vvq, 1)
                        v.tensor_tensor(rt2[64:128, :], p[0:64, :], sq1s[64:128, sl], ALU.mult).then_inc(vvq, 1)
                        v.wait_ge(vvq, vb + 3)
                        v.tensor_tensor(qo[G % 2][:], rt1[:], rt2[:], ALU.add).then_inc(evq, 1)


        if upto <= 4:
            return nc
        _phreset()

        dqu = []
        for mi, (n_g, n_ch) in enumerate([(32, 11008), (32, 11008), (86, 4096)]):
            for g_ in range(n_g):
                off = 0
                while off < n_ch:
                    w = min(1024, n_ch - off)
                    dqu.append((mi, g_, off, w))
                    off += w
        NDQ = len(dqu)
        dq_sched = {}
        ui = 0
        for ag in range(64):
            take = (NDQ * (ag + 1)) // 64 - (NDQ * ag) // 64
            dq_sched[ag] = list(range(ui, ui + take))
            ui += take
        assert ui == NDQ
        dq_qw = [GQ, UQ, DQ]
        dq_ss = [GS, US, DS]
        dq_ws = [WGSc, WUSc, WDSc]

        with ExitStack() as st24:
            sb = lambda *a: st24.enter_context(nc.sbuf_tensor(*a))
            _cst[0] = st24
            ktg = [sb(f"p2kt{i}", [128, S], BF16) for i in range(2)]
            vg = [sb(f"p2vg{i}", [128, 32, 129], BF16) for i in range(2)]
            qth = [sb(f"p2qt{i}", [128, T], BF16) for i in range(2)]
            biat = sb("p2bia", [128, 64], F32)
            m4s = sb("p2m4", [128, 4, 512], F32)
            expb = [sb(f"p2ex{i}", [128, 512], BF16) for i in range(2)]
            sxp = [sb(f"p2sx{i}", [128, 512], F32) for i in range(2)]
            rcp = sb("p2rcp", [128, 4], F32)
            cbig = [sb(f"p2cb{i}", [128, 4, 128], BF16) for i in range(2)]
            onerb = sb("p4onerb", [1, 128], BF16)
            qwt = [sb(f"p4qw{i}", [128, 128], I32) for i in range(2)]
            sbr = [sb(f"p4sb{i}", [1, 1024], BF16) for i in range(2)]
            nibi = sb("p4nibi", [128, 1024], I32)
            wout = [sb(f"p4wo{i}", [128, 1024], BF16) for i in range(4)]
            scp = [st24.enter_context(nc.psum_tensor(f"p2scp{i}", [128, 512], F32)) for i in range(2)]
            ctp = [st24.enter_context(nc.psum_tensor(f"p2ctp{i}", [128, 129], F32)) for i in range(4)]
            psS = st24.enter_context(nc.psum_tensor("p4psS", [128, 1024], F32))
            l2k = sem2("p2lk")
            l2v = sem2("p2lv")
            l2q = sem2("p2lq")
            l2c = sem("p2lc")
            msv = sem("p2msv")
            msm = sem("p2msm")
            p2mm = sem("p2mm")
            p2vm = sem("p2vm")
            p2ex = sem("p2ex")
            p2av = sem("p2av")
            p2rc = sem("p2rc")
            p2ce = sem("p2ce")
            p2cs = sem2("p2cs")
            l4q = sem2("p4lq")
            l4s = sem2("p4ls")
            p4bc = sem("p4bc")
            p4un = sem("p4un")
            p4tt = sem("p4tt")
            p4st = [_cst[0].enter_context(nc.semaphore(f"p4st{i}")) for i in range(4)]

            def window(qc, kt):
                return 4 * qc <= kt < 4 * qc + 4

            wm_idx = {}
            wmc = 0
            for ag in range(64):
                qc = ag % 2
                for kt in range(32):
                    if window(qc, kt):
                        wm_idx[32 * ag + kt] = wmc
                        wmc += 1

            with nc.named_scope("P24"), nc.Block() as b:
                @b.sync
                def _(sy):
                    sy.dma_start(out=biat[:], in_=BIA).then_inc(l2c, 16)
                    sy.dma_start(out=m4s[:], in_=M4).then_inc(l2c, 16)
                    for ag in range(64):
                        g, qh, qc = ag // 8, (ag // 2) % 4, ag % 2
                        h = 4 * g + qh
                        if qh == 0 and qc == 0:
                            if g >= 2:
                                sy.wait_ge(p2av, 32 * 8 * (g - 1))
                            sy.dma_start(out=ktg[g % 2][:], in_=KTS[128 * g:128 * (g + 1), :]).then_inc(l2k[g % 2], 16)
                            sy.dma_start(out=vg[g % 2][:, :, 0:128],
                                         in_=VS.rearrange("(kt p) n -> p kt n", p=128)[:, :, 128 * g:128 * (g + 1)]).then_inc(l2v[g % 2], 16)
                        if qc == 0:
                            if h >= 2:
                                sy.wait_ge(p2av, 32 * 2 * (h - 1))
                            sy.dma_start(out=qth[h % 2][:], in_=QTS[128 * h:128 * (h + 1), :]).then_inc(l2q[h % 2], 16)
                        if ag >= 2:
                            ag2 = ag - 2
                            g2, qh2, qc2 = ag2 // 8, (ag2 // 2) % 4, ag2 % 2
                            sy.wait_ge(p2ce, 4 * ag2 + 4)
                            sy.dma_start(out=CXS.rearrange("(s p) d -> p s d", p=128)[:, 4 * qc2:4 * qc2 + 4, 128 * (4 * g2 + qh2):128 * (4 * g2 + qh2 + 1)],
                                         in_=cbig[ag2 % 2][:]).then_inc(p2cs[ag2 % 2], 16)
                        for u in dq_sched[ag]:
                            mi, g_, off, wdt = dqu[u]
                            if u >= 4:
                                u2 = u - 4
                                mi2, g2_, off2, wdt2 = dqu[u2]
                                sy.wait_ge(p4tt, u2 + 1)
                                sy.dma_start(out=dq_ws[mi2][128 * g2_:128 * (g2_ + 1), off2:off2 + wdt2],
                                             in_=wout[u2 % 4][:, 0:wdt2]).then_inc(p4st[u2 % 4], 16)
                    for u2 in range(NDQ - 4, NDQ):
                        mi2, g2_, off2, wdt2 = dqu[u2]
                        sy.wait_ge(p4tt, u2 + 1)
                        sy.dma_start(out=dq_ws[mi2][128 * g2_:128 * (g2_ + 1), off2:off2 + wdt2],
                                     in_=wout[u2 % 4][:, 0:wdt2]).then_inc(p4st[u2 % 4], 16)
                    for ag2 in [62, 63]:
                        g2, qh2, qc2 = ag2 // 8, (ag2 // 2) % 4, ag2 % 2
                        sy.wait_ge(p2ce, 4 * ag2 + 4)
                        sy.dma_start(out=CXS.rearrange("(s p) d -> p s d", p=128)[:, 4 * qc2:4 * qc2 + 4, 128 * (4 * g2 + qh2):128 * (4 * g2 + qh2 + 1)],
                                     in_=cbig[ag2 % 2][:]).then_inc(p2cs[ag2 % 2], 16)
                    for par in range(2):
                        sy.wait_ge(p2cs[par], 16 * 32)
                    for par in range(4):
                        sy.wait_ge(p4st[par], 16 * len([u for u in range(NDQ) if u % 4 == par]))

                @b.gpsimd
                def _(g_e):
                    g_e.dma_start(out=onerb[:], in_=ONE).then_inc(msv, 16)
                    g_e.memset(vg[0][:, :, 128:129], 1.0).then_inc(msm, 1)
                    g_e.memset(vg[1][:, :, 128:129], 1.0).then_inc(msm, 1)
                    for u in range(NDQ):
                        mi, g_, off, wdt = dqu[u]
                        if u >= 2:
                            g_e.wait_ge(p4un, u - 1)
                        g_e.dma_start(out=qwt[u % 2][:, 0:wdt // 8],
                                      in_=dq_qw[mi].rearrange("(gg p) ccc -> p gg ccc", p=128)[:, g_, off // 8:(off + wdt) // 8]).then_inc(l4q[u % 2], 16)
                        if u >= 2:
                            g_e.wait_ge(p4bc, u - 1)
                        g_e.dma_start(out=sbr[u % 2][:, 0:wdt],
                                      in_=dq_ss[mi][g_:g_ + 1, off:off + wdt]).then_inc(l4s[u % 2], 16)

                @b.tensor
                def _(t):
                    t.wait_ge(msv, 16)
                    t.wait_ge(msm, 2)
                    t.wait_ge(l2c, 32)
                    pend = None
                    for ag in range(64):
                        g, qh, qc = ag // 8, (ag // 2) % 4, ag % 2
                        h = 4 * g + qh
                        for kt in range(32):
                            n = 32 * ag + kt
                            if kt == 0:
                                t.wait_ge(l2k[g % 2], 16 * (g // 2 + 1))
                                t.wait_ge(l2v[g % 2], 16 * (g // 2 + 1))
                                t.wait_ge(l2q[h % 2], 16 * (h // 2 + 1))
                            if n >= 2:
                                t.wait_ge(p2ex, n - 1)
                            t.matmul(scp[n % 2][:], ktg[g % 2][:, 128 * kt:128 * (kt + 1)],
                                     qth[h % 2][:, 512 * qc:512 * (qc + 1)],
                                     start=True, stop=True).then_inc(p2mm, 1)
                            if pend is not None:
                                _mega_av(t, pend, expb, vg, ctp, p2ex, p2ce, p2av)
                            pend = n
                        _mega_av(t, pend, expb, vg, ctp, p2ex, p2ce, p2av)
                        pend = None
                        for u in dq_sched[ag]:
                            mi, g_, off, wdt = dqu[u]
                            t.wait_ge(l4s[u % 2], 16 * (u // 2 + 1))
                            if u >= 1:
                                t.wait_ge(p4tt, u)
                            nch = (wdt + 511) // 512
                            for ci in range(nch):
                                cw = min(512, wdt - 512 * ci)
                                inst = t.matmul(psS[:, 512 * ci:512 * ci + cw], onerb[:],
                                                sbr[u % 2][:, 512 * ci:512 * ci + cw], start=True, stop=True)
                            inst.then_inc(p4bc, 1)

                @b.scalar
                def _(s):
                    for ag in range(64):
                        g, qh, qc = ag // 8, (ag // 2) % 4, ag % 2
                        for kt in range(32):
                            n = 32 * ag + kt
                            if n >= 2:
                                s.wait_ge(p2av, n - 1)
                            if window(qc, kt):
                                s.wait_ge(p2vm, wm_idx[n] + 1)
                                s.activation(expb[n % 2][:], sxp[n % 2][:], AF.Exp).then_inc(p2ex, 1)
                            else:
                                s.wait_ge(p2mm, n + 1)
                                s.activation(expb[n % 2][:], scp[n % 2][:], AF.Exp,
                                             bias=biat[:, 32 * qc + kt:32 * qc + kt + 1]).then_inc(p2ex, 1)
                        s.wait_ge(p2rc, 4 * (ag + 1))
                        if ag >= 2:
                            s.wait_ge(p2cs[ag % 2], 16 * (ag // 2))
                        for qs4 in range(4):
                            s.activation(cbig[ag % 2][:, qs4, :], ctp[qs4][:, 0:128], AF.Copy,
                                         scale=rcp[:, qs4:qs4 + 1]).then_inc(p2ce, 1)


                @b.vector
                def _(v):
                    v.wait_ge(l2c, 32)
                    for ag in range(64):
                        g, qh, qc = ag // 8, (ag // 2) % 4, ag % 2
                        for kt in range(32):
                            if not window(qc, kt):
                                continue
                            n = 32 * ag + kt
                            r = kt - 4 * qc
                            v.wait_ge(p2mm, n + 1)
                            if n >= 2:
                                v.wait_ge(p2ex, n - 1)
                            v.tensor_tensor(sxp[n % 2][:], scp[n % 2][:], m4s[:, r, :], ALU.add).then_inc(p2vm, 1)
                        for u in dq_sched[ag]:
                            mi, g_, off, wdt = dqu[u]
                            nb = wdt // 128
                            v.wait_ge(l4q[u % 2], 16 * (u // 2 + 1))
                            if u >= 1:
                                v.wait_ge(p4tt, u)
                            qview = qwt[u % 2][:, 0:wdt // 8].rearrange("p (nb c) -> p nb c", c=16)
                            nview = nibi[:, 0:wdt].rearrange("p (nb c) -> p nb c", c=128)
                            for pl in range(8):
                                inst = v.tensor_scalar(nview[:, :, 16 * pl:16 * (pl + 1)], qview,
                                                       4 * ARO[pl], 15,
                                                       ALU.logical_shift_right, ALU.bitwise_and)
                            inst.then_inc(p4un, 1)
                            v.wait_ge(p4un, u + 1)
                            v.wait_ge(p4bc, u + 1)
                            if u >= 4:
                                v.wait_ge(p4st[u % 4], 16 * (u // 4))
                            v.scalar_tensor_tensor(wout[u % 4][:, 0:wdt], nibi[:, 0:wdt], -8.0,
                                                   psS[:, 0:wdt], ALU.add, ALU.mult).then_inc(p4tt, 1)
                        v.wait_ge(p2av, 32 * (ag + 1))
                        if ag >= 1:
                            v.wait_ge(p2ce, 4 * ag)
                        for qs4 in range(4):
                            v.reciprocal(rcp[:, qs4:qs4 + 1], ctp[qs4][:, 128:129]).then_inc(p2rc, 1)

        if upto <= 5:
            return nc

        with ExitStack() as st3:
            sb = lambda *a: st3.enter_context(nc.sbuf_tensor(*a))
            _cst[0] = st3
            cxr = [sb(f"p3cx{i}", [128, D], BF16) for i in range(2)]
            cxstg = [sb(f"p3st{i}", [128, 32, 128], BF16) for i in range(2)]
            ptp3 = [st3.enter_context(nc.psum_tensor(f"p3tp{i}", [128, 128], BF16)) for i in range(2)]
            l3 = sem2("p3l")
            tp3 = sem("p3tp")
            tc3 = sem("p3tc")
            s3g = sem2("p3sg")
            with nc.named_scope("P3"), nc.Block() as b:
                @b.sync
                def _(sy):
                    for qt in range(8):
                        if qt >= 2:
                            sy.wait_ge(tp3, 32 * (qt - 1))
                        sy.dma_start(out=cxr[qt % 2][:], in_=CXS[128 * qt:128 * (qt + 1), :]).then_inc(l3[qt % 2], 16)
                        if qt >= 2:
                            sy.wait_ge(tc3, 32 * (qt - 1))
                            sy.dma_start(out=CTS.rearrange("(j p) t -> p j t", p=128)[:, :, 128 * (qt - 2):128 * (qt - 1)],
                                         in_=cxstg[qt % 2][:]).then_inc(s3g[qt % 2], 16)
                    for qt in [6, 7]:
                        sy.wait_ge(tc3, 32 * (qt + 1))
                        sy.dma_start(out=CTS.rearrange("(j p) t -> p j t", p=128)[:, :, 128 * qt:128 * (qt + 1)],
                                     in_=cxstg[qt % 2][:]).then_inc(s3g[qt % 2], 16)
                    for par in range(2):
                        sy.wait_ge(s3g[par], 16 * 4)
                @b.tensor
                def _(t):
                    for qt in range(8):
                        t.wait_ge(l3[qt % 2], 16 * (qt // 2 + 1))
                        for j in range(32):
                            n = 32 * qt + j
                            if n >= 2:
                                t.wait_ge(tc3, n - 1)
                            t.transpose(ptp3[n % 2][:], cxr[qt % 2][:, 128 * j:128 * (j + 1)], eyeb[:]).then_inc(tp3, 1)
                @b.vector
                def _(v):
                    for qt in range(8):
                        for j in range(32):
                            n = 32 * qt + j
                            v.wait_ge(tp3, n + 1)
                            if qt >= 2 and j == 0:
                                v.wait_ge(s3g[qt % 2], 16 * (qt // 2))
                            v.tensor_copy(cxstg[qt % 2][:, j, :], ptp3[n % 2][:]).then_inc(tc3, 1)

        if upto <= 6:
            return nc

        z_chunks = [(GZ, GS, ZSG, 32, 688, 5504, 0), (GZ, GS, ZSG, 32, 688, 5504, 5504),
                    (UZ, US, ZSU, 32, 688, 5504, 0), (UZ, US, ZSU, 32, 688, 5504, 5504),
                    (DZ, DS, ZSD, 86, 512, 4096, 0)]
        with ExitStack() as stz:
            sb = lambda *a: stz.enter_context(nc.sbuf_tensor(*a))
            _cst[0] = stz
            zqc = sb("pzq", [86, 688], I32)
            nzc = sb("pzn", [86, 5504], I32)
            ssc = sb("pzs", [86, 5504], F32)
            zoc = sb("pzo", [86, 5504], BF16)
            z3l = sem("pz3l")
            z3u = sem("pz3u")
            z3n = sem("pz3n")
            z3t = sem("pz3t")
            z3s = sem("pz3s")
            with nc.named_scope("P3z"), nc.Block() as b:
                @b.sync
                def _(sy):
                    for ci, (zsrc, ssrc, dst, P, NI, NC, off) in enumerate(z_chunks):
                        if ci >= 1:
                            sy.wait_ge(z3u, 8 * ci)
                            sy.wait_ge(z3t, ci)
                        sy.dma_start(out=zqc[0:P, 0:NI], in_=zsrc[:, off // 8:off // 8 + NI]).then_inc(z3l, 16)
                        sy.dma_start(out=ssc[0:P, 0:NC], in_=ssrc[:, off:off + NC]).then_inc(z3l, 16)
                        sy.wait_ge(z3t, ci + 1)
                        sy.dma_start(out=dst[:, off:off + NC], in_=zoc[0:P, 0:NC]).then_inc(z3s, 16)
                    sy.wait_ge(z3s, 16 * len(z_chunks))
                @b.vector
                def _(v):
                    for ci, (zsrc, ssrc, dst, P, NI, NC, off) in enumerate(z_chunks):
                        v.wait_ge(z3l, 32 * (ci + 1))
                        if ci >= 1:
                            v.wait_ge(z3t, ci)
                        qv = zqc[0:P, 0:NI].rearrange("p (nb c) -> p nb c", c=16)
                        nv = nzc[0:P, 0:NC].rearrange("p (nb c) -> p nb c", c=128)
                        for pl in range(8):
                            inst = v.tensor_scalar(nv[:, :, 16 * pl:16 * (pl + 1)], qv,
                                                   4 * ARO[pl], 15,
                                                   ALU.logical_shift_right, ALU.bitwise_and)
                        inst.then_inc(z3u, 8)
                        v.tensor_scalar(ssc[0:P, 0:NC], ssc[0:P, 0:NC], -1.0, None, ALU.mult).then_inc(z3n, 1)
                        v.wait_ge(z3u, 8 * ci + 8)
                        v.wait_ge(z3n, ci + 1)
                        if ci >= 1:
                            v.wait_ge(z3s, 16 * ci)
                        v.scalar_tensor_tensor(zoc[0:P, 0:NC], nzc[0:P, 0:NC], -8.0,
                                               ssc[0:P, 0:NC], ALU.add, ALU.mult).then_inc(z3t, 1)

        if upto <= 7:
            return nc
        _phreset()

        with ExitStack() as st5:
            sb = lambda *a: st5.enter_context(nc.sbuf_tensor(*a))
            _cst[0] = st5
            xq5 = sb("p5xq", [128, 32, T], BF16)
            cselb = sb("p5cs", [128, 256], BF16)
            xgs = sb("p5xg", [32, T], BF16)
            wg5 = [sb(f"p5wg{i}", [128, 32, 128], BF16) for i in range(2)]
            wu5 = [sb(f"p5wu{i}", [128, 32, 128], BF16) for i in range(2)]
            zg5 = [sb(f"p5zg{i}", [32, 128], BF16) for i in range(2)]
            zu5 = [sb(f"p5zu{i}", [32, 128], BF16) for i in range(2)]
            sg5 = [sb(f"p5sg{i}", [128, 512], F32) for i in range(2)]
            ht5 = [sb(f"p5ht{i}", [128, 512], BF16) for i in range(2)]
            psG = [st5.enter_context(nc.psum_tensor(f"p5pG{i}", [128, 512], F32)) for i in range(2)]
            psU = [st5.enter_context(nc.psum_tensor(f"p5pU{i}", [128, 512], F32)) for i in range(2)]
            pxg = st5.enter_context(nc.psum_tensor("p5px", [32, 512], F32))
            l5x = sem("p5lx")
            l5gw = sem2("p5lg")
            l5uw = sem2("p5lu")
            s5ht = sem2("p5sh")
            m5g = sem("p5mg")
            m5xg = sem("p5mx")
            x5e = sem("p5xe")
            s5gs = sem("p5sgs")
            h5v = sem("p5hv")
            with nc.named_scope("P5a"), nc.Block() as b:
                @b.sync
                def _(sy):
                    sy.dma_start(out=xq5[:], in_=XNS.rearrange("(j p) t -> p j t", p=128)[:, :, 0:T]).then_inc(l5x, 16)
                    sy.dma_start(out=cselb[:], in_=CSEL).then_inc(l5x, 16)
                    for gi in range(172):
                        nt, th = gi // 2, gi % 2
                        sy.wait_ge(h5v, gi + 1)
                        sy.dma_start(out=HTS[128 * nt:128 * (nt + 1), 512 * th:512 * (th + 1)],
                                     in_=ht5[gi % 2][:]).then_inc(s5ht[gi % 2], 16)
                    for par in range(2):
                        sy.wait_ge(s5ht[par], 16 * 86)
                @b.gpsimd
                def _(g):
                    for nt in range(86):
                        if nt >= 2:
                            g.wait_ge(m5g, 4 * (nt - 1))
                        g.dma_start(out=wg5[nt % 2][:],
                                    in_=WGSc.rearrange("(j p) n -> p j n", p=128)[:, :, 128 * nt:128 * (nt + 1)]).then_inc(l5gw[nt % 2], 16)
                        g.dma_start(out=zg5[nt % 2][:], in_=ZSG[:, 128 * nt:128 * (nt + 1)]).then_inc(l5gw[nt % 2], 16)
                        g.dma_start(out=wu5[nt % 2][:],
                                    in_=WUSc.rearrange("(j p) n -> p j n", p=128)[:, :, 128 * nt:128 * (nt + 1)]).then_inc(l5uw[nt % 2], 16)
                        g.dma_start(out=zu5[nt % 2][:], in_=ZSU[:, 128 * nt:128 * (nt + 1)]).then_inc(l5uw[nt % 2], 16)
                @b.tensor
                def _(t):
                    t.wait_ge(l5x, 32)
                    for th in range(2):
                        if th == 1:
                            t.wait_ge(x5e, 1)
                        for j in range(32):
                            inst = t.matmul(pxg[:], cselb[:, 127 - j:159 - j],
                                            xq5[:, j, 512 * th:512 * (th + 1)],
                                            start=(j == 0), stop=(j == 31))
                        inst.then_inc(m5xg, 1)
                    t.wait_ge(x5e, 2)
                    for gi in range(172):
                        nt, th = gi // 2, gi % 2
                        sl = slice(512 * th, 512 * (th + 1))
                        if th == 0:
                            t.wait_ge(l5gw[nt % 2], 32 * (nt // 2 + 1))
                            t.wait_ge(l5uw[nt % 2], 32 * (nt // 2 + 1))
                        if gi >= 2:
                            t.wait_ge(s5gs, gi - 1)
                            t.wait_ge(h5v, gi - 1)
                        for j in range(32):
                            t.matmul(psG[gi % 2][:], wg5[nt % 2][:, j, :], xq5[:, j, sl],
                                     start=(j == 0), stop=False)
                        t.matmul(psG[gi % 2][:], zg5[nt % 2][:], xgs[:, sl],
                                 start=False, stop=True).then_inc(m5g, 1)
                        for j in range(32):
                            t.matmul(psU[gi % 2][:], wu5[nt % 2][:, j, :], xq5[:, j, sl],
                                     start=(j == 0), stop=False)
                        t.matmul(psU[gi % 2][:], zu5[nt % 2][:], xgs[:, sl],
                                 start=False, stop=True).then_inc(m5g, 1)
                @b.scalar
                def _(s):
                    for th in range(2):
                        s.wait_ge(m5xg, th + 1)
                        s.activation(xgs[:, 512 * th:512 * (th + 1)], pxg[:], AF.Copy).then_inc(x5e, 1)
                    for gi in range(172):
                        s.wait_ge(m5g, 2 * gi + 1)
                        if gi >= 2:
                            s.wait_ge(h5v, gi - 1)
                        s.activation(sg5[gi % 2][:], psG[gi % 2][:], AF.Silu).then_inc(s5gs, 1)
                @b.vector
                def _(v):
                    for gi in range(172):
                        v.wait_ge(m5g, 2 * gi + 2)
                        v.wait_ge(s5gs, gi + 1)
                        if gi >= 2:
                            v.wait_ge(s5ht[gi % 2], 16 * (gi // 2))
                        v.tensor_tensor(ht5[gi % 2][:], sg5[gi % 2][:], psU[gi % 2][:], ALU.mult).then_inc(h5v, 1)

        if upto <= 8:
            return nc

        st5o = ExitStack()
        st5o.__enter__()
        g0h = st5o.enter_context(nc.sbuf_tensor("p5g0", [128, 32, T], BF16))
        with ExitStack() as st5e:
            sb = lambda *a: st5e.enter_context(nc.sbuf_tensor(*a))
            _cst[0] = st5e
            hth = sb("p5hth", [128, 86, 512], BF16)
            cse2 = sb("p5cs2", [128, 256], BF16)
            hgs = sb("p5hgs", [86, 512], BF16)
            wd = [sb(f"p5wd{i}", [128, 86, 128], BF16) for i in range(2)]
            zd5 = [sb(f"p5zd{i}", [86, 128], BF16) for i in range(2)]
            psD = st5e.enter_context(nc.psum_tensor("p5pD", [128, 512], F32))
            pshg = st5e.enter_context(nc.psum_tensor("p5phg", [86, 512], F32))
            l5h = sem("p5lh")
            l5c2 = sem("p5lc2")
            e5wd = sem2("p5ewd")
            m5d = sem("p5md")
            g5l = sem("p5gl")
            m5hg = sem("p5mhg")
            h5ge = sem("p5hge")
            with nc.named_scope("P5d"), nc.Block() as b:
                @b.sync
                def _(sy):
                    sy.dma_start(out=cse2[:], in_=CSEL[:, 0:256]).then_inc(l5c2, 16)
                    for hh in range(2):
                        if hh >= 1:
                            sy.wait_ge(m5d, 32 * hh)
                        sy.dma_start(out=hth[:], in_=HTS.rearrange("(j p) t -> p j t", p=128)[:, :, 512 * hh:512 * (hh + 1)]).then_inc(l5h, 16)
                    sy.wait_ge(g5l, 64)
                @b.gpsimd
                def _(g):
                    for hh in range(2):
                        for ot in range(32):
                            u = 32 * hh + ot
                            if u >= 2:
                                g.wait_ge(m5d, u - 1)
                            g.dma_start(out=wd[ot % 2][:],
                                        in_=WDSc.rearrange("(j p) n -> p j n", p=128)[:, :, 128 * ot:128 * (ot + 1)]).then_inc(e5wd[ot % 2], 16)
                            g.dma_start(out=zd5[ot % 2][:], in_=ZSD[:, 128 * ot:128 * (ot + 1)]).then_inc(e5wd[ot % 2], 16)
                @b.tensor
                def _(t):
                    for hh in range(2):
                        t.wait_ge(l5h, 16 * (hh + 1))
                        if hh == 0:
                            t.wait_ge(l5c2, 16)
                        else:
                            t.wait_ge(h5ge, hh)
                        for j in range(86):
                            inst = t.matmul(pshg[:], cse2[:, 127 - j:213 - j], hth[:, j, :],
                                            start=(j == 0), stop=(j == 85))
                        inst.then_inc(m5hg, 1)
                        t.wait_ge(h5ge, hh + 1)
                        for ot in range(32):
                            u = 32 * hh + ot
                            t.wait_ge(e5wd[ot % 2], 32 * (u // 2 + 1))
                            if u >= 1:
                                t.wait_ge(g5l, u)
                            for j in range(86):
                                t.matmul(psD[:], wd[ot % 2][:, j, :], hth[:, j, :],
                                         start=(j == 0), stop=False)
                            t.matmul(psD[:], zd5[ot % 2][:], hgs[:],
                                     start=False, stop=True).then_inc(m5d, 1)
                @b.scalar
                def _(s):
                    for hh in range(2):
                        s.wait_ge(m5hg, hh + 1)
                        if hh >= 1:
                            s.wait_ge(m5d, 32 * hh)
                        s.activation(hgs[:], pshg[:], AF.Copy).then_inc(h5ge, 1)
                        for ot in range(32):
                            u = 32 * hh + ot
                            s.wait_ge(m5d, u + 1)
                            s.activation(g0h[:, ot, 512 * hh:512 * (hh + 1)], psD[:], AF.Silu).then_inc(g5l, 1)

        with ExitStack() as st5w:
            sb = lambda *a: st5w.enter_context(nc.sbuf_tensor(*a))
            _cst[0] = st5w
            cxh5 = sb("p5cxh", [128, 32, 512], BF16)
            x1bh = sb("p5x1b", [128, 32, 512], BF16)
            wo5 = [sb(f"p5wo{i}", [128, 32, 128], BF16) for i in range(2)]
            xt5 = [sb(f"p5xt{i}", [128, 512], F32) for i in range(2)]
            s05 = [sb(f"p5s0{i}", [128, 512], F32) for i in range(2)]
            tsb = sb("p5tsb", [128, 512], F32)
            x1s = [sb(f"p5x1{i}", [128, 512], F32) for i in range(2)]
            x1q = [sb(f"p5x1q{i}", [128, 512], BF16) for i in range(2)]
            xn1 = [sb(f"p5xn{i}", [128, 512], BF16) for i in range(2)]
            mrow = sb("p5mr", [1, 512], F32)
            t1row = sb("p5t1", [1, 512], F32)
            vrow = sb("p5vr", [1, 512], F32)
            hrow = sb("p5hr", [1, 512], F32)
            rstdrow = sb("p5rs", [1, 512], F32)
            nbrow = sb("p5nbr", [1, 512], F32)
            psS2 = st5w.enter_context(nc.psum_tensor("p5pS2", [128, 512], F32))
            psSum = st5w.enter_context(nc.psum_tensor("p5pSm", [1, 512], F32))
            psSq = st5w.enter_context(nc.psum_tensor("p5pSq", [1, 512], F32))
            psR = st5w.enter_context(nc.psum_tensor("p5pR", [128, 512], F32))
            psN = st5w.enter_context(nc.psum_tensor("p5pN", [128, 512], F32))
            l5cx = sem("p5lcx")
            e5wo = sem2("p5ewo")
            e5xp = sem2("p5exp")
            e5x1 = sem2("p5ex1")
            e5xn = sem2("p5exn")
            m5s = sem("p5ms")
            m5q = sem("p5mq")
            s5c0 = sem("p5sc0")
            s5xc = sem("p5sxc")
            v5v = sem("p5vv")
            v5w = sem("p5vw")
            s5ln = sem("p5sln")
            b5cm = sem("p5bcm")
            v5n = sem("p5vn")
            with nc.named_scope("P5w"), nc.Block() as b:
                @b.sync
                def _(sy):
                    for h2 in range(2):
                        if h2 >= 1:
                            sy.wait_ge(m5s, 32 * h2)
                        sy.dma_start(out=cxh5[:], in_=CTS.rearrange("(j p) t -> p j t", p=128)[:, :, 512 * h2:512 * (h2 + 1)]).then_inc(l5cx, 16)
                        for ot in range(32):
                            u = 32 * h2 + ot
                            if u >= 2:
                                sy.wait_ge(v5v, 2 * (u - 2) + 2)
                            sy.dma_start(out=xt5[ot % 2][:], in_=XTP[128 * ot:128 * (ot + 1), 512 * h2:512 * (h2 + 1)]).then_inc(e5xp[u % 2], 16)
                            if ot >= 2:
                                sy.dma_start(out=X1S[128 * (ot - 2):128 * (ot - 1), 512 * h2:512 * (h2 + 1)],
                                             in_=x1s[ot % 2][:]).then_inc(e5x1[ot % 2], 16)
                        for ot in [30, 31]:
                            sy.wait_ge(v5v, 2 * (32 * h2 + ot) + 2)
                            sy.dma_start(out=X1S[128 * ot:128 * (ot + 1), 512 * h2:512 * (h2 + 1)],
                                         in_=x1s[ot % 2][:]).then_inc(e5x1[ot % 2], 16)
                        for ot in range(32):
                            sy.wait_ge(v5n, 2 * (32 * h2 + ot) + 2)
                            sy.dma_start(out=XN1S[128 * ot:128 * (ot + 1), 512 * h2:512 * (h2 + 1)],
                                         in_=xn1[ot % 2][:]).then_inc(e5xn[ot % 2], 16)
                    for par in range(2):
                        sy.wait_ge(e5x1[par], 16 * 32)
                        sy.wait_ge(e5xn[par], 16 * 32)
                @b.gpsimd
                def _(g):
                    for h2 in range(2):
                        for ot in range(32):
                            u = 32 * h2 + ot
                            if u >= 2:
                                g.wait_ge(m5s, u - 1)
                            g.dma_start(out=wo5[ot % 2][:], in_=WOP[ot].rearrange("(j p) n -> p j n", p=128)).then_inc(e5wo[ot % 2], 16)
                @b.tensor
                def _(t):
                    for h2 in range(2):
                        t.wait_ge(l5cx, 16 * (h2 + 1))
                        if h2 >= 1:
                            t.wait_ge(v5w, 6 * h2)
                            t.wait_ge(v5n, 64 * h2)
                        for ot in range(34):
                            if ot < 32:
                                u = 32 * h2 + ot
                                t.wait_ge(e5wo[ot % 2], 16 * (u // 2 + 1))
                                if u >= 1:
                                    t.wait_ge(s5c0, u)
                                for j in range(32):
                                    inst = t.matmul(psS2[:], wo5[ot % 2][:, j, :],
                                                    cxh5[:, j, :],
                                                    start=(j == 0), stop=(j == 31))
                                inst.then_inc(m5s, 1)
                            if ot >= 2:
                                o2 = ot - 2
                                u2 = 32 * h2 + o2
                                t.wait_ge(s5xc, 2 * u2 + 2)
                                t.matmul(psSum[:], onecs[:], x1bh[:, o2, :],
                                         start=(o2 == 0), stop=(o2 == 31))
                                inst = t.matmul(psSq[:], onecs[:], x1q[o2 % 2][:],
                                                start=(o2 == 0), stop=(o2 == 31))
                                inst.then_inc(m5q, 1)
                        t.wait_ge(s5ln, 2 * h2 + 2)
                        t.wait_ge(v5w, 6 * h2 + 6)
                        t.matmul(psR[:], oner[:], rstdrow[:], start=True, stop=True)
                        inst = t.matmul(psN[:], oner[:], nbrow[:], start=True, stop=True)
                        inst.then_inc(b5cm, 1)
                @b.scalar
                def _(s):
                    for h2 in range(2):
                        for ot in range(32):
                            u = 32 * h2 + ot
                            s.wait_ge(m5s, u + 1)
                            if u >= 2:
                                s.wait_ge(v5v, 2 * (u - 2) + 1)
                            s.activation(s05[ot % 2][:], psS2[:], AF.Copy).then_inc(s5c0, 1)
                            s.wait_ge(v5v, 2 * u + 2)
                            if h2 >= 1 and ot == 0:
                                s.wait_ge(v5n, 64 * h2)
                            if u >= 2:
                                s.wait_ge(m5q, u - 1)
                            s.activation(x1bh[:, ot, :], x1s[ot % 2][:], AF.Copy)
                            inst = s.activation(x1q[ot % 2][:], x1s[ot % 2][:], AF.Square)
                            inst.then_inc(s5xc, 2)
                        s.wait_ge(v5w, 6 * h2 + 4)
                        s.activation(vrow[:], hrow[:], AF.Ln).then_inc(s5ln, 1)
                        s.wait_ge(v5w, 6 * h2 + 5)
                        s.activation(rstdrow[:], hrow[:], AF.Exp).then_inc(s5ln, 1)
                @b.vector
                def _(v):
                    for h2 in range(2):
                        for ot in range(32):
                            u = 32 * h2 + ot
                            v.wait_ge(s5c0, u + 1)
                            v.wait_ge(e5xp[u % 2], 16 * (u // 2 + 1))
                            if h2 >= 1 and ot == 0:
                                v.wait_ge(v5n, 64 * h2)
                            if u >= 1:
                                v.wait_ge(v5v, 2 * u)
                            v.tensor_tensor(tsb[:], g0h[:, ot, 512 * h2:512 * (h2 + 1)],
                                            s05[ot % 2][:], ALU.mult).then_inc(v5v, 1)
                            v.wait_ge(v5v, 2 * u + 1)
                            if u >= 2:
                                v.wait_ge(e5x1[u % 2], 16 * (u // 2))
                            v.tensor_tensor(x1s[ot % 2][:], tsb[:], xt5[ot % 2][:], ALU.add).then_inc(v5v, 1)
                        v.wait_ge(m5q, 32 * h2 + 32)
                        vb = 6 * h2
                        v.tensor_scalar(mrow[:], psSum[:], 1.0 / D, None, ALU.mult).then_inc(v5w, 1)
                        v.wait_ge(v5w, vb + 1)
                        v.tensor_tensor(t1row[:], mrow[:], mrow[:], ALU.mult).then_inc(v5w, 1)
                        v.wait_ge(v5w, vb + 2)
                        v.scalar_tensor_tensor(vrow[:], psSq[:], 1.0 / D, t1row[:], ALU.mult, ALU.subtract).then_inc(v5w, 1)
                        v.wait_ge(v5w, vb + 3)
                        v.tensor_scalar(hrow[:], vrow[:], EPS, None, ALU.add).then_inc(v5w, 1)
                        v.wait_ge(s5ln, 2 * h2 + 1)
                        v.tensor_scalar(hrow[:], vrow[:], -0.5, None, ALU.mult).then_inc(v5w, 1)
                        v.wait_ge(s5ln, 2 * h2 + 2)
                        v.scalar_tensor_tensor(nbrow[:], mrow[:], -1.0, rstdrow[:], ALU.mult, ALU.mult).then_inc(v5w, 1)
                        v.wait_ge(v5v, 64 * h2 + 64)
                        v.wait_ge(b5cm, h2 + 1)
                        v.wait_ge(s5xc, 2 * (32 * h2 + 31) + 2)
                        for ot in range(32):
                            u = 32 * h2 + ot
                            if ot >= 1:
                                v.wait_ge(v5n, 2 * u)
                            v.tensor_tensor(tsb[:], x1bh[:, ot, :], psR[:], ALU.mult).then_inc(v5n, 1)
                            v.wait_ge(v5n, 2 * u + 1)
                            if u >= 2:
                                v.wait_ge(e5xn[u % 2], 16 * (u // 2))
                            v.tensor_tensor(xn1[ot % 2][:], tsb[:], psN[:], ALU.add).then_inc(v5n, 1)
        st5o.__exit__(None, None, None)

        if upto <= 9:
            return nc
        _phreset()

        with ExitStack() as st6:
            sb = lambda *a: st6.enter_context(nc.sbuf_tensor(*a))
            _cst[0] = st6
            ps = lambda *a: st6.enter_context(nc.psum_tensor(*a))
            xnT6 = sb("p6xn", [128, 32, T], BF16)
            q1T = sb("p6q1", [128, 32, T], BF16)
            wq6 = [sb(f"p6wq{i}", [128, 32, 128], BF16) for i in range(2)]
            c1s = sb("p6c1", [128, T], F32)
            s1s = sb("p6s1", [128, T], F32)
            rt1 = sb("p6r1", [128, 512], F32)
            rt2 = sb("p6r2", [128, 512], F32)
            idx6 = sb("p6idx", [128, 2], I32)
            kgB = sb("p6kg", [128, 2, 16, 128], BF16)
            vstage = sb("p6vs", [128, 2048], BF16)
            vgB = sb("p6vg", [128, 2, 16, 129], BF16)
            kT1 = sb("p6kt", [128, 32, 128], BF16)
            expb6 = [sb(f"p6ex{i}", [128, 512], BF16) for i in range(2)]
            rcp6 = sb("p6rc", [128, 4], F32)
            cxg = [sb(f"p6cx{i}", [128, 32, 128], BF16) for i in range(2)]
            cxs6 = sb("p6cs", [128, 2, T], BF16)
            pmm = [ps(f"p6pm{i}", [128, 512], F32) for i in range(2)]
            ptp = ps("p6tp", [128, 128], BF16)
            ctp6 = [ps(f"p6ct{i}", [128, 129], F32) for i in range(4)]
            ptp2 = ps("p6tp2", [128, 128], BF16)
            l6d = sem("p6ld")
            l6xn = sem("p6lxn")
            l6wq = sem2("p6lwq")
            kf = sem2("p6kf")
            vfp = sem2("p6vfp")
            s6f = sem("p6sf")
            msd = sem("p6msd")
            mmp = sem("p6mmp")
            evp = sem("p6evp")
            vvp = sem("p6vvp")
            vcp = sem("p6vcp")
            tpf = sem("p6tpf")
            tcf = sem("p6tcf")
            mmf = sem("p6mmf")
            exf = sem("p6exf")
            avf = sem("p6avf")
            vrc = sem("p6vrc")
            cte = sem("p6cte")
            tp6 = sem("p6tp6")
            tc6 = sem("p6tc6")

            def av6(t, n):
                g, ti = divmod(n, 256)
                kt = ti % 32
                half, j = kt // 16, kt % 16
                t.wait_ge(exf, n + 1)
                t.wait_ge(vcp, 2 * g + half + 1)
                if kt == 0:
                    t.wait_ge(msd, 1)
                    gidx = n // 32
                    if gidx >= 1:
                        t.wait_ge(cte, gidx)
                for qs4 in range(4):
                    inst = t.matmul(ctp6[qs4][:], expb6[n % 2][:, 128 * qs4:128 * (qs4 + 1)],
                                    vgB[:, half, j, :], start=(kt == 0), stop=(kt == 31))
                inst.then_inc(avf, 1)

            def ctx6_transposes(t, gg):
                for sl in range(32):
                    n6 = 32 * gg + sl
                    if sl == 0:
                        t.wait_ge(cte, 8 * (gg + 1))
                    if n6 >= 1:
                        t.wait_ge(tc6, n6)
                    t.transpose(ptp2[:], cxg[gg % 2][:, sl, :], eyeb[:]).then_inc(tp6, 1)

            def ctx6_copies(v, gg):
                for sl in range(32):
                    n6 = 32 * gg + sl
                    v.wait_ge(tp6, n6 + 1)
                    if sl == 0 and gg >= 1:
                        v.wait_ge(s6f, 16 * 2 * gg)
                    if sl == 16:
                        v.wait_ge(s6f, 16 * (2 * gg + 1))
                    qh6, th6, q46 = sl // 8, (sl // 4) % 2, sl % 4
                    o6 = 512 * th6 + 128 * q46
                    v.tensor_copy(cxs6[:, qh6 % 2, o6:o6 + 128], ptp2[:]).then_inc(tc6, 1)

            with nc.named_scope("P6"), nc.Block() as b:
                @b.sync
                def _(sy):
                    sy.dma_start(out=xnT6[:], in_=XN1S.rearrange("(j p) t -> p j t", p=128)).then_inc(l6xn, 16)
                    sy.dma_start(out=c1s[:], in_=C1T).then_inc(l6d, 16)
                    sy.dma_start(out=s1s[:], in_=S1T).then_inc(l6d, 16)
                    sy.dma_start(out=idx6[:], in_=BTI).then_inc(l6d, 16)
                    for k6 in range(2 * NKV):
                        gg, hh = k6 // 2, k6 % 2
                        sy.wait_ge(tc6, 32 * gg + 16 * (hh + 1))
                        sy.dma_start(out=CX1S.rearrange("(j p) t -> p j t", p=128)[:, 4 * gg + 2 * hh:4 * gg + 2 * hh + 2, :],
                                     in_=cxs6[:].rearrange("p a t -> p (a t)")).then_inc(s6f, 16)
                    sy.wait_ge(s6f, 16 * 2 * NKV)
                @b.gpsimd
                def _(g_):
                    g_.memset(vgB[:, :, :, 128:129], 1.0).then_inc(msd, 1)
                    for pi in range(32):
                        if pi >= 2:
                            g_.wait_ge(mmp, 2 * pi - 2)
                        g_.dma_start(out=wq6[pi % 2][:], in_=WQ1[pi].rearrange("(j p) n -> p j n", p=128)).then_inc(l6wq[pi % 2], 16)
                    g_.wait_ge(l6d, 48)
                    kcf = KC.rearrange("b h j d -> b (h j d)")
                    vcf = VC.rearrange("b h j d -> b (h j d)")
                    for g in range(NKV):
                        if g >= 1:
                            g_.wait_ge(tpf, 32 * g)
                        for half in range(2):
                            g_.indirect_dma_start(
                                out=kgB[:, half, :, :].rearrange("p a b -> p (a b)"), out_offset=None,
                                in_=kcf, in_offset=IndirectOffsetOnAxis(idx6[:, half:half + 1], axis=0),
                                element_offset=g * 2048).then_inc(kf[half], 16)
                            if g >= 1 or half >= 1:
                                g_.wait_ge(vcp, 2 * g + half)
                            g_.indirect_dma_start(
                                out=vstage[:], out_offset=None,
                                in_=vcf, in_offset=IndirectOffsetOnAxis(idx6[:, half:half + 1], axis=0),
                                element_offset=g * 2048).then_inc(vfp[half], 16)
                @b.tensor
                def _(t):
                    t.wait_ge(l6xn, 16)
                    for G in range(64):
                        pi, th = G // 2, G % 2
                        if th == 0:
                            t.wait_ge(l6wq[pi % 2], 16 * (pi // 2 + 1))
                        if G >= 2:
                            t.wait_ge(evp, G - 1)
                        for j in range(32):
                            inst = t.matmul(pmm[G % 2][:], wq6[pi % 2][:, j, :],
                                            xnT6[:, j, 512 * th:512 * (th + 1)],
                                            start=(j == 0), stop=(j == 31))
                        inst.then_inc(mmp, 1)
                    pending = None
                    for g in range(NKV):
                        if pending is not None:
                            av6(t, pending)
                            pending = None
                        if g >= 1:
                            ctx6_transposes(t, g - 1)
                        for kt in range(32):
                            n_t = 32 * g + kt
                            t.wait_ge(kf[kt // 16], 16 * (g + 1))
                            if n_t >= 1:
                                t.wait_ge(tcf, n_t)
                            t.transpose(ptp[:], kgB[:, kt // 16, kt % 16, :], eyeb[:]).then_inc(tpf, 1)
                        for ti in range(256):
                            n = 256 * g + ti
                            qh, th, kt = ti // 64, (ti // 32) % 2, ti % 32
                            h = 4 * g + qh
                            if ti == 0:
                                t.wait_ge(tcf, 32 * (g + 1))
                                t.wait_ge(evp, 64)
                            if n >= 2:
                                t.wait_ge(exf, n - 1)
                            t.matmul(pmm[n % 2][:], kT1[:, kt, :], q1T[:, h, 512 * th:512 * (th + 1)],
                                     start=True, stop=True).then_inc(mmf, 1)
                            if ti >= 1:
                                av6(t, n - 1)
                        pending = 256 * (g + 1) - 1
                    av6(t, pending)
                    ctx6_transposes(t, NKV - 1)
                @b.scalar
                def _(s):
                    for n in range(256 * NKV):
                        if n >= 2:
                            s.wait_ge(avf, n - 1)
                        s.wait_ge(mmf, n + 1)
                        s.activation(expb6[n % 2][:], pmm[n % 2][:], AF.Exp).then_inc(exf, 1)
                        ti = n % 256
                        if ti % 32 == 31:
                            gidx = n // 32
                            g, qh, th = n // 256, ti // 64, (ti // 32) % 2
                            s.wait_ge(vrc, 4 * (gidx + 1))
                            if qh == 0 and th == 0 and g >= 2:
                                s.wait_ge(tp6, 32 * (g - 1))
                            for qs4 in range(4):
                                inst = s.activation(cxg[g % 2][:, qh * 8 + th * 4 + qs4, :],
                                                    ctp6[qs4][:, 0:128], AF.Copy,
                                                    scale=rcp6[:, qs4:qs4 + 1])
                            inst.then_inc(cte, 1)
                @b.vector
                def _(v):
                    v.wait_ge(l6d, 48)
                    for G in range(64):
                        pi, th = G // 2, G % 2
                        sl = slice(512 * th, 512 * (th + 1))
                        p = pmm[G % 2]
                        v.wait_ge(mmp, G + 1)
                        if G >= 1:
                            v.wait_ge(evp, G)
                        vb = 3 * G
                        v.tensor_tensor(rt1[:], p[:], c1s[:, sl], ALU.mult).then_inc(vvp, 1)
                        v.tensor_tensor(rt2[0:64, :], p[64:128, :], s1s[0:64, sl], ALU.mult).then_inc(vvp, 1)
                        v.tensor_tensor(rt2[64:128, :], p[0:64, :], s1s[64:128, sl], ALU.mult).then_inc(vvp, 1)
                        v.wait_ge(vvp, vb + 3)
                        v.tensor_tensor(q1T[:, pi, sl], rt1[:], rt2[:], ALU.add).then_inc(evp, 1)
                    for g in range(NKV):
                        if g >= 1:
                            ctx6_copies(v, g - 1)
                        for kt in range(32):
                            n_t = 32 * g + kt
                            v.wait_ge(tpf, n_t + 1)
                            if kt == 0 and g >= 1:
                                v.wait_ge(mmf, 256 * g)
                            v.tensor_copy(kT1[:, kt, :], ptp[:]).then_inc(tcf, 1)
                        for half in range(2):
                            v.wait_ge(vfp[half], 16 * (g + 1))
                            if g >= 1:
                                v.wait_ge(avf, 256 * g)
                            v.tensor_copy(vgB[:, half, :, 0:128],
                                          vstage[:].rearrange("p (a b) -> p a b", a=16)).then_inc(vcp, 1)
                        for gidx in range(8 * g, 8 * (g + 1)):
                            v.wait_ge(avf, 32 * (gidx + 1))
                            if gidx >= 1:
                                v.wait_ge(cte, gidx)
                            for qs4 in range(4):
                                v.reciprocal(rcp6[:, qs4:qs4 + 1], ctp6[qs4][:, 128:129]).then_inc(vrc, 1)
                    ctx6_copies(v, NKV - 1)

        if upto <= 10:
            return nc
        _phreset()

        with ExitStack() as st7:
            sb = lambda *a: st7.enter_context(nc.sbuf_tensor(*a))
            _cst[0] = st7
            ps = lambda *a: st7.enter_context(nc.psum_tensor(*a))
            xnT7 = sb("p7xn", [128, 32, T], BF16)
            cxT7 = sb("p7cx", [128, 32, T], BF16)
            wo7 = [sb(f"p7wo{i}", [128, 32, 128], BF16) for i in range(2)]
            wb7 = [sb("p7wb0", [128, 32, 128], BF16)]
            x1t7 = [sb(f"p7x1{i}", [128, 512], F32) for i in range(2)]
            bb7 = sb("p7bb", [128, 32], F32)
            wg87 = sb("p7wg8", [128, 32, 8], BF16)
            at7 = sb("p7at", [128, 32, 128], BF16)
            bt87 = sb("p7bt8", [128, D], BF16)
            e167 = sb("p7e16", [8, 128], BF16)
            eyef = sb("p7eyf", [8, 8], F32)
            lg7 = sb("p7lg", [8, T], F32)
            lgT = sb("p7lgT", [128, 8, 8], F32)
            mc = sb("p7mc", [128, 8], F32)
            msk1 = sb("p7mk1", [128, 8], F32)
            msk2 = sb("p7mk2", [128, 8], F32)
            l2c7 = sb("p7l2c", [128, 8], F32)
            r1 = sb("p7r1", [128, 8], F32)
            r2 = sb("p7r2", [128, 8], F32)
            routeb = sb("p7rb", [128, 8], BF16)
            routeT = sb("p7rT", [8, T], BF16)
            routeX = sb("p7rX", [128, T], BF16)
            hw7 = sb("p7hw", [128, T], BF16)
            g17 = [sb(f"p7g1{i}", [128, 512], F32) for i in range(2)]
            s17 = [sb(f"p7s1{i}", [128, 512], F32) for i in range(2)]
            tt27 = sb("p7tt2", [128, 512], F32)
            ob7 = [sb(f"p7ob{i}", [128, 512], F32) for i in range(2)]
            psL = ps("p7pL", [8, T], F32)
            psT = ps("p7pT", [128, 8], F32)
            psTb = ps("p7pTb", [8, 128], BF16)
            psRX = ps("p7pRX", [128, 512], F32)
            psH = ps("p7pH", [128, 512], F32)
            psR2 = ps("p7pR2", [128, 512], F32)
            psS3 = ps("p7pS3", [128, 512], F32)
            l7d = sem("p7ld")
            l7g = sem("p7lg2")
            l7wb = sem("p7lwb")
            l7wo = sem2("p7lwo")
            l7x1 = sem2("p7lx1")
            sow7 = sem2("p7sow")
            mml = sem("p7mml")
            vga = sem("p7vga")
            sga = sem("p7sga")
            tra = sem("p7tra")
            mmx = sem("p7mmx")
            mmh = sem("p7mmh")
            vhw = sem("p7vhw")
            mmr = sem("p7mmr")
            mmo = sem("p7mmo")
            sgl = sem("p7sgl")
            ss1 = sem("p7ss1")
            vvo = sem("p7vvo")
            VGA_CH = lambda tc: 9 + 9 * tc

            with nc.named_scope("P7"), nc.Block() as b:
                @b.sync
                def _(sy):
                    sy.dma_start(out=xnT7[:], in_=XN1S.rearrange("(j p) t -> p j t", p=128)).then_inc(l7d, 16)
                    sy.dma_start(out=cxT7[:], in_=CX1S.rearrange("(j p) t -> p j t", p=128)).then_inc(l7d, 16)
                    sy.dma_start(out=bb7[:], in_=BB).then_inc(l7d, 16)
                    sy.dma_start(out=eyef[:], in_=EYE[0:8, 0:8]).then_inc(l7d, 16)
                    for u in range(64):
                        ot, th = divmod(u, 2)
                        if u >= 2:
                            sy.wait_ge(vvo, 2 * (u - 2) + 2)
                        sy.dma_start(out=x1t7[u % 2][:], in_=X1S[128 * ot:128 * (ot + 1), 512 * th:512 * (th + 1)]).then_inc(l7x1[u % 2], 16)
                        if u >= 2:
                            u2 = u - 2
                            ot2, th2 = divmod(u2, 2)
                            sy.wait_ge(vvo, 2 * u2 + 2)
                            sy.dma_start(out=OUT[128 * ot2:128 * (ot2 + 1), 512 * th2:512 * (th2 + 1)],
                                         in_=ob7[u2 % 2][:]).then_inc(sow7[u2 % 2], 16)
                    for u2 in [62, 63]:
                        ot2, th2 = divmod(u2, 2)
                        sy.wait_ge(vvo, 2 * u2 + 2)
                        sy.dma_start(out=OUT[128 * ot2:128 * (ot2 + 1), 512 * th2:512 * (th2 + 1)],
                                     in_=ob7[u2 % 2][:]).then_inc(sow7[u2 % 2], 16)
                    for i in range(2):
                        sy.wait_ge(sow7[i], 16 * 32)
                @b.gpsimd
                def _(g):
                    g.dma_start(out=wg87[:], in_=WG8.rearrange("(j p) e -> p j e", p=128)).then_inc(l7g, 16)
                    g.dma_start(out=at7[:], in_=AT.rearrange("(j p) n -> p j n", p=128)).then_inc(l7g, 16)
                    g.dma_start(out=bt87[:], in_=BT8).then_inc(l7g, 16)
                    g.dma_start(out=e167[:], in_=E16).then_inc(l7g, 16)
                    for ot in range(32):
                        if ot >= 1:
                            g.wait_ge(mmr, 2 * ot)
                        g.dma_start(out=wb7[0][:], in_=WB1[ot].rearrange("(j p) n -> p j n", p=128)).then_inc(l7wb, 16)
                        if ot >= 2:
                            g.wait_ge(mmo, 2 * ot - 2)
                        g.dma_start(out=wo7[ot % 2][:], in_=WO1[ot].rearrange("(j p) n -> p j n", p=128)).then_inc(l7wo[ot % 2], 16)
                @b.tensor
                def _(t):
                    t.wait_ge(l7d, 64)
                    t.wait_ge(l7g, 16 * 4)
                    for th in range(2):
                        for j in range(32):
                            inst = t.matmul(psL[:, 512 * th:512 * (th + 1)], wg87[:, j, :],
                                            xnT7[:, j, 512 * th:512 * (th + 1)],
                                            start=(j == 0), stop=(j == 31))
                        inst.then_inc(mml, 1)
                    for tc in range(8):
                        t.wait_ge(vga, 1)
                        if tc >= 1:
                            t.wait_ge(vga, 1 + tc)
                        t.transpose(psT[:, 0:8], lg7[:, 128 * tc:128 * (tc + 1)], eyef[:]).then_inc(tra, 1)
                    for tc in range(8):
                        t.wait_ge(vga, VGA_CH(tc) + 8)
                        if tc >= 1:
                            t.wait_ge(vga, VGA_CH(tc - 1) + 9)
                        t.transpose(psTb[:], routeb[:], eyeb[:]).then_inc(tra, 1)
                    for th in range(2):
                        t.wait_ge(vga, 81 + th)
                        t.matmul(psRX[:], e167[:], routeT[:, 512 * th:512 * (th + 1)],
                                 start=True, stop=True).then_inc(mmx, 1)
                    for th in range(2):
                        t.wait_ge(vga, 83)
                        if th >= 1:
                            t.wait_ge(vhw, th)
                        for j in range(32):
                            inst = t.matmul(psH[:], at7[:, j, :], xnT7[:, j, 512 * th:512 * (th + 1)],
                                            start=(j == 0), stop=(j == 31))
                        inst.then_inc(mmh, 1)
                    t.wait_ge(vhw, 2)
                    for ot in range(32):
                        t.wait_ge(l7wb, 16 * (ot + 1))
                        for th in range(2):
                            u = 2 * ot + th
                            if u >= 1:
                                t.wait_ge(sgl, u)
                            for j in range(32):
                                t.matmul(psR2[:], wb7[0][:, j, :], xnT7[:, j, 512 * th:512 * (th + 1)],
                                         start=(j == 0), stop=False)
                            inst = t.matmul(psR2[:], bt87[:, 128 * ot:128 * (ot + 1)],
                                            hw7[:, 512 * th:512 * (th + 1)], start=False, stop=True)
                            inst.then_inc(mmr, 1)
                        t.wait_ge(l7wo[ot % 2], 16 * (ot // 2 + 1))
                        for th in range(2):
                            u = 2 * ot + th
                            if u >= 1:
                                t.wait_ge(ss1, u)
                            for j in range(32):
                                inst = t.matmul(psS3[:], wo7[ot % 2][:, j, :], cxT7[:, j, 512 * th:512 * (th + 1)],
                                                start=(j == 0), stop=(j == 31))
                            inst.then_inc(mmo, 1)
                @b.scalar
                def _(s):
                    for tc in range(8):
                        vb = VGA_CH(tc)
                        s.wait_ge(vga, vb + 6)
                        s.activation(mc[:, 3:4], mc[:, 2:3], AF.Sigmoid).then_inc(sga, 1)
                        s.wait_ge(sga, 3 * tc + 1)
                        s.activation(r1[:], msk1[:], AF.Copy, scale=mc[:, 3:4]).then_inc(sga, 1)
                        s.wait_ge(vga, vb + 7)
                        s.activation(r2[:], msk2[:], AF.Copy, scale=mc[:, 4:5]).then_inc(sga, 1)
                    for ot in range(32):
                        for th in range(2):
                            u = 2 * ot + th
                            s.wait_ge(mmr, u + 1)
                            if u >= 2:
                                s.wait_ge(vvo, 2 * (u - 2) + 1)
                            s.activation(g17[th][:], psR2[:], AF.Silu, bias=bb7[:, ot:ot + 1]).then_inc(sgl, 1)
                            s.wait_ge(mmo, u + 1)
                            if u >= 2:
                                s.wait_ge(vvo, 2 * (u - 2) + 1)
                            s.activation(s17[th][:], psS3[:], AF.Copy).then_inc(ss1, 1)
                @b.vector
                def _(v):
                    v.wait_ge(mml, 2)
                    v.tensor_copy(lg7[:], psL[:]).then_inc(vga, 1)
                    for tc in range(8):
                        v.wait_ge(tra, tc + 1)
                        v.tensor_copy(lgT[:, tc, :], psT[:, 0:8]).then_inc(vga, 1)
                    for tc in range(8):
                        vb = VGA_CH(tc)
                        x = lgT[:, tc, :]
                        if tc >= 1:
                            v.wait_ge(sga, 3 * tc)
                        v.reduce_max(mc[:, 0:1], x, axis=AX.X).then_inc(vga, 1)
                        v.wait_ge(vga, vb + 1)
                        v.tensor_scalar(msk1[:], x, mc[:, 0:1], None, ALU.is_ge).then_inc(vga, 1)
                        v.wait_ge(vga, vb + 2)
                        v.scalar_tensor_tensor(l2c7[:], msk1[:], NEG, x, ALU.mult, ALU.add).then_inc(vga, 1)
                        v.wait_ge(vga, vb + 3)
                        v.reduce_max(mc[:, 1:2], l2c7[:], axis=AX.X).then_inc(vga, 1)
                        v.wait_ge(vga, vb + 4)
                        v.tensor_scalar(msk2[:], l2c7[:], mc[:, 1:2], None, ALU.is_ge).then_inc(vga, 1)
                        v.tensor_tensor(mc[:, 2:3], mc[:, 0:1], mc[:, 1:2], ALU.subtract).then_inc(vga, 1)
                        v.wait_ge(sga, 3 * tc + 1)
                        v.tensor_scalar(mc[:, 4:5], mc[:, 3:4], -1.0, 1.0, ALU.mult, ALU.add).then_inc(vga, 1)
                        v.wait_ge(sga, 3 * tc + 3)
                        if tc >= 1:
                            v.wait_ge(tra, 8 + tc)
                        v.tensor_tensor(routeb[:], r1[:], r2[:], ALU.add).then_inc(vga, 1)
                        v.wait_ge(tra, 8 + tc + 1)
                        v.tensor_copy(routeT[:, 128 * tc:128 * (tc + 1)], psTb[:]).then_inc(vga, 1)
                    for th in range(2):
                        v.wait_ge(mmx, th + 1)
                        v.tensor_copy(routeX[:, 512 * th:512 * (th + 1)], psRX[:]).then_inc(vga, 1)
                    for th in range(2):
                        v.wait_ge(mmh, th + 1)
                        v.tensor_tensor(hw7[:, 512 * th:512 * (th + 1)], psH[:],
                                        routeX[:, 512 * th:512 * (th + 1)], ALU.mult).then_inc(vhw, 1)
                    for u in range(64):
                        ot, th = divmod(u, 2)
                        v.wait_ge(sgl, u + 1)
                        v.wait_ge(ss1, u + 1)
                        v.wait_ge(l7x1[u % 2], 16 * (u // 2 + 1))
                        if u >= 1:
                            v.wait_ge(vvo, 2 * u)
                        v.tensor_tensor(tt27[:], g17[th][:], s17[th][:], ALU.mult).then_inc(vvo, 1)
                        v.wait_ge(vvo, 2 * u + 1)
                        if u >= 2:
                            v.wait_ge(sow7[u % 2], 16 * (u // 2))
                        v.tensor_tensor(ob7[u % 2][:], tt27[:], x1t7[u % 2][:],
                                        ALU.add).then_inc(vvo, 1)

    return nc


def _mega_av(t, n, expb, vg, ctp, p2ex, p2ce, p2av):
    ag, kt = n // 32, n % 32
    g = ag // 8
    t.wait_ge(p2ex, n + 1)
    if kt == 0 and ag >= 1:
        t.wait_ge(p2ce, 4 * ag)
    for qs4 in range(4):
        inst = t.matmul(ctp[qs4][:], expb[n % 2][:, 128 * qs4:128 * (qs4 + 1)],
                        vg[g % 2][:, kt, :], start=(kt == 0), stop=(kt == 31))
    inst.then_inc(p2av, 1)


def plane_perm(ncols):
    pp = np.zeros(ncols, dtype=np.int64)
    for nb in range(ncols // 128):
        base = 128 * nb
        for pl in range(8):
            for ci in range(16):
                pp[base + 16 * pl + ci] = base + 8 * ci + pl
    return pp


def mega_inputs(L0_attn_x, L0_attn_wq, L0_attn_wk, L0_attn_wv, L0_attn_wo,
                L0_ffn_gate_qweight, L0_ffn_gate_scales, L0_ffn_gate_qzeros,
                L0_ffn_up_qweight, L0_ffn_up_scales, L0_ffn_up_qzeros,
                L0_ffn_down_qweight, L0_ffn_down_scales, L0_ffn_down_qzeros,
                L1_attn_k_cache, L1_attn_v_cache, L1_attn_Wq, L1_attn_Wo,
                L1_attn_block_table, L1_ffn_W_base, L1_ffn_b_base, L1_ffn_W_gate,
                L1_ffn_A, L1_ffn_B):
    import ml_dtypes
    f32 = np.float32
    x2 = np.asarray(L0_attn_x, dtype=f32).reshape(8192, D)
    pp11 = plane_perm(11008)
    pp4 = plane_perm(4096)
    eye = np.eye(128, dtype=f32)
    one_r = np.ones((1, 128), f32)
    one_c = np.ones((128, 1), f32)
    csel = np.zeros((128, 256), ml_dtypes.bfloat16)
    csel[:, 127] = 1
    e16 = np.zeros((8, 128), f32)
    for e in range(8):
        e16[e, 16 * e:16 * (e + 1)] = 1.0
    perm_d = np.concatenate([np.arange(0, HD, 2), np.arange(1, HD, 2)])
    colp_q = np.concatenate([HD * h + perm_d for h in range(NH)])
    colp_k = np.concatenate([HD * g + perm_d for g in range(NKV)])
    wq0 = np.asarray(L0_attn_wq, f32)
    wk0 = np.asarray(L0_attn_wk, f32)
    wv0 = cc(np.asarray(L0_attn_wv, f32))
    wq_t = cc(wq0[:, colp_q].reshape(D, NH, HD).transpose(1, 0, 2))
    wk_t = cc(wk0[:, colp_k].reshape(D, NKV, HD).transpose(1, 0, 2))
    wop = cc(np.asarray(L0_attn_wo, f32)[:, pp4].reshape(D, 32, 128).transpose(1, 0, 2))
    klc = np.arange(128)[:, None]
    qlc = np.arange(512)[None, :]
    m4 = np.zeros((128, 4, 512), f32)
    for r in range(4):
        m4[:, r, :] = np.where(128 * r + klc <= qlc, 0.0, NEG)
    m4 = cc(m4)
    gq = cc(np.asarray(L0_ffn_gate_qweight, np.int32))
    gs = cc(np.asarray(L0_ffn_gate_scales, f32)[:, pp11])
    gz = cc(np.asarray(L0_ffn_gate_qzeros, np.int32))
    uq = cc(np.asarray(L0_ffn_up_qweight, np.int32))
    us = cc(np.asarray(L0_ffn_up_scales, f32)[:, pp11])
    uz = cc(np.asarray(L0_ffn_up_qzeros, np.int32))
    dq = cc(np.asarray(L0_ffn_down_qweight, np.int32)[pp11, :])
    ds = cc(np.asarray(L0_ffn_down_scales, f32)[:, pp4])
    dz = cc(np.asarray(L0_ffn_down_qzeros, np.int32))
    wq1 = cc(np.asarray(L1_attn_Wq, f32).T[pp4].reshape(D, 32, 128).transpose(1, 0, 2))
    wo1 = cc(np.asarray(L1_attn_Wo, f32).T[:, pp4].reshape(D, 32, 128).transpose(1, 0, 2))
    kc = cc(np.asarray(L1_attn_k_cache, f32))
    vc = cc(np.asarray(L1_attn_v_cache, f32))
    bti = cc(np.asarray(L1_attn_block_table, np.int32).reshape(2, 128).T.copy())
    wb1 = cc(np.asarray(L1_ffn_W_base, f32).T[pp4][:, pp4].reshape(D, 32, 128).transpose(1, 0, 2))
    bb = cc(np.asarray(L1_ffn_b_base, f32)[pp4].reshape(32, 128).T)
    wg8 = cc(np.asarray(L1_ffn_W_gate, f32).T[pp4])
    at = cc(np.asarray(L1_ffn_A, f32).transpose(2, 0, 1).reshape(D, 128)[pp4])
    bt8 = cc(np.asarray(L1_ffn_B, f32).transpose(0, 2, 1).reshape(128, D)[:, pp4])
    inv = 10000.0 ** (-np.arange(64) / 64.0)
    sc = f32(1.0 / np.sqrt(HD))

    def tables(pos):
        ang = pos.astype(np.float64)[:, None] * inv[None, :]
        Ct = cc(np.concatenate([np.cos(ang), np.cos(ang)], 1).T.astype(f32))
        St = cc(np.concatenate([-np.sin(ang), np.sin(ang)], 1).T.astype(f32))
        return Ct, St

    maps = []
    for c in range(NCORES):
        b, rho = c // 4, c % 4
        xro = cc(np.roll(x2[S * b:S * (b + 1)], -T * rho, axis=0))
        xTp = cc(x2[T * c:T * (c + 1)].T[pp4])
        posq = np.arange(T * rho, T * rho + T)
        cq1, sq1 = tables(posq)
        posk = (np.arange(S) + T * rho) % S
        ckf, skf = tables(posk)
        c1t, s1t = tables(posq)
        bia = np.zeros((128, 64), f32)
        for qc in range(2):
            for kt in range(32):
                keep = (kt < 4 * qc) or (kt >= 32 - 8 * rho)
                win = (4 * qc <= kt < 4 * qc + 4)
                bia[:, 32 * qc + kt] = 0.0 if (keep or win) else NEG
        maps.append(dict(
            xro=xro, xTp=xTp, wq_t=wq_t, wk_t=wk_t, wv=wv0,
            cq1=cq1 * sc, sq1=sq1 * sc, ckf=ckf, skf=skf,
            bia=cc(bia), m4=m4, eye=eye, one=one_r, onec=one_c,
            wop=wop, gq=gq, gs=gs, gz=gz, uq=uq, us=us, uz=uz,
            dq=dq, ds=ds, dz=dz,
            wq1=wq1, c1t=c1t * sc, s1t=s1t * sc, kc=kc, vc=vc, bti=bti,
            wo1=wo1, wb1=wb1, bb=bb, wg8=wg8, at=at, bt8=bt8, e16=e16,
            csel=csel,
        ))
    return maps


def kernel_mega(**inputs):
    maps = mega_inputs(**inputs)
    res = run_bass_kernel_spmd(build_mega(), maps, core_ids=list(range(NCORES))).results
    pp4 = plane_perm(4096)
    out = np.empty((8192, D), dtype=np.float32)
    for c in range(NCORES):
        out[T * c:T * (c + 1)][:, pp4] = np.asarray(res[c]["outT"]).T
    return out.reshape(2, 4096, 4096)


def kernel(**inputs):
    return kernel_mega(**inputs)


# version 07
# speedup vs baseline: 1.1342x; 1.1342x over previous
import numpy as np
import ml_dtypes
from contextlib import ExitStack

import concourse.bass as bass
import concourse.mybir as mybir
from concourse.bass import IndirectOffsetOnAxis
from concourse.bass_utils import run_bass_kernel_spmd

F32 = mybir.dt.float32
BF16 = mybir.dt.bfloat16
I32 = mybir.dt.int32
AF = mybir.ActivationFunctionType
ALU = mybir.AluOpType
AX = mybir.AxisListType
BF = ml_dtypes.bfloat16

NCORES = 8
T = 1024
D = 4096
HD = 128
NH = 32
NKV = 8
KVD = NKV * HD
S = 4096
NEG = -1.0e30
EPS = 1e-5

cc = np.ascontiguousarray


def build_prep0(stage=3, nqk=40):
    nc = bass.Bass("TRN2", target_bir_lowering=False)
    X = nc.dram_tensor("x", [T, D], F32, kind="ExternalInput").ap()
    WQ = nc.dram_tensor("wq_t", [NH, D, HD], F32, kind="ExternalInput").ap()
    WK = nc.dram_tensor("wk_t", [NKV, D, HD], F32, kind="ExternalInput").ap()
    WV = nc.dram_tensor("wv", [D, KVD], F32, kind="ExternalInput").ap()
    CQ = nc.dram_tensor("cq", [HD, T], F32, kind="ExternalInput").ap()
    SQ = nc.dram_tensor("sq", [HD, T], F32, kind="ExternalInput").ap()
    CK = nc.dram_tensor("ck", [HD, T], F32, kind="ExternalInput").ap()
    SK = nc.dram_tensor("sk", [HD, T], F32, kind="ExternalInput").ap()
    EYE = nc.dram_tensor("eye", [128, 128], F32, kind="ExternalInput").ap()
    XNT = nc.dram_tensor("xnT", [D, T], BF16, kind="ExternalOutput").ap()
    QT = nc.dram_tensor("qT", [D, T], BF16, kind="ExternalOutput").ap()
    KT = nc.dram_tensor("kT", [KVD, T], BF16, kind="ExternalOutput").ap()
    VO = nc.dram_tensor("v", [T, KVD], BF16, kind="ExternalOutput").ap()

    qk_proj = ([("q", h) for h in range(NH)] + [("k", g) for g in range(NKV)])[:nqk]
    NQK = len(qk_proj)
    NVG = 32

    with ExitStack() as st:
        sb = lambda *a: st.enter_context(nc.sbuf_tensor(*a))
        ps = lambda *a: st.enter_context(nc.psum_tensor(*a))
        xt = [sb(f"xt{i}", [128, D], F32) for i in range(2)]
        junk = sb("junk", [128, D], BF16)
        stat = sb("stat", [128, 8], F32)
        xn = sb("xn", [128, D], BF16)
        xnT = sb("xnTs", [128, 32, T], BF16)
        eye = sb("eyeb", [128, 128], BF16)
        cqs = sb("cqs", [128, T], F32)
        sqs = sb("sqs", [128, T], F32)
        cks = sb("cks", [128, T], F32)
        sks = sb("sks", [128, T], F32)
        wbuf = [sb(f"wbuf{i}", [128, 32, 128], BF16) for i in range(2)]
        wvb = [sb(f"wvb{i}", [128, 32, 256], BF16) for i in range(2)]
        rt1 = sb("rt1", [128, 512], F32)
        rt2 = sb("rt2", [128, 512], F32)
        ro = [sb(f"ro{i}", [128, 512], BF16) for i in range(2)]
        vout = [sb(f"vout{i}", [128, 256], BF16) for i in range(2)]
        ptp = [ps(f"ptp{i}", [128, 128], BF16) for i in range(2)]
        pmm = [ps(f"pmm{i}", [128, 512], F32) for i in range(2)]
        lt = st.enter_context(nc.semaphore("lt"))
        lx = [st.enter_context(nc.semaphore(f"lx{i}")) for i in range(2)]
        ley = st.enter_context(nc.semaphore("ley"))
        lwq = [st.enter_context(nc.semaphore(f"lwq{i}")) for i in range(2)]
        lwv = [st.enter_context(nc.semaphore(f"lwv{i}")) for i in range(2)]
        sxn = st.enter_context(nc.semaphore("sxn"))
        sqk = [st.enter_context(nc.semaphore(f"sqk{i}")) for i in range(2)]
        svv = [st.enter_context(nc.semaphore(f"svv{i}")) for i in range(2)]
        s_ln = st.enter_context(nc.semaphore("s_ln"))
        v_ln = st.enter_context(nc.semaphore("v_ln"))
        tp = st.enter_context(nc.semaphore("tp"))
        tc = st.enter_context(nc.semaphore("tc"))
        mm = st.enter_context(nc.semaphore("mm"))
        ev = st.enter_context(nc.semaphore("ev"))
        so = st.enter_context(nc.semaphore("so"))
        block = st.enter_context(nc.Block())

        @block.sync
        def _(sy):
            sy.dma_start(out=cqs[:], in_=CQ).then_inc(lt, 16)
            sy.dma_start(out=sqs[:], in_=SQ).then_inc(lt, 16)
            sy.dma_start(out=cks[:], in_=CK).then_inc(lt, 16)
            sy.dma_start(out=sks[:], in_=SK).then_inc(lt, 16)
            for i in range(8):
                if i >= 2:
                    sy.wait_ge(s_ln, 4 * i - 4)
                    sy.wait_ge(v_ln, 7 * i - 13)
                sy.dma_start(out=xt[i % 2][:], in_=X[128 * i:128 * (i + 1), :]).then_inc(lx[i % 2], 16)
            sy.wait_ge(tc, 256)
            for j in range(32):
                sy.dma_start(out=XNT[128 * j:128 * (j + 1), :], in_=xnT[:, j, :]).then_inc(sxn, 16)
            if stage < 2:
                sy.wait_ge(so, 16 * 32)
                return
            for G in range(2 * NQK):
                pi, th = G // 2, G % 2
                kind, idx = qk_proj[pi]
                sy.wait_ge(ev, G + 1)
                dst = QT if kind == "q" else KT
                sy.dma_start(out=dst[128 * idx:128 * (idx + 1), 512 * th:512 * (th + 1)],
                             in_=ro[G % 2][:]).then_inc(sqk[G % 2], 16)
            if stage < 3:
                sy.wait_ge(so, 16 * (32 + 2 * NQK))
                return
            for h in range(NVG):
                hf, tt = h // 8, h % 8
                sy.wait_ge(ev, 2 * NQK + h + 1)
                sy.dma_start(out=VO[128 * tt:128 * (tt + 1), 256 * hf:256 * (hf + 1)],
                             in_=vout[h % 2][:]).then_inc(svv[h % 2], 16)
            sy.wait_ge(sxn, 16 * 32)
            sy.wait_ge(sqk[0], 16 * NQK)
            sy.wait_ge(sqk[1], 16 * NQK)
            sy.wait_ge(svv[0], 16 * (NVG // 2))
            sy.wait_ge(svv[1], 16 * (NVG // 2))

        @block.gpsimd
        def _(g):
            g.dma_start(out=eye[:], in_=EYE).then_inc(ley, 16)
            if stage < 2:
                return
            for pi, (kind, idx) in enumerate(qk_proj):
                if pi >= 2:
                    g.wait_ge(mm, 2 * pi - 2)
                src = WQ[idx] if kind == "q" else WK[idx]
                g.dma_start(out=wbuf[pi % 2][:],
                            in_=src.rearrange("(j p) n -> p j n", p=128)).then_inc(lwq[pi % 2], 16)
            if stage < 3:
                return
            for hf in range(4):
                if hf >= 2:
                    g.wait_ge(mm, 2 * NQK + 8 * (hf - 1))
                g.dma_start(out=wvb[hf % 2][:],
                            in_=WV[:, 256 * hf:256 * (hf + 1)].rearrange("(j p) n -> p j n", p=128)
                            ).then_inc(lwv[hf % 2], 16)

        @block.scalar
        def _(s):
            for i in range(8):
                s.wait_ge(lx[i % 2], 16 * (i // 2 + 1))
                s.activation(junk[:], xt[i % 2][:], AF.Square,
                             accum_out=stat[:, 1:2]).then_inc(s_ln, 1)
                s.wait_ge(v_ln, 7 * i + 5)
                s.activation(stat[:, 3:4], stat[:, 4:5], AF.Ln).then_inc(s_ln, 1)
                s.wait_ge(v_ln, 7 * i + 6)
                s.activation(stat[:, 5:6], stat[:, 4:5], AF.Exp).then_inc(s_ln, 1)
                s.wait_ge(v_ln, 7 * i + 7)
                s.wait_ge(s_ln, 4 * i + 3)
                if i > 0:
                    s.wait_ge(tp, 32 * i)
                s.activation(xn[:], xt[i % 2][:], AF.Identity, scale=stat[:, 5:6],
                             bias=stat[:, 6:7]).then_inc(s_ln, 1)
            if stage < 3:
                return
            for h in range(NVG):
                s.wait_ge(mm, 2 * NQK + h + 1)
                if h >= 2:
                    s.wait_ge(svv[h % 2], 16 * (h // 2))
                s.activation(vout[h % 2][:], pmm[(2 * NQK + h) % 2][:, 0:256], AF.Copy).then_inc(ev, 1)

        @block.vector
        def _(v):
            for i in range(8):
                v.wait_ge(s_ln, 4 * i + 1)
                v.wait_ge(lx[i % 2], 16 * (i // 2 + 1))
                v.reduce_sum(stat[:, 0:1], xt[i % 2][:], axis=AX.X).then_inc(v_ln, 1)
                v.wait_ge(v_ln, 7 * i + 1)
                v.tensor_scalar(stat[:, 2:3], stat[:, 0:1], 1.0 / D, None, ALU.mult).then_inc(v_ln, 1)
                v.wait_ge(v_ln, 7 * i + 2)
                v.tensor_tensor(stat[:, 7:8], stat[:, 2:3], stat[:, 2:3], ALU.mult).then_inc(v_ln, 1)
                v.wait_ge(v_ln, 7 * i + 3)
                v.scalar_tensor_tensor(stat[:, 3:4], stat[:, 1:2], 1.0 / D, stat[:, 7:8],
                                       ALU.mult, ALU.subtract).then_inc(v_ln, 1)
                v.wait_ge(v_ln, 7 * i + 4)
                v.tensor_scalar(stat[:, 4:5], stat[:, 3:4], EPS, None,
                                ALU.add).then_inc(v_ln, 1)
                v.wait_ge(s_ln, 4 * i + 2)
                v.tensor_scalar(stat[:, 4:5], stat[:, 3:4], -0.5, None,
                                ALU.mult).then_inc(v_ln, 1)
                v.wait_ge(s_ln, 4 * i + 3)
                v.scalar_tensor_tensor(stat[:, 6:7], stat[:, 2:3], -1.0, stat[:, 5:6],
                                       ALU.mult, ALU.mult).then_inc(v_ln, 1)
                for j in range(32):
                    n = 32 * i + j
                    v.wait_ge(tp, n + 1)
                    v.tensor_copy(xnT[:, j, 128 * i:128 * (i + 1)], ptp[n % 2][:]).then_inc(tc, 1)
            if stage < 2:
                return
            v.wait_ge(lt, 64)
            for G in range(2 * NQK):
                pi, th = G // 2, G % 2
                kind, _ = qk_proj[pi]
                Ct = cqs if kind == "q" else cks
                St = sqs if kind == "q" else sks
                sl = slice(512 * th, 512 * (th + 1))
                p = pmm[G % 2]
                v.wait_ge(mm, G + 1)
                if G >= 2:
                    v.wait_ge(sqk[G % 2], 16 * (G // 2))
                if G >= 1:
                    v.wait_ge(ev, G)
                vb = 56 + 3 * G
                v.tensor_tensor(rt1[:], p[:], Ct[:, sl], ALU.mult).then_inc(v_ln, 1)
                v.tensor_tensor(rt2[0:64, :], p[64:128, :], St[0:64, sl], ALU.mult).then_inc(v_ln, 1)
                v.tensor_tensor(rt2[64:128, :], p[0:64, :], St[64:128, sl], ALU.mult).then_inc(v_ln, 1)
                v.wait_ge(v_ln, vb + 3)
                v.tensor_tensor(ro[G % 2][:], rt1[:], rt2[:], ALU.add).then_inc(ev, 1)

        @block.tensor
        def _(t):
            t.wait_ge(ley, 16)
            for i in range(8):
                t.wait_ge(s_ln, 4 * i + 4)
                for j in range(32):
                    n = 32 * i + j
                    if n >= 2:
                        t.wait_ge(tc, n - 1)
                    t.transpose(ptp[n % 2][:], xn[:, 128 * j:128 * (j + 1)], eye[:]).then_inc(tp, 1)
            t.wait_ge(tc, 256)
            if stage < 2:
                return
            for G in range(2 * NQK):
                pi, th = G // 2, G % 2
                if th == 0:
                    t.wait_ge(lwq[pi % 2], 16 * (pi // 2 + 1))
                if G >= 2:
                    t.wait_ge(ev, G - 1)
                for j in range(32):
                    inst = t.matmul(pmm[G % 2][:], wbuf[pi % 2][:, j, :],
                                    xnT[:, j, 512 * th:512 * (th + 1)],
                                    start=(j == 0), stop=(j == 31))
                inst.then_inc(mm, 1)
            if stage < 3:
                return
            for h in range(NVG):
                hf, tt = h // 8, h % 8
                G = 2 * NQK + h
                if tt == 0:
                    t.wait_ge(lwv[hf % 2], 16 * (hf // 2 + 1))
                t.wait_ge(ev, G - 1)
                for j in range(32):
                    inst = t.matmul(pmm[G % 2][:, 0:256], xnT[:, j, 128 * tt:128 * (tt + 1)],
                                    wvb[hf % 2][:, j, :], start=(j == 0), stop=(j == 31))
                inst.then_inc(mm, 1)
    return nc


def host_prep0_inputs(x_full2d, wq, wk, wv):
    perm_d = np.concatenate([np.arange(0, HD, 2), np.arange(1, HD, 2)])
    colp_q = np.concatenate([HD * h + perm_d for h in range(NH)])
    colp_k = np.concatenate([HD * g + perm_d for g in range(NKV)])
    wq_t = cc(wq[:, colp_q].reshape(D, NH, HD).transpose(1, 0, 2))
    wk_t = cc(wk[:, colp_k].reshape(D, NKV, HD).transpose(1, 0, 2))
    wv_c = cc(wv)
    eye = np.eye(128, dtype=np.float32)
    inv = 10000.0 ** (-np.arange(64) / 64.0)
    maps = []
    for c in range(NCORES):
        pos = np.arange(1024 * (c % 4), 1024 * (c % 4) + 1024, dtype=np.float64)
        ang = pos[:, None] * inv[None, :]
        cos = np.cos(ang).astype(np.float32)
        sin = np.sin(ang).astype(np.float32)
        Ct = cc(np.concatenate([cos, cos], 1).T)
        St = cc(np.concatenate([-sin, sin], 1).T)
        sc = np.float32(1.0 / np.sqrt(HD))
        maps.append(dict(
            x=cc(x_full2d[T * c:T * (c + 1)]),
            wq_t=wq_t, wk_t=wk_t, wv=wv_c,
            cq=Ct * sc, sq=St * sc, ck=Ct, sk=St, eye=eye,
        ))
    return maps




def build_attn0():
    nc = bass.Bass("TRN2", target_bir_lowering=False)
    QTF = nc.dram_tensor("qTf", [128, 8, S], BF16, kind="ExternalInput").ap()
    KTF = nc.dram_tensor("kTf", [128, 2, S], BF16, kind="ExternalInput").ap()
    VA = nc.dram_tensor("vA", [128, 32, 2, 129], BF16, kind="ExternalInput").ap()
    MSK = nc.dram_tensor("msk", [128, 4, 512], F32, kind="ExternalInput").ap()
    CTX = nc.dram_tensor("ctx", [128, 32, 8, 128], BF16, kind="ExternalOutput").ap()

    tiles = []
    for h in range(8):
        for qc in range(8):
            nkt = 4 * qc + 4
            for kt in range(nkt):
                tiles.append((h, qc, kt, nkt, h * 8 + qc))
    NT = len(tiles)
    cumtiles = {}
    n = 0
    for h in range(8):
        for qc in range(8):
            n += 4 * qc + 4
            cumtiles[h * 8 + qc] = n
    masked_idx = {}
    mi = 0
    for n_, (h, qc, kt, nkt, g) in enumerate(tiles):
        if kt >= nkt - 4:
            masked_idx[n_] = mi
            mi += 1
    NMSK = mi

    with ExitStack() as st:
        sb = lambda *a: st.enter_context(nc.sbuf_tensor(*a))
        ps = lambda *a: st.enter_context(nc.psum_tensor(*a))
        qT = sb("qTs", [128, 8, S], BF16)
        kT = sb("kTs", [128, 2, S], BF16)
        vA = sb("vAs", [128, 32, 2, 129], BF16)
        msk = sb("msks", [128, 4, 512], F32)
        expb = [sb(f"expb{i}", [128, 512], BF16) for i in range(2)]
        sxp = [sb(f"sxp{i}", [128, 512], F32) for i in range(2)]
        rcp = sb("rcp", [128, 4], F32)
        ctx = sb("ctxs", [128, 32, 8, 128], BF16)
        scp = [ps(f"scp{i}", [128, 512], F32) for i in range(2)]
        ctp = [ps(f"ctp{i}", [128, 129], F32) for i in range(4)]
        lds = st.enter_context(nc.semaphore("lds"))
        mm = st.enter_context(nc.semaphore("mm"))
        avm = st.enter_context(nc.semaphore("avm"))
        exv = st.enter_context(nc.semaphore("exv"))
        vx = st.enter_context(nc.semaphore("vx"))
        vrc = st.enter_context(nc.semaphore("vrc"))
        ctev = st.enter_context(nc.semaphore("ctev"))
        so = st.enter_context(nc.semaphore("so"))
        block = st.enter_context(nc.Block())

        @block.sync
        def _(sy):
            sy.dma_start(out=qT[:], in_=QTF).then_inc(lds, 16)
            sy.dma_start(out=kT[:], in_=KTF).then_inc(lds, 16)
            sy.dma_start(out=vA[:], in_=VA).then_inc(lds, 16)
            sy.dma_start(out=msk[:], in_=MSK).then_inc(lds, 16)
            sy.wait_ge(ctev, 64)
            sy.dma_start(out=CTX, in_=ctx[:]).then_inc(so, 16)
            sy.wait_ge(so, 16)

        @block.tensor
        def _(t):
            t.wait_ge(lds, 64)
            for n_, (h, qc, kt, nkt, g) in enumerate(tiles):
                if n_ >= 2:
                    t.wait_ge(exv, n_ - 1)
                t.matmul(scp[n_ % 2][:], kT[:, h // 4, 128 * kt:128 * (kt + 1)],
                         qT[:, h, 512 * qc:512 * (qc + 1)],
                         start=True, stop=True).then_inc(mm, 1)
                if n_ >= 1:
                    _emit_av(t, tiles[n_ - 1], n_ - 1, expb, vA, ctp, exv, ctev, avm)
            _emit_av(t, tiles[NT - 1], NT - 1, expb, vA, ctp, exv, ctev, avm)

        @block.vector
        def _(v):
            nv = 0
            for n_, (h, qc, kt, nkt, g) in enumerate(tiles):
                if kt >= nkt - 4:
                    r = kt - 4 * qc
                    v.wait_ge(mm, n_ + 1)
                    if n_ >= 2:
                        v.wait_ge(exv, n_ - 1)
                    v.tensor_tensor(sxp[n_ % 2][:], scp[n_ % 2][:], msk[:, r, :],
                                    ALU.add).then_inc(vx, 1)
                if kt == nkt - 1:
                    v.wait_ge(avm, cumtiles[g])
                    if g >= 1:
                        v.wait_ge(ctev, g)
                    for qs4 in range(4):
                        v.reciprocal(rcp[:, qs4:qs4 + 1],
                                     ctp[qs4][:, 128:129]).then_inc(vrc, 1)

        @block.scalar
        def _(s):
            for n_, (h, qc, kt, nkt, g) in enumerate(tiles):
                if n_ >= 2:
                    s.wait_ge(avm, n_ - 1)
                if kt >= nkt - 4:
                    s.wait_ge(vx, masked_idx[n_] + 1)
                    s.activation(expb[n_ % 2][:], sxp[n_ % 2][:], AF.Exp).then_inc(exv, 1)
                else:
                    s.wait_ge(mm, n_ + 1)
                    s.activation(expb[n_ % 2][:], scp[n_ % 2][:], AF.Exp).then_inc(exv, 1)
                if kt == nkt - 1:
                    s.wait_ge(vrc, 4 * (g + 1))
                    qt_global = 4 * qc
                    for qs4 in range(4):
                        inst = s.activation(ctx[:, qt_global + qs4, h, :],
                                            ctp[qs4][:, 0:128], AF.Copy,
                                            scale=rcp[:, qs4:qs4 + 1])
                    inst.then_inc(ctev, 1)

    return nc


def _emit_av(t, tile, m, expb, vA, ctp, exv, ctev, avm):
    h, qc, kt, nkt, g = tile
    t.wait_ge(exv, m + 1)
    if kt == 0 and g >= 1:
        t.wait_ge(ctev, g)
    for qs4 in range(4):
        inst = t.matmul(ctp[qs4][:], expb[m % 2][:, 128 * qs4:128 * (qs4 + 1)],
                        vA[:, kt, h // 4, :], start=(kt == 0), stop=(kt == nkt - 1))
    inst.then_inc(avm, 1)


def host_attn0_inputs(qT_cores, kT_cores, v_cores):
    maps = []
    qT_b = [np.concatenate(qT_cores[4 * b:4 * b + 4], axis=1) for b in range(2)]
    kT_b = [np.concatenate(kT_cores[4 * b:4 * b + 4], axis=1) for b in range(2)]
    v_b = [np.concatenate(v_cores[4 * b:4 * b + 4], axis=0) for b in range(2)]
    kl = np.arange(128)[:, None]
    ql = np.arange(512)[None, :]
    m4 = np.stack([np.where(128 * r + kl > ql, NEG, 0.0) for r in range(4)], 0).astype(np.float32)
    msk = cc(m4.transpose(1, 0, 2))
    for i in range(NCORES):
        b, m = i // 4, i % 4
        qTf = cc(qT_b[b][128 * 8 * m:128 * 8 * (m + 1)].reshape(8, 128, S).transpose(1, 0, 2))
        kTf = cc(kT_b[b][256 * m:256 * (m + 1)].reshape(2, 128, S).transpose(1, 0, 2))
        vA = np.ones((128, 32, 2, 129), dtype=BF)
        for gl in range(2):
            vg = v_b[b][:, 128 * (2 * m + gl):128 * (2 * m + gl + 1)]
            vA[:, :, gl, 0:128] = vg.reshape(32, 128, 128).transpose(1, 0, 2)
        maps.append(dict(qTf=qTf, kTf=kTf, vA=cc(vA), msk=msk))
    return maps


def host_attn0_ctx(results):
    ctx_b = [np.zeros((4096, 4096), dtype=BF) for _ in range(2)]
    for i in range(NCORES):
        b, m = i // 4, i % 4
        arr = np.asarray(results[i]["ctx"])
        ctx_b[b][:, 1024 * m:1024 * (m + 1)] = arr.transpose(1, 0, 2, 3).reshape(4096, 1024)
    return ctx_b


ARO = [0, 4, 1, 5, 2, 6, 3, 7]


def lperm_global(ncol_percore):
    return np.array([8 * (ncol_percore * c + col) + p
                     for c in range(8) for p in range(8) for col in range(ncol_percore)])


def build_dequant():
    nc = bass.Bass("TRN2", target_bir_lowering=False)
    GQ = nc.dram_tensor("gq", [4096, 172], I32, kind="ExternalInput").ap()
    GS = nc.dram_tensor("gs", [32, 1376], F32, kind="ExternalInput").ap()
    GZ = nc.dram_tensor("gz", [32, 172], I32, kind="ExternalInput").ap()
    UQ = nc.dram_tensor("uq", [4096, 172], I32, kind="ExternalInput").ap()
    US = nc.dram_tensor("us", [32, 1376], F32, kind="ExternalInput").ap()
    UZ = nc.dram_tensor("uz", [32, 172], I32, kind="ExternalInput").ap()
    DQ = nc.dram_tensor("dq", [11008, 64], I32, kind="ExternalInput").ap()
    DSE = nc.dram_tensor("dse", [11008, 512], F32, kind="ExternalInput").ap()
    DZE = nc.dram_tensor("dze", [11008, 64], I32, kind="ExternalInput").ap()
    ONE = nc.dram_tensor("one", [1, 128], F32, kind="ExternalInput").ap()
    WG = nc.dram_tensor("Wg", [4096, 1376], BF16, kind="ExternalOutput").ap()
    WU = nc.dram_tensor("Wu", [4096, 1376], BF16, kind="ExternalOutput").ap()
    WD = nc.dram_tensor("Wd", [11008, 512], BF16, kind="ExternalOutput").ap()

    mats = [("g", GQ, GS, GZ, WG), ("u", UQ, US, UZ, WU)]
    units = [(mi, g) for mi in range(2) for g in range(32)]
    NU = len(units)
    ND = 86
    C, N = 172, 1376

    with ExitStack() as st:
        sb = lambda *a: st.enter_context(nc.sbuf_tensor(*a))
        ps = lambda *a: st.enter_context(nc.psum_tensor(*a))
        qw_sb = [sb("gqs", [128, 32, 172], I32), sb("uqs", [128, 32, 172], I32),
                 sb("dqs", [128, 86, 64], I32)]
        ones = sb("ones", [1, 128], F32)
        zqr = [sb(f"zqr{i}", [1, 172], I32) for i in range(2)]
        srow = [sb(f"srow{i}", [1, 1376], F32) for i in range(2)]
        znib = sb("znib", [1, 1376], I32)
        zb = sb("zb", [1, 1376], F32)
        nibi = sb("nibi", [128, 1376], I32)
        nibb = sb("nibb", [128, 1376], BF16)
        tb = sb("tb", [128, 1376], F32)
        wout = [sb(f"wout{i}", [128, 1376], BF16) for i in range(2)]
        dzt = [sb(f"dzt{i}", [128, 64], I32) for i in range(2)]
        dst = [sb(f"dst{i}", [128, 512], F32) for i in range(2)]
        znibt = sb("znibt", [128, 512], I32)
        zft = sb("zft", [128, 512], F32)
        nibi2 = sb("nibi2", [128, 512], I32)
        nibb2 = sb("nibb2", [128, 512], BF16)
        tb2 = sb("tb2", [128, 512], F32)
        wod = [sb(f"wod{i}", [128, 512], BF16) for i in range(2)]
        psS = ps("psS", [128, 1376], F32)
        psZ = ps("psZ", [128, 1376], F32)
        ldg = st.enter_context(nc.semaphore("ldg"))
        lzr = [st.enter_context(nc.semaphore(f"lzr{i}")) for i in range(2)]
        srw = [st.enter_context(nc.semaphore(f"srw{i}")) for i in range(2)]
        dldp = [st.enter_context(nc.semaphore(f"dldp{i}")) for i in range(2)]
        sdw = [st.enter_context(nc.semaphore(f"sdw{i}")) for i in range(2)]
        zn = st.enter_context(nc.semaphore("zn"))
        zc = st.enter_context(nc.semaphore("zc"))
        bc = st.enter_context(nc.semaphore("bc"))
        wn = st.enter_context(nc.semaphore("wn"))
        vsf = st.enter_context(nc.semaphore("vsf"))
        wv = st.enter_context(nc.semaphore("wv"))
        so = st.enter_context(nc.semaphore("so"))
        dld = st.enter_context(nc.semaphore("dld"))
        dzn = st.enter_context(nc.semaphore("dzn"))
        dzc = st.enter_context(nc.semaphore("dzc"))
        dwn = st.enter_context(nc.semaphore("dwn"))
        dvs = st.enter_context(nc.semaphore("dvs"))
        dwv = st.enter_context(nc.semaphore("dwv"))
        block = st.enter_context(nc.Block())

        @block.gpsimd
        def _(g_):
            g_.dma_start(out=qw_sb[0][:], in_=GQ.rearrange("(g p) c -> p g c", p=128)).then_inc(ldg, 16)
            g_.dma_start(out=qw_sb[1][:], in_=UQ.rearrange("(g p) c -> p g c", p=128)).then_inc(ldg, 16)
            g_.dma_start(out=qw_sb[2][:], in_=DQ.rearrange("(g p) c -> p g c", p=128)).then_inc(ldg, 16)
            g_.dma_start(out=ones[:], in_=ONE).then_inc(ldg, 16)

        @block.sync
        def _(sy):
            def store(u2):
                mi2, g2 = units[u2]
                sy.wait_ge(wv, u2 + 1)
                sy.dma_start(out=mats[mi2][4][128 * g2:128 * (g2 + 1), :],
                             in_=wout[u2 % 2][:]).then_inc(srw[u2 % 2], 16)
            for u, (mi, g) in enumerate(units):
                ZZ, SS = mats[mi][3], mats[mi][2]
                if u >= 2:
                    sy.wait_ge(zn, u - 1)
                    sy.wait_ge(bc, u - 1)
                sy.dma_start(out=zqr[u % 2][:], in_=ZZ[g:g + 1, :]).then_inc(lzr[u % 2], 16)
                sy.dma_start(out=srow[u % 2][:], in_=SS[g:g + 1, :]).then_inc(lzr[u % 2], 16)
                if u >= 2:
                    store(u - 2)
            store(NU - 2)
            store(NU - 1)
            def dstore(u2):
                sy.wait_ge(dwv, u2 + 1)
                sy.dma_start(out=WD[128 * u2:128 * (u2 + 1), :],
                             in_=wod[u2 % 2][:]).then_inc(sdw[u2 % 2], 16)
            for ud in range(ND):
                if ud >= 2:
                    sy.wait_ge(dzn, ud - 1)
                    sy.wait_ge(dwv, ud - 1)
                sy.dma_start(out=dzt[ud % 2][:], in_=DZE[128 * ud:128 * (ud + 1), :]).then_inc(dldp[ud % 2], 16)
                sy.dma_start(out=dst[ud % 2][:], in_=DSE[128 * ud:128 * (ud + 1), :]).then_inc(dldp[ud % 2], 16)
                if ud >= 2:
                    dstore(ud - 2)
            dstore(ND - 2)
            dstore(ND - 1)
            sy.wait_ge(srw[0], 16 * (NU // 2))
            sy.wait_ge(srw[1], 16 * (NU // 2))
            sy.wait_ge(sdw[0], 16 * 43)
            sy.wait_ge(sdw[1], 16 * 43)

        @block.vector
        def _(v):
            v.wait_ge(ldg, 64)
            for u, (mi, g) in enumerate(units):
                v.wait_ge(lzr[u % 2], 32 * (u // 2 + 1))
                if u >= 1:
                    v.wait_ge(zc, 2 * (u - 1) + 1)
                for p in range(8):
                    inst = v.tensor_scalar(znib[:, p * C:(p + 1) * C], zqr[u % 2][:],
                                           4 * ARO[p], 15, ALU.logical_shift_right, ALU.bitwise_and)
                inst.then_inc(zn, 1)
                if u >= 1:
                    v.wait_ge(zc, 2 * (u - 1) + 2)
                for p in range(8):
                    inst = v.tensor_scalar(nibi[:, p * C:(p + 1) * C], qw_sb[mi][:, g, :],
                                           4 * ARO[p], 15, ALU.logical_shift_right, ALU.bitwise_and)
                inst.then_inc(wn, 1)
                v.wait_ge(bc, u + 1)
                v.wait_ge(zc, 2 * u + 2)
                if u >= 1:
                    v.wait_ge(wv, u)
                v.tensor_tensor(tb[:], nibb[:], psZ[:], ALU.subtract).then_inc(vsf, 1)
                v.wait_ge(vsf, u + 1)
                if u >= 2:
                    v.wait_ge(srw[u % 2], 16 * (u // 2))
                v.tensor_tensor(wout[u % 2][:], tb[:], psS[:], ALU.mult).then_inc(wv, 1)
            for ud in range(ND):
                v.wait_ge(dldp[ud % 2], 32 * (ud // 2 + 1))
                if ud >= 1:
                    v.wait_ge(dzc, 2 * (ud - 1) + 1)
                for p in range(8):
                    inst = v.tensor_scalar(znibt[:, p * 64:(p + 1) * 64], dzt[ud % 2][:],
                                           4 * ARO[p], 15, ALU.logical_shift_right, ALU.bitwise_and)
                inst.then_inc(dzn, 1)
                if ud >= 1:
                    v.wait_ge(dzc, 2 * (ud - 1) + 2)
                for p in range(8):
                    inst = v.tensor_scalar(nibi2[:, p * 64:(p + 1) * 64], qw_sb[2][:, ud, :],
                                           4 * ARO[p], 15, ALU.logical_shift_right, ALU.bitwise_and)
                inst.then_inc(dwn, 1)
                v.wait_ge(dzc, 2 * ud + 2)
                if ud >= 1:
                    v.wait_ge(dwv, ud)
                v.tensor_tensor(tb2[:], nibb2[:], zft[:], ALU.subtract).then_inc(dvs, 1)
                v.wait_ge(dvs, ud + 1)
                if ud >= 2:
                    v.wait_ge(sdw[ud % 2], 16 * (ud // 2))
                v.tensor_tensor(wod[ud % 2][:], tb2[:], dst[ud % 2][:], ALU.mult).then_inc(dwv, 1)

        @block.scalar
        def _(s):
            for u, (mi, g) in enumerate(units):
                s.wait_ge(zn, u + 1)
                if u >= 1:
                    s.wait_ge(bc, u)
                s.activation(zb[:], znib[:], AF.Copy).then_inc(zc, 1)
                s.wait_ge(wn, u + 1)
                if u >= 1:
                    s.wait_ge(vsf, u)
                s.activation(nibb[:], nibi[:], AF.Copy).then_inc(zc, 1)
            for ud in range(ND):
                s.wait_ge(dzn, ud + 1)
                if ud >= 1:
                    s.wait_ge(dvs, ud)
                s.activation(zft[:], znibt[:], AF.Copy).then_inc(dzc, 1)
                s.wait_ge(dwn, ud + 1)
                if ud >= 1:
                    s.wait_ge(dvs, ud)
                s.activation(nibb2[:], nibi2[:], AF.Copy).then_inc(dzc, 1)

        @block.tensor
        def _(t):
            t.wait_ge(ldg, 64)
            for u, (mi, g) in enumerate(units):
                t.wait_ge(zc, 2 * u + 1)
                t.wait_ge(lzr[u % 2], 32 * (u // 2 + 1))
                if u >= 1:
                    t.wait_ge(wv, u)
                for ci, cw in [(0, 512), (1, 512), (2, 352)]:
                    t.matmul(psS[:, 512 * ci:512 * ci + cw], ones[:],
                             srow[u % 2][:, 512 * ci:512 * ci + cw], start=True, stop=True)
                for ci, cw in [(0, 512), (1, 512), (2, 352)]:
                    inst = t.matmul(psZ[:, 512 * ci:512 * ci + cw], ones[:],
                                    zb[:, 512 * ci:512 * ci + cw], start=True, stop=True)
                inst.then_inc(bc, 1)
    return nc


def host_dequant_inputs(gq, gs, gz, uq, us, uz, dq, ds, dz):
    lp_gate = lperm_global(172)
    lp_down = lperm_global(64)
    dq_perm = dq[lp_gate, :]
    ds_exp = np.repeat(ds, 128, axis=0)[lp_gate, :]
    dz_exp = np.repeat(dz, 128, axis=0)[lp_gate, :]
    one = np.ones((1, 128), np.float32)
    maps = []
    for c in range(8):
        maps.append(dict(
            gq=cc(gq[:, 172 * c:172 * (c + 1)]),
            gs=cc(gs[:, lp_gate[1376 * c:1376 * (c + 1)]]),
            gz=cc(gz[:, 172 * c:172 * (c + 1)]),
            uq=cc(uq[:, 172 * c:172 * (c + 1)]),
            us=cc(us[:, lp_gate[1376 * c:1376 * (c + 1)]]),
            uz=cc(uz[:, 172 * c:172 * (c + 1)]),
            dq=cc(dq_perm[:, 64 * c:64 * (c + 1)]),
            dse=cc(ds_exp[:, lp_down[512 * c:512 * (c + 1)]]),
            dze=cc(dz_exp[:, 64 * c:64 * (c + 1)]),
            one=one,
        ))
    return maps, lp_gate, lp_down


def build_ffn0():
    nc = bass.Bass("TRN2", target_bir_lowering=False)
    XNT = nc.dram_tensor("xnT", [D, T], BF16, kind="ExternalInput").ap()
    WGS = nc.dram_tensor("wgs", [86, D, 128], BF16, kind="ExternalInput").ap()
    WUS = nc.dram_tensor("wus", [86, D, 128], BF16, kind="ExternalInput").ap()
    HT = nc.dram_tensor("hT", [11008, T], BF16, kind="ExternalOutput").ap()

    NG = 86 * 2
    with ExitStack() as st:
        sb = lambda *a: st.enter_context(nc.sbuf_tensor(*a))
        ps = lambda *a: st.enter_context(nc.psum_tensor(*a))
        xnT = sb("xnTs", [128, 32, T], BF16)
        wg = [sb(f"wg{i}", [128, 32, 128], BF16) for i in range(2)]
        wu = [sb(f"wu{i}", [128, 32, 128], BF16) for i in range(2)]
        sg = [sb(f"sg{i}", [128, 512], F32) for i in range(2)]
        ht = [sb(f"ht{i}", [128, 512], BF16) for i in range(2)]
        psG = [ps(f"psG{i}", [128, 512], F32) for i in range(2)]
        psU = [ps(f"psU{i}", [128, 512], F32) for i in range(2)]
        lds = st.enter_context(nc.semaphore("lds"))
        lgw = [st.enter_context(nc.semaphore(f"lgw{i}")) for i in range(2)]
        luw = [st.enter_context(nc.semaphore(f"luw{i}")) for i in range(2)]
        sht = [st.enter_context(nc.semaphore(f"sht{i}")) for i in range(2)]
        mmg = st.enter_context(nc.semaphore("mmg"))
        sgs = st.enter_context(nc.semaphore("sgs"))
        hv = st.enter_context(nc.semaphore("hv"))
        so = st.enter_context(nc.semaphore("so"))
        block = st.enter_context(nc.Block())

        @block.sync
        def _(sy):
            sy.dma_start(out=xnT[:], in_=XNT.rearrange("(j p) t -> p j t", p=128)).then_inc(lds, 16)
            for gi in range(NG):
                nt, th = gi // 2, gi % 2
                sy.wait_ge(hv, gi + 1)
                sy.dma_start(out=HT[128 * nt:128 * (nt + 1), 512 * th:512 * (th + 1)],
                             in_=ht[gi % 2][:]).then_inc(sht[gi % 2], 16)
            sy.wait_ge(sht[0], 16 * (NG // 2))
            sy.wait_ge(sht[1], 16 * (NG // 2))

        @block.gpsimd
        def _(g):
            for nt in range(86):
                if nt >= 2:
                    g.wait_ge(mmg, 4 * (nt - 1))
                g.dma_start(out=wg[nt % 2][:], in_=WGS[nt].rearrange("(j p) n -> p j n", p=128)).then_inc(lgw[nt % 2], 16)
                g.dma_start(out=wu[nt % 2][:], in_=WUS[nt].rearrange("(j p) n -> p j n", p=128)).then_inc(luw[nt % 2], 16)

        @block.tensor
        def _(t):
            t.wait_ge(lds, 16)
            for gi in range(NG):
                nt, th = gi // 2, gi % 2
                if th == 0:
                    t.wait_ge(lgw[nt % 2], 16 * (nt // 2 + 1))
                    t.wait_ge(luw[nt % 2], 16 * (nt // 2 + 1))
                if gi >= 2:
                    t.wait_ge(sgs, gi - 1)
                    t.wait_ge(hv, gi - 1)
                sl = slice(512 * th, 512 * (th + 1))
                for j in range(32):
                    inst = t.matmul(psG[gi % 2][:], wg[nt % 2][:, j, :], xnT[:, j, sl],
                                    start=(j == 0), stop=(j == 31))
                inst.then_inc(mmg, 1)
                for j in range(32):
                    inst = t.matmul(psU[gi % 2][:], wu[nt % 2][:, j, :], xnT[:, j, sl],
                                    start=(j == 0), stop=(j == 31))
                inst.then_inc(mmg, 1)

        @block.scalar
        def _(s):
            for gi in range(NG):
                s.wait_ge(mmg, 2 * gi + 1)
                if gi >= 2:
                    s.wait_ge(hv, gi - 1)
                s.activation(sg[gi % 2][:], psG[gi % 2][:], AF.Silu).then_inc(sgs, 1)

        @block.vector
        def _(v):
            for gi in range(NG):
                v.wait_ge(mmg, 2 * gi + 2)
                v.wait_ge(sgs, gi + 1)
                if gi >= 2:
                    v.wait_ge(sht[gi % 2], 16 * (gi // 2))
                v.tensor_tensor(ht[gi % 2][:], sg[gi % 2][:], psU[gi % 2][:], ALU.mult).then_inc(hv, 1)
    return nc


def build_downwo():
    nc = bass.Bass("TRN2", target_bir_lowering=False)
    HT = nc.dram_tensor("hT", [11008, T], BF16, kind="ExternalInput").ap()
    WDS = nc.dram_tensor("wds", [32, 11008, 128], BF16, kind="ExternalInput").ap()
    CXT = nc.dram_tensor("cxT", [D, T], BF16, kind="ExternalInput").ap()
    WOS = nc.dram_tensor("wos", [32, D, 128], F32, kind="ExternalInput").ap()
    XTP = nc.dram_tensor("xTp", [D, T], F32, kind="ExternalInput").ap()
    ONE = nc.dram_tensor("one", [1, 128], F32, kind="ExternalInput").ap()
    ONEC = nc.dram_tensor("onec", [128, 1], F32, kind="ExternalInput").ap()
    X1T = nc.dram_tensor("x1T", [D, T], F32, kind="ExternalOutput").ap()
    XN1T = nc.dram_tensor("xn1T", [D, T], BF16, kind="ExternalOutput").ap()

    with ExitStack() as st:
        sb = lambda *a: st.enter_context(nc.sbuf_tensor(*a))
        ps = lambda *a: st.enter_context(nc.psum_tensor(*a))
        hth = sb("hth", [128, 86, 256], BF16)
        cxh = sb("cxh", [128, 32, 256], BF16)
        g0h = sb("g0h", [128, 32, 256], BF16)
        x1bh = sb("x1bh", [128, 32, 256], BF16)
        wd = [sb(f"wd{i}", [128, 86, 128], BF16) for i in range(2)]
        wo = [sb(f"wo{i}", [128, 32, 128], BF16) for i in range(2)]
        xt = [sb(f"xt{i}", [128, 256], F32) for i in range(2)]
        s0 = [sb(f"s0{i}", [128, 256], F32) for i in range(2)]
        tsb = sb("tsb", [128, 256], F32)
        x1s = [sb(f"x1s{i}", [128, 256], F32) for i in range(2)]
        x1q = [sb(f"x1q{i}", [128, 256], BF16) for i in range(2)]
        xn1 = [sb(f"xn1{i}", [128, 256], BF16) for i in range(2)]
        onesc = sb("onesc", [128, 1], BF16)
        oner = sb("oner", [1, 128], F32)
        mrow = sb("mrow", [1, 256], F32)
        t1row = sb("t1row", [1, 256], F32)
        vrow = sb("vrow", [1, 256], F32)
        hrow = sb("hrow", [1, 256], F32)
        rstdrow = sb("rstdrow", [1, 256], F32)
        nbrow = sb("nbrow", [1, 256], F32)
        psD = ps("psD", [128, 256], F32)
        psS = ps("psS2", [128, 256], F32)
        psSum = ps("psSum", [1, 256], F32)
        psSq = ps("psSq", [1, 256], F32)
        psR = ps("psR", [128, 256], F32)
        psN = ps("psN", [128, 256], F32)
        lo = st.enter_context(nc.semaphore("lo"))
        lh = st.enter_context(nc.semaphore("lh"))
        lc = st.enter_context(nc.semaphore("lc"))
        lon = st.enter_context(nc.semaphore("lon"))
        lwd = [st.enter_context(nc.semaphore(f"lwd{i}")) for i in range(2)]
        lwo = [st.enter_context(nc.semaphore(f"lwo{i}")) for i in range(2)]
        lxp = [st.enter_context(nc.semaphore(f"lxp{i}")) for i in range(2)]
        sx1 = [st.enter_context(nc.semaphore(f"sx1{i}")) for i in range(2)]
        sxn1 = [st.enter_context(nc.semaphore(f"sxn1{i}")) for i in range(2)]
        mmd = st.enter_context(nc.semaphore("mmd"))
        mms = st.enter_context(nc.semaphore("mms"))
        mmq = st.enter_context(nc.semaphore("mmq"))
        gsl = st.enter_context(nc.semaphore("gsl"))
        sc0 = st.enter_context(nc.semaphore("sc0"))
        sxc = st.enter_context(nc.semaphore("sxc"))
        vv = st.enter_context(nc.semaphore("vv"))
        vw = st.enter_context(nc.semaphore("vw"))
        sln = st.enter_context(nc.semaphore("sln"))
        bcm = st.enter_context(nc.semaphore("bcm"))
        vn = st.enter_context(nc.semaphore("vn"))
        so = st.enter_context(nc.semaphore("so"))
        block = st.enter_context(nc.Block())

        def x1_store_count(u):
            hf2, ot2 = divmod(u, 32)
            return 64 * hf2 + ot2 + 1

        def xn1_store_count(u):
            hf2, ot2 = divmod(u, 32)
            return 64 * hf2 + 32 + ot2 + 1

        @block.sync
        def _(sy):
            sy.dma_start(out=oner[:], in_=ONE).then_inc(lo, 16)
            for hf in range(4):
                if hf >= 1:
                    sy.wait_ge(mmd, 32 * hf)
                sy.dma_start(out=hth[:], in_=HT.rearrange("(j p) t -> p j t", p=128)[:, :, 256 * hf:256 * (hf + 1)]).then_inc(lh, 16)
                if hf >= 1:
                    sy.wait_ge(mms, 32 * hf)
                sy.dma_start(out=cxh[:], in_=CXT.rearrange("(j p) t -> p j t", p=128)[:, :, 256 * hf:256 * (hf + 1)]).then_inc(lc, 16)
                for ot in range(32):
                    u = 32 * hf + ot
                    if u >= 2:
                        sy.wait_ge(vv, 2 * (u - 2) + 2)
                    sy.dma_start(out=xt[ot % 2][:], in_=XTP[128 * ot:128 * (ot + 1), 256 * hf:256 * (hf + 1)]).then_inc(lxp[u % 2], 16)
                    if ot >= 2:
                        sy.dma_start(out=X1T[128 * (ot - 2):128 * (ot - 1), 256 * hf:256 * (hf + 1)],
                                     in_=x1s[ot % 2][:]).then_inc(sx1[ot % 2], 16)
                for ot in [30, 31]:
                    sy.wait_ge(vv, 2 * (32 * hf + ot) + 2)
                    sy.dma_start(out=X1T[128 * ot:128 * (ot + 1), 256 * hf:256 * (hf + 1)],
                                 in_=x1s[ot % 2][:]).then_inc(sx1[ot % 2], 16)
                for ot in range(32):
                    sy.wait_ge(vn, 2 * (32 * hf + ot) + 2)
                    sy.dma_start(out=XN1T[128 * ot:128 * (ot + 1), 256 * hf:256 * (hf + 1)],
                                 in_=xn1[ot % 2][:]).then_inc(sxn1[ot % 2], 16)
            for par in range(2):
                sy.wait_ge(sx1[par], 16 * 64)
                sy.wait_ge(sxn1[par], 16 * 64)

        @block.gpsimd
        def _(g):
            g.dma_start(out=onesc[:], in_=ONEC).then_inc(lon, 16)
            for hf in range(4):
                for ot in range(32):
                    u = 32 * hf + ot
                    if u >= 2:
                        g.wait_ge(mmd, u - 1)
                    g.dma_start(out=wd[ot % 2][:], in_=WDS[ot].rearrange("(j p) n -> p j n", p=128)).then_inc(lwd[ot % 2], 16)
                for ot in range(32):
                    u = 32 * hf + ot
                    if u >= 2:
                        g.wait_ge(mms, u - 1)
                    g.dma_start(out=wo[ot % 2][:], in_=WOS[ot].rearrange("(j p) n -> p j n", p=128)).then_inc(lwo[ot % 2], 16)

        @block.tensor
        def _(t):
            for hf in range(4):
                t.wait_ge(lh, 16 * (hf + 1))
                for ot in range(32):
                    u = 32 * hf + ot
                    t.wait_ge(lwd[ot % 2], 16 * (u // 2 + 1))
                    if u >= 1:
                        t.wait_ge(gsl, u)
                    for j in range(86):
                        inst = t.matmul(psD[:], wd[ot % 2][:, j, :], hth[:, j, :],
                                        start=(j == 0), stop=(j == 85))
                    inst.then_inc(mmd, 1)
                t.wait_ge(lc, 16 * (hf + 1))
                if hf >= 1:
                    t.wait_ge(vw, 6 * hf)
                    t.wait_ge(vn, 64 * hf)
                for ot in range(34):
                    if ot < 32:
                        u = 32 * hf + ot
                        t.wait_ge(lwo[ot % 2], 16 * (u // 2 + 1))
                        if u >= 1:
                            t.wait_ge(sc0, u)
                        for j in range(32):
                            inst = t.matmul(psS[:], wo[ot % 2][:, j, :], cxh[:, j, :],
                                            start=(j == 0), stop=(j == 31))
                        inst.then_inc(mms, 1)
                    if ot >= 2:
                        o2 = ot - 2
                        u2 = 32 * hf + o2
                        t.wait_ge(sxc, 2 * u2 + 2)
                        if hf == 0 and o2 == 0:
                            t.wait_ge(lon, 16)
                        t.matmul(psSum[:], onesc[:], x1bh[:, o2, :],
                                 start=(o2 == 0), stop=(o2 == 31))
                        inst = t.matmul(psSq[:], onesc[:], x1q[o2 % 2][:],
                                        start=(o2 == 0), stop=(o2 == 31))
                        inst.then_inc(mmq, 1)
                t.wait_ge(sln, 2 * hf + 2)
                t.wait_ge(vw, 6 * hf + 6)
                if hf == 0:
                    t.wait_ge(lo, 16)
                t.matmul(psR[:], oner[:], rstdrow[:], start=True, stop=True)
                inst = t.matmul(psN[:], oner[:], nbrow[:], start=True, stop=True)
                inst.then_inc(bcm, 1)

        @block.scalar
        def _(s):
            for hf in range(4):
                for ot in range(32):
                    u = 32 * hf + ot
                    s.wait_ge(mmd, u + 1)
                    if hf >= 1 and ot == 0:
                        s.wait_ge(vv, 64 * hf)
                    s.activation(g0h[:, ot, :], psD[:], AF.Silu).then_inc(gsl, 1)
                for ot in range(32):
                    u = 32 * hf + ot
                    s.wait_ge(mms, u + 1)
                    if u >= 2:
                        s.wait_ge(vv, 2 * (u - 2) + 1)
                    s.activation(s0[ot % 2][:], psS[:], AF.Copy).then_inc(sc0, 1)
                    s.wait_ge(vv, 2 * u + 2)
                    if hf >= 1 and ot == 0:
                        s.wait_ge(vn, 64 * hf)
                    if u >= 2:
                        s.wait_ge(mmq, u - 1)
                    s.activation(x1bh[:, ot, :], x1s[ot % 2][:], AF.Copy)
                    inst = s.activation(x1q[ot % 2][:], x1s[ot % 2][:], AF.Square)
                    inst.then_inc(sxc, 2)
                s.wait_ge(vw, 6 * hf + 4)
                s.activation(vrow[:], hrow[:], AF.Ln).then_inc(sln, 1)
                s.wait_ge(vw, 6 * hf + 5)
                s.activation(rstdrow[:], hrow[:], AF.Exp).then_inc(sln, 1)

        @block.vector
        def _(v):
            for hf in range(4):
                for ot in range(32):
                    u = 32 * hf + ot
                    v.wait_ge(gsl, u + 1)
                    v.wait_ge(sc0, u + 1)
                    v.wait_ge(lxp[u % 2], 16 * (u // 2 + 1))
                    if hf >= 1 and ot == 0:
                        v.wait_ge(vn, 64 * hf)
                    if u >= 1:
                        v.wait_ge(vv, 2 * u)
                    v.tensor_tensor(tsb[:], g0h[:, ot, :], s0[ot % 2][:], ALU.mult).then_inc(vv, 1)
                    v.wait_ge(vv, 2 * u + 1)
                    if u >= 2:
                        v.wait_ge(sx1[u % 2], 16 * ((u - 2) // 2 + 1) if False else 16 * (u // 2))
                    v.tensor_tensor(x1s[ot % 2][:], tsb[:], xt[ot % 2][:], ALU.add).then_inc(vv, 1)
                v.wait_ge(mmq, 32 * hf + 32)
                vb = 6 * hf
                v.tensor_scalar(mrow[:], psSum[:], 1.0 / D, None, ALU.mult).then_inc(vw, 1)
                v.wait_ge(vw, vb + 1)
                v.tensor_tensor(t1row[:], mrow[:], mrow[:], ALU.mult).then_inc(vw, 1)
                v.wait_ge(vw, vb + 2)
                v.scalar_tensor_tensor(vrow[:], psSq[:], 1.0 / D, t1row[:], ALU.mult, ALU.subtract).then_inc(vw, 1)
                v.wait_ge(vw, vb + 3)
                v.tensor_scalar(hrow[:], vrow[:], EPS, None, ALU.add).then_inc(vw, 1)
                v.wait_ge(sln, 2 * hf + 1)
                v.tensor_scalar(hrow[:], vrow[:], -0.5, None, ALU.mult).then_inc(vw, 1)
                v.wait_ge(sln, 2 * hf + 2)
                v.scalar_tensor_tensor(nbrow[:], mrow[:], -1.0, rstdrow[:], ALU.mult, ALU.mult).then_inc(vw, 1)
                v.wait_ge(vv, 64 * hf + 64)
                v.wait_ge(bcm, hf + 1)
                v.wait_ge(sxc, 2 * (32 * hf + 31) + 2)
                for ot in range(32):
                    u = 32 * hf + ot
                    if ot >= 1:
                        v.wait_ge(vn, 2 * u)
                    v.tensor_tensor(tsb[:], x1bh[:, ot, :], psR[:], ALU.mult).then_inc(vn, 1)
                    v.wait_ge(vn, 2 * u + 1)
                    if u >= 2:
                        v.wait_ge(sxn1[u % 2], 16 * (u // 2))
                    v.tensor_tensor(xn1[ot % 2][:], tsb[:], psN[:], ALU.add).then_inc(vn, 1)
    return nc


def build_attn1(stage=2, nkv=NKV):
    nc = bass.Bass("TRN2", target_bir_lowering=False)
    XNT = nc.dram_tensor("xn1T", [D, T], BF16, kind="ExternalInput").ap()
    WQS = nc.dram_tensor("wqs", [32, D, 128], F32, kind="ExternalInput").ap()
    C1 = nc.dram_tensor("c1", [HD, T], F32, kind="ExternalInput").ap()
    S1 = nc.dram_tensor("s1", [HD, T], F32, kind="ExternalInput").ap()
    KC = nc.dram_tensor("kc", [512, NKV, 16, HD], F32, kind="ExternalInput").ap()
    VC = nc.dram_tensor("vc", [512, NKV, 16, HD], F32, kind="ExternalInput").ap()
    BTI = nc.dram_tensor("bti", [128, 2], I32, kind="ExternalInput").ap()
    EYE = nc.dram_tensor("eye", [128, 128], F32, kind="ExternalInput").ap()
    CTX1 = nc.dram_tensor("ctx1", [NKV, 128, 32, 128], BF16, kind="ExternalOutput").ap()

    with ExitStack() as st:
        sb = lambda *a: st.enter_context(nc.sbuf_tensor(*a))
        ps = lambda *a: st.enter_context(nc.psum_tensor(*a))
        xnT = sb("xnTs", [128, 32, T], BF16)
        q1T = sb("q1Ts", [128, 32, T], BF16)
        wq = [sb(f"wq{i}", [128, 32, 128], BF16) for i in range(2)]
        c1s = sb("c1s", [128, T], F32)
        s1s = sb("s1s", [128, T], F32)
        rt1 = sb("rt1", [128, 512], F32)
        rt2 = sb("rt2", [128, 512], F32)
        eyeb = sb("eyeb", [128, 128], BF16)
        idx = sb("idx", [128, 2], I32)
        kgB = sb("kgB", [128, 2, 16, 128], BF16)
        vstage = sb("vstage", [128, 2048], BF16)
        vgB = sb("vgB", [128, 2, 16, 129], BF16)
        kT1 = sb("kT1", [128, 32, 128], BF16)
        expb = [sb(f"expb{i}", [128, 512], BF16) for i in range(2)]
        rcp = sb("rcp", [128, 4], F32)
        cxg = [sb(f"cxg{i}", [128, 32, 128], BF16) for i in range(2)]
        pmm = [ps(f"pmm{i}", [128, 512], F32) for i in range(2)]
        ptp = ps("ptp", [128, 128], BF16)
        ctp = [ps(f"ctp{i}", [128, 129], F32) for i in range(4)]
        lds = st.enter_context(nc.semaphore("lds"))
        lxn = st.enter_context(nc.semaphore("lxn"))
        ley = st.enter_context(nc.semaphore("ley"))
        lwq = [st.enter_context(nc.semaphore(f"lwq{i}")) for i in range(2)]
        kf = [st.enter_context(nc.semaphore(f"kf{i}")) for i in range(2)]
        vfp = [st.enter_context(nc.semaphore(f"vfp{i}")) for i in range(2)]
        sf = [st.enter_context(nc.semaphore(f"sf{i}")) for i in range(2)]
        msd = st.enter_context(nc.semaphore("msd"))
        mmp = st.enter_context(nc.semaphore("mmp"))
        evp = st.enter_context(nc.semaphore("evp"))
        vvp = st.enter_context(nc.semaphore("vvp"))
        vcp = st.enter_context(nc.semaphore("vcp"))
        tpf = st.enter_context(nc.semaphore("tpf"))
        tcf = st.enter_context(nc.semaphore("tcf"))
        mmf = st.enter_context(nc.semaphore("mmf"))
        exf = st.enter_context(nc.semaphore("exf"))
        avf = st.enter_context(nc.semaphore("avf"))
        vrc = st.enter_context(nc.semaphore("vrc"))
        cte = st.enter_context(nc.semaphore("cte"))
        sof = st.enter_context(nc.semaphore("sof"))
        block = st.enter_context(nc.Block())

        def av_tile(t, n):
            g, ti = divmod(n, 256)
            qh, th, kt = ti // 64, (ti // 32) % 2, ti % 32
            half, j = kt // 16, kt % 16
            t.wait_ge(exf, n + 1)
            t.wait_ge(vcp, 2 * g + half + 1)
            if kt == 0:
                t.wait_ge(msd, 1)
                gidx = n // 32
                if gidx >= 1:
                    t.wait_ge(cte, gidx)
            for qs4 in range(4):
                inst = t.matmul(ctp[qs4][:], expb[n % 2][:, 128 * qs4:128 * (qs4 + 1)],
                                vgB[:, half, j, :], start=(kt == 0), stop=(kt == 31))
            inst.then_inc(avf, 1)

        @block.sync
        def _(sy):
            sy.dma_start(out=xnT[:], in_=XNT.rearrange("(j p) t -> p j t", p=128)).then_inc(lxn, 16)
            sy.dma_start(out=c1s[:], in_=C1).then_inc(lds, 16)
            sy.dma_start(out=s1s[:], in_=S1).then_inc(lds, 16)
            sy.dma_start(out=idx[:], in_=BTI).then_inc(lds, 16)
            if stage < 2:
                sy.wait_ge(evp, 64)
                sy.dma_start(out=CTX1.rearrange("g p a b -> p (g a b)", p=128) if False else CTX1.rearrange("g p a b -> p g (a b)"),
                             in_=q1T[:].rearrange("p a t -> p (a t)").rearrange("p (g x) -> p g x", g=8)).then_inc(sof, 16)
                sy.wait_ge(sof, 16)
                return
            for g in range(nkv):
                sy.wait_ge(cte, 8 * (g + 1))
                sy.dma_start(out=CTX1[g].rearrange("p a b -> p (a b)"),
                             in_=cxg[g % 2][:].rearrange("p a b -> p (a b)")).then_inc(sf[g % 2], 16)
            for par in range(2):
                sy.wait_ge(sf[par], 16 * len([g for g in range(nkv) if g % 2 == par]))

        @block.gpsimd
        def _(g_):
            g_.dma_start(out=eyeb[:], in_=EYE).then_inc(ley, 16)
            g_.memset(vgB[:, :, :, 128:129], 1.0).then_inc(msd, 1)
            for pi in range(32):
                if pi >= 2:
                    g_.wait_ge(mmp, 2 * pi - 2)
                g_.dma_start(out=wq[pi % 2][:], in_=WQS[pi].rearrange("(j p) n -> p j n", p=128)).then_inc(lwq[pi % 2], 16)
            if stage < 2:
                return
            g_.wait_ge(lds, 48)
            kcf = KC.rearrange("b h j d -> b (h j d)")
            vcf = VC.rearrange("b h j d -> b (h j d)")
            for g in range(nkv):
                if g >= 1:
                    g_.wait_ge(tpf, 32 * g)
                for half in range(2):
                    g_.indirect_dma_start(
                        out=kgB[:, half, :, :].rearrange("p a b -> p (a b)"), out_offset=None,
                        in_=kcf, in_offset=IndirectOffsetOnAxis(idx[:, half:half + 1], axis=0),
                        element_offset=g * 2048).then_inc(kf[half], 16)
                    if g >= 1 or half >= 1:
                        g_.wait_ge(vcp, 2 * g + half)
                    g_.indirect_dma_start(
                        out=vstage[:], out_offset=None,
                        in_=vcf, in_offset=IndirectOffsetOnAxis(idx[:, half:half + 1], axis=0),
                        element_offset=g * 2048).then_inc(vfp[half], 16)

        @block.tensor
        def _(t):
            t.wait_ge(lxn, 16)
            for G in range(64):
                pi, th = G // 2, G % 2
                if th == 0:
                    t.wait_ge(lwq[pi % 2], 16 * (pi // 2 + 1))
                if G >= 2:
                    t.wait_ge(evp, G - 1)
                for j in range(32):
                    inst = t.matmul(pmm[G % 2][:], wq[pi % 2][:, j, :],
                                    xnT[:, j, 512 * th:512 * (th + 1)],
                                    start=(j == 0), stop=(j == 31))
                inst.then_inc(mmp, 1)
            if stage < 2:
                return
            pending = None
            for g in range(nkv):
                if pending is not None:
                    av_tile(t, pending)
                    pending = None
                if g == 0:
                    t.wait_ge(ley, 16)
                for kt in range(32):
                    n_t = 32 * g + kt
                    t.wait_ge(kf[kt // 16], 16 * (g + 1))
                    if n_t >= 1:
                        t.wait_ge(tcf, n_t)
                    t.transpose(ptp[:], kgB[:, kt // 16, kt % 16, :], eyeb[:]).then_inc(tpf, 1)
                for ti in range(256):
                    n = 256 * g + ti
                    qh, th, kt = ti // 64, (ti // 32) % 2, ti % 32
                    h = 4 * g + qh
                    if ti == 0:
                        t.wait_ge(tcf, 32 * (g + 1))
                        t.wait_ge(evp, 64)
                    if n >= 2:
                        t.wait_ge(exf, n - 1)
                    t.matmul(pmm[n % 2][:], kT1[:, kt, :], q1T[:, h, 512 * th:512 * (th + 1)],
                             start=True, stop=True).then_inc(mmf, 1)
                    if ti >= 1:
                        av_tile(t, n - 1)
                pending = 256 * (g + 1) - 1
            av_tile(t, pending)

        @block.scalar
        def _(s):
            if stage < 2:
                return
            for n in range(256 * nkv):
                if n >= 2:
                    s.wait_ge(avf, n - 1)
                s.wait_ge(mmf, n + 1)
                s.activation(expb[n % 2][:], pmm[n % 2][:], AF.Exp).then_inc(exf, 1)
                ti = n % 256
                if ti % 32 == 31:
                    gidx = n // 32
                    g, qh, th = n // 256, ti // 64, (ti // 32) % 2
                    s.wait_ge(vrc, 4 * (gidx + 1))
                    if qh == 0 and th == 0 and g >= 2:
                        s.wait_ge(sf[g % 2], 16 * (g // 2))
                    for qs4 in range(4):
                        inst = s.activation(cxg[g % 2][:, qh * 8 + th * 4 + qs4, :],
                                            ctp[qs4][:, 0:128], AF.Copy,
                                            scale=rcp[:, qs4:qs4 + 1])
                    inst.then_inc(cte, 1)

        @block.vector
        def _(v):
            v.wait_ge(lds, 48)
            for G in range(64):
                pi, th = G // 2, G % 2
                sl = slice(512 * th, 512 * (th + 1))
                p = pmm[G % 2]
                v.wait_ge(mmp, G + 1)
                if G >= 1:
                    v.wait_ge(evp, G)
                vb = 3 * G
                v.tensor_tensor(rt1[:], p[:], c1s[:, sl], ALU.mult).then_inc(vvp, 1)
                v.tensor_tensor(rt2[0:64, :], p[64:128, :], s1s[0:64, sl], ALU.mult).then_inc(vvp, 1)
                v.tensor_tensor(rt2[64:128, :], p[0:64, :], s1s[64:128, sl], ALU.mult).then_inc(vvp, 1)
                v.wait_ge(vvp, vb + 3)
                v.tensor_tensor(q1T[:, pi, sl], rt1[:], rt2[:], ALU.add).then_inc(evp, 1)
            if stage < 2:
                return
            for g in range(nkv):
                for kt in range(32):
                    n_t = 32 * g + kt
                    v.wait_ge(tpf, n_t + 1)
                    if kt == 0 and g >= 1:
                        v.wait_ge(mmf, 256 * g)
                    v.tensor_copy(kT1[:, kt, :], ptp[:]).then_inc(tcf, 1)
                for half in range(2):
                    v.wait_ge(vfp[half], 16 * (g + 1))
                    if g >= 1:
                        v.wait_ge(avf, 256 * g)
                    v.tensor_copy(vgB[:, half, :, 0:128],
                                  vstage[:].rearrange("p (a b) -> p a b", a=16)).then_inc(vcp, 1)
                for gidx in range(8 * g, 8 * (g + 1)):
                    v.wait_ge(avf, 32 * (gidx + 1))
                    if gidx >= 1:
                        v.wait_ge(cte, gidx)
                    for qs4 in range(4):
                        v.reciprocal(rcp[:, qs4:qs4 + 1], ctp[qs4][:, 128:129]).then_inc(vrc, 1)
    return nc


def build_final():
    nc = bass.Bass("TRN2", target_bir_lowering=False)
    XNT = nc.dram_tensor("xn1T", [D, T], BF16, kind="ExternalInput").ap()
    CXT = nc.dram_tensor("cx1T", [D, T], BF16, kind="ExternalInput").ap()
    X1T = nc.dram_tensor("x1T", [D, T], F32, kind="ExternalInput").ap()
    WOS = nc.dram_tensor("wos", [32, D, 128], F32, kind="ExternalInput").ap()
    WBS = nc.dram_tensor("wbs", [32, D, 128], F32, kind="ExternalInput").ap()
    BB = nc.dram_tensor("bb", [128, 32], F32, kind="ExternalInput").ap()
    WG8 = nc.dram_tensor("wg8", [D, 8], F32, kind="ExternalInput").ap()
    AT = nc.dram_tensor("at", [D, 128], F32, kind="ExternalInput").ap()
    BT8 = nc.dram_tensor("bt8", [128, D], F32, kind="ExternalInput").ap()
    E16 = nc.dram_tensor("e16", [8, 128], F32, kind="ExternalInput").ap()
    EYE = nc.dram_tensor("eye", [128, 128], F32, kind="ExternalInput").ap()
    OUT = nc.dram_tensor("outT", [D, T], F32, kind="ExternalOutput").ap()

    with ExitStack() as st:
        sb = lambda *a: st.enter_context(nc.sbuf_tensor(*a))
        ps = lambda *a: st.enter_context(nc.psum_tensor(*a))
        xnT = sb("xnTs", [128, 32, T], BF16)
        cxT = sb("cxTs", [128, 32, T], BF16)
        wo = [sb(f"wo{i}", [128, 32, 128], BF16) for i in range(2)]
        wb = [sb("wb0", [128, 32, 128], BF16)]
        x1t = [sb(f"x1t{i}", [128, 512], F32) for i in range(2)]
        bb = sb("bbs", [128, 32], F32)
        wg8 = sb("wg8s", [128, 32, 8], BF16)
        at = sb("ats", [128, 32, 128], BF16)
        bt8 = sb("bt8s", [128, D], BF16)
        e16 = sb("e16s", [8, 128], BF16)
        eyeb = sb("eyeb", [128, 128], BF16)
        eyef = sb("eyef", [8, 8], F32)
        lg = sb("lg", [8, T], F32)
        lgT = sb("lgT", [128, 8, 8], F32)
        mc = sb("mc", [128, 8], F32)
        msk1 = sb("msk1", [128, 8], F32)
        msk2 = sb("msk2", [128, 8], F32)
        l2c = sb("l2c", [128, 8], F32)
        r1 = sb("r1", [128, 8], F32)
        r2 = sb("r2", [128, 8], F32)
        routeb = sb("routeb", [128, 8], BF16)
        routeT = sb("routeT", [8, T], BF16)
        routeX = sb("routeX", [128, T], BF16)
        hw = sb("hws", [128, T], BF16)
        g1 = [sb(f"g1{i}", [128, 512], F32) for i in range(2)]
        s1 = [sb(f"s1{i}", [128, 512], F32) for i in range(2)]
        tt2 = sb("tt2", [128, 512], F32)
        ob = [sb(f"ob{i}", [128, 512], F32) for i in range(2)]
        psL = ps("psL", [8, T], F32)
        psT = ps("psT", [128, 8], F32)
        psTb = ps("psTb", [8, 128], BF16)
        psRX = ps("psRX", [128, 512], F32)
        psH = ps("psH", [128, 512], F32)
        psR2 = ps("psR2", [128, 512], F32)
        psS3 = ps("psS3", [128, 512], F32)
        lds = st.enter_context(nc.semaphore("lds"))
        lgf = st.enter_context(nc.semaphore("lgf"))
        lwb = st.enter_context(nc.semaphore("lwb"))
        lwo = [st.enter_context(nc.semaphore(f"lwo{i}")) for i in range(2)]
        lx1 = [st.enter_context(nc.semaphore(f"lx1{i}")) for i in range(2)]
        sow2 = [st.enter_context(nc.semaphore(f"sow2{i}")) for i in range(2)]
        mml = st.enter_context(nc.semaphore("mml"))
        vga = st.enter_context(nc.semaphore("vga"))
        sga = st.enter_context(nc.semaphore("sga"))
        tra = st.enter_context(nc.semaphore("tra"))
        mmx = st.enter_context(nc.semaphore("mmx"))
        mmh = st.enter_context(nc.semaphore("mmh"))
        vhw = st.enter_context(nc.semaphore("vhw"))
        mmr = st.enter_context(nc.semaphore("mmr"))
        mmo = st.enter_context(nc.semaphore("mmo"))
        sgl = st.enter_context(nc.semaphore("sgl"))
        ss1 = st.enter_context(nc.semaphore("ss1"))
        vvo = st.enter_context(nc.semaphore("vvo"))
        so = st.enter_context(nc.semaphore("so"))
        block = st.enter_context(nc.Block())

        VGA_CH = lambda tc: 9 + 9 * tc

        @block.sync
        def _(sy):
            sy.dma_start(out=xnT[:], in_=XNT.rearrange("(j p) t -> p j t", p=128)).then_inc(lds, 16)
            sy.dma_start(out=cxT[:], in_=CXT.rearrange("(j p) t -> p j t", p=128)).then_inc(lds, 16)
            sy.dma_start(out=bb[:], in_=BB).then_inc(lds, 16)
            sy.dma_start(out=eyef[:], in_=EYE[0:8, 0:8]).then_inc(lds, 16)
            for u in range(64):
                ot, th = divmod(u, 2)
                if u >= 2:
                    sy.wait_ge(vvo, 2 * (u - 2) + 2)
                sy.dma_start(out=x1t[u % 2][:], in_=X1T[128 * ot:128 * (ot + 1), 512 * th:512 * (th + 1)]).then_inc(lx1[u % 2], 16)
                if u >= 2:
                    u2 = u - 2
                    ot2, th2 = divmod(u2, 2)
                    sy.wait_ge(vvo, 2 * u2 + 2)
                    sy.dma_start(out=OUT[128 * ot2:128 * (ot2 + 1), 512 * th2:512 * (th2 + 1)],
                                 in_=ob[u2 % 2][:]).then_inc(sow2[u2 % 2], 16)
            for u2 in [62, 63]:
                ot2, th2 = divmod(u2, 2)
                sy.wait_ge(vvo, 2 * u2 + 2)
                sy.dma_start(out=OUT[128 * ot2:128 * (ot2 + 1), 512 * th2:512 * (th2 + 1)],
                             in_=ob[u2 % 2][:]).then_inc(sow2[u2 % 2], 16)
            for i in range(2):
                sy.wait_ge(sow2[i], 16 * 32)

        @block.gpsimd
        def _(g):
            g.dma_start(out=wg8[:], in_=WG8.rearrange("(j p) e -> p j e", p=128)).then_inc(lgf, 16)
            g.dma_start(out=at[:], in_=AT.rearrange("(j p) n -> p j n", p=128)).then_inc(lgf, 16)
            g.dma_start(out=bt8[:], in_=BT8).then_inc(lgf, 16)
            g.dma_start(out=e16[:], in_=E16).then_inc(lgf, 16)
            g.dma_start(out=eyeb[:], in_=EYE).then_inc(lgf, 16)
            for ot in range(32):
                if ot >= 1:
                    g.wait_ge(mmr, 2 * ot)
                g.dma_start(out=wb[0][:], in_=WBS[ot].rearrange("(j p) n -> p j n", p=128)).then_inc(lwb, 16)
                if ot >= 2:
                    g.wait_ge(mmo, 2 * ot - 2)
                g.dma_start(out=wo[ot % 2][:], in_=WOS[ot].rearrange("(j p) n -> p j n", p=128)).then_inc(lwo[ot % 2], 16)

        @block.tensor
        def _(t):
            t.wait_ge(lds, 64)
            t.wait_ge(lgf, 16 * 5)
            for th in range(2):
                for j in range(32):
                    inst = t.matmul(psL[:, 512 * th:512 * (th + 1)], wg8[:, j, :],
                                    xnT[:, j, 512 * th:512 * (th + 1)],
                                    start=(j == 0), stop=(j == 31))
                inst.then_inc(mml, 1)
            for tc in range(8):
                t.wait_ge(vga, 1)
                if tc >= 1:
                    t.wait_ge(vga, 1 + tc)
                t.transpose(psT[:, 0:8], lg[:, 128 * tc:128 * (tc + 1)], eyef[:]).then_inc(tra, 1)
            for tc in range(8):
                t.wait_ge(vga, VGA_CH(tc) + 8)
                if tc >= 1:
                    t.wait_ge(vga, VGA_CH(tc - 1) + 9)
                t.transpose(psTb[:], routeb[:], eyeb[:]).then_inc(tra, 1)
            for th in range(2):
                t.wait_ge(vga, 81 + th)
                t.matmul(psRX[:], e16[:], routeT[:, 512 * th:512 * (th + 1)],
                         start=True, stop=True).then_inc(mmx, 1)
            for th in range(2):
                t.wait_ge(vga, 83)
                if th >= 1:
                    t.wait_ge(vhw, th)
                for j in range(32):
                    inst = t.matmul(psH[:], at[:, j, :], xnT[:, j, 512 * th:512 * (th + 1)],
                                    start=(j == 0), stop=(j == 31))
                inst.then_inc(mmh, 1)
            t.wait_ge(vhw, 2)
            for ot in range(32):
                t.wait_ge(lwb, 16 * (ot + 1))
                for th in range(2):
                    u = 2 * ot + th
                    if u >= 1:
                        t.wait_ge(sgl, u)
                    for j in range(32):
                        t.matmul(psR2[:], wb[0][:, j, :], xnT[:, j, 512 * th:512 * (th + 1)],
                                 start=(j == 0), stop=False)
                    inst = t.matmul(psR2[:], bt8[:, 128 * ot:128 * (ot + 1)],
                                    hw[:, 512 * th:512 * (th + 1)], start=False, stop=True)
                    inst.then_inc(mmr, 1)
                t.wait_ge(lwo[ot % 2], 16 * (ot // 2 + 1))
                for th in range(2):
                    u = 2 * ot + th
                    if u >= 1:
                        t.wait_ge(ss1, u)
                    for j in range(32):
                        inst = t.matmul(psS3[:], wo[ot % 2][:, j, :], cxT[:, j, 512 * th:512 * (th + 1)],
                                        start=(j == 0), stop=(j == 31))
                    inst.then_inc(mmo, 1)

        @block.scalar
        def _(s):
            for tc in range(8):
                vb = VGA_CH(tc)
                s.wait_ge(vga, vb + 6)
                s.activation(mc[:, 3:4], mc[:, 2:3], AF.Sigmoid).then_inc(sga, 1)
                s.wait_ge(sga, 3 * tc + 1)
                s.activation(r1[:], msk1[:], AF.Copy, scale=mc[:, 3:4]).then_inc(sga, 1)
                s.wait_ge(vga, vb + 7)
                s.activation(r2[:], msk2[:], AF.Copy, scale=mc[:, 4:5]).then_inc(sga, 1)
            for ot in range(32):
                for th in range(2):
                    u = 2 * ot + th
                    s.wait_ge(mmr, u + 1)
                    if u >= 2:
                        s.wait_ge(vvo, 2 * (u - 2) + 1)
                    s.activation(g1[th][:], psR2[:], AF.Silu, bias=bb[:, ot:ot + 1]).then_inc(sgl, 1)
                    s.wait_ge(mmo, u + 1)
                    if u >= 2:
                        s.wait_ge(vvo, 2 * (u - 2) + 1)
                    s.activation(s1[th][:], psS3[:], AF.Copy).then_inc(ss1, 1)

        @block.vector
        def _(v):
            v.wait_ge(mml, 2)
            v.tensor_copy(lg[:], psL[:]).then_inc(vga, 1)
            for tc in range(8):
                v.wait_ge(tra, tc + 1)
                v.tensor_copy(lgT[:, tc, :], psT[:, 0:8]).then_inc(vga, 1)
            for tc in range(8):
                vb = VGA_CH(tc)
                x = lgT[:, tc, :]
                if tc >= 1:
                    v.wait_ge(sga, 3 * tc)
                v.reduce_max(mc[:, 0:1], x, axis=AX.X).then_inc(vga, 1)
                v.wait_ge(vga, vb + 1)
                v.tensor_scalar(msk1[:], x, mc[:, 0:1], None, ALU.is_ge).then_inc(vga, 1)
                v.wait_ge(vga, vb + 2)
                v.scalar_tensor_tensor(l2c[:], msk1[:], NEG, x, ALU.mult, ALU.add).then_inc(vga, 1)
                v.wait_ge(vga, vb + 3)
                v.reduce_max(mc[:, 1:2], l2c[:], axis=AX.X).then_inc(vga, 1)
                v.wait_ge(vga, vb + 4)
                v.tensor_scalar(msk2[:], l2c[:], mc[:, 1:2], None, ALU.is_ge).then_inc(vga, 1)
                v.tensor_tensor(mc[:, 2:3], mc[:, 0:1], mc[:, 1:2], ALU.subtract).then_inc(vga, 1)
                v.wait_ge(sga, 3 * tc + 1)
                v.tensor_scalar(mc[:, 4:5], mc[:, 3:4], -1.0, 1.0, ALU.mult, ALU.add).then_inc(vga, 1)
                v.wait_ge(sga, 3 * tc + 3)
                if tc >= 1:
                    v.wait_ge(tra, 8 + tc)
                v.tensor_tensor(routeb[:], r1[:], r2[:], ALU.add).then_inc(vga, 1)
                v.wait_ge(tra, 8 + tc + 1)
                v.tensor_copy(routeT[:, 128 * tc:128 * (tc + 1)], psTb[:]).then_inc(vga, 1)
            for th in range(2):
                v.wait_ge(mmx, th + 1)
                v.tensor_copy(routeX[:, 512 * th:512 * (th + 1)], psRX[:]).then_inc(vga, 1)
            for th in range(2):
                v.wait_ge(mmh, th + 1)
                v.tensor_tensor(hw[:, 512 * th:512 * (th + 1)], psH[:],
                                routeX[:, 512 * th:512 * (th + 1)], ALU.mult).then_inc(vhw, 1)
            for u in range(64):
                ot, th = divmod(u, 2)
                v.wait_ge(sgl, u + 1)
                v.wait_ge(ss1, u + 1)
                v.wait_ge(lx1[u % 2], 16 * (u // 2 + 1))
                if u >= 1:
                    v.wait_ge(vvo, 2 * u)
                v.tensor_tensor(tt2[:], g1[th][:], s1[th][:], ALU.mult).then_inc(vvo, 1)
                v.wait_ge(vvo, 2 * u + 1)
                if u >= 2:
                    v.wait_ge(sow2[u % 2], 16 * (u // 2))
                v.tensor_tensor(ob[u % 2][:], tt2[:], x1t[u % 2][:],
                                ALU.add).then_inc(vvo, 1)
    return nc


def kernel_unfused(L0_attn_x, L0_attn_wq, L0_attn_wk, L0_attn_wv, L0_attn_wo,
           L0_ffn_gate_qweight, L0_ffn_gate_scales, L0_ffn_gate_qzeros,
           L0_ffn_up_qweight, L0_ffn_up_scales, L0_ffn_up_qzeros,
           L0_ffn_down_qweight, L0_ffn_down_scales, L0_ffn_down_qzeros,
           L1_attn_k_cache, L1_attn_v_cache, L1_attn_Wq, L1_attn_Wo,
           L1_attn_block_table, L1_ffn_W_base, L1_ffn_b_base, L1_ffn_W_gate,
           L1_ffn_A, L1_ffn_B):
    f32 = np.float32
    cores = list(range(NCORES))
    x2 = np.asarray(L0_attn_x, dtype=f32).reshape(8192, D)
    eye = np.eye(128, dtype=f32)

    mapsA = host_prep0_inputs(x2, np.asarray(L0_attn_wq, f32), np.asarray(L0_attn_wk, f32),
                              np.asarray(L0_attn_wv, f32))
    resA = run_bass_kernel_spmd(build_prep0(), mapsA, core_ids=cores).results
    xnT_c = [np.asarray(resA[c]["xnT"]) for c in range(NCORES)]

    mapsB = host_attn0_inputs([np.asarray(resA[c]["qT"]) for c in range(NCORES)],
                              [np.asarray(resA[c]["kT"]) for c in range(NCORES)],
                              [np.asarray(resA[c]["v"]) for c in range(NCORES)])
    resB = run_bass_kernel_spmd(build_attn0(), mapsB, core_ids=cores).results
    ctx_b = host_attn0_ctx(resB)

    mapsC, lp_gate, lp_down = host_dequant_inputs(
        np.asarray(L0_ffn_gate_qweight), np.asarray(L0_ffn_gate_scales, f32), np.asarray(L0_ffn_gate_qzeros),
        np.asarray(L0_ffn_up_qweight), np.asarray(L0_ffn_up_scales, f32), np.asarray(L0_ffn_up_qzeros),
        np.asarray(L0_ffn_down_qweight), np.asarray(L0_ffn_down_scales, f32), np.asarray(L0_ffn_down_qzeros))
    resC = run_bass_kernel_spmd(build_dequant(), mapsC, core_ids=cores).results
    Wg_full = np.concatenate([np.asarray(resC[c]["Wg"]) for c in range(NCORES)], axis=1)
    Wu_full = np.concatenate([np.asarray(resC[c]["Wu"]) for c in range(NCORES)], axis=1)
    Wd_full = np.concatenate([np.asarray(resC[c]["Wd"]) for c in range(NCORES)], axis=1)

    wgs = cc(Wg_full.reshape(D, 86, 128).transpose(1, 0, 2))
    wus = cc(Wu_full.reshape(D, 86, 128).transpose(1, 0, 2))
    mapsD = [dict(xnT=xnT_c[c], wgs=wgs, wus=wus) for c in range(NCORES)]
    resD = run_bass_kernel_spmd(build_ffn0(), mapsD, core_ids=cores).results

    wds = cc(Wd_full.reshape(11008, 32, 128).transpose(1, 0, 2))
    wo_perm = np.asarray(L0_attn_wo, f32)[:, lp_down]
    wosE = cc(wo_perm.reshape(D, 32, 128).transpose(1, 0, 2))
    one_r = np.ones((1, 128), f32)
    one_c = np.ones((128, 1), f32)
    mapsE = []
    for c in range(NCORES):
        b, m = c // 4, c % 4
        ctxT = cc(ctx_b[b][1024 * m:1024 * (m + 1), :].T)
        xTp = cc(x2[1024 * c:1024 * (c + 1)].T[lp_down])
        mapsE.append(dict(hT=np.asarray(resD[c]["hT"]), wds=wds, cxT=ctxT,
                          wos=wosE, xTp=xTp, one=one_r, onec=one_c))
    resE = run_bass_kernel_spmd(build_downwo(), mapsE, core_ids=cores).results
    inv_perm = np.argsort(lp_down)
    x1T_c = [cc(np.asarray(resE[c]["x1T"])[inv_perm]) for c in range(NCORES)]
    xn1T_c = [cc(np.asarray(resE[c]["xn1T"])[inv_perm]) for c in range(NCORES)]

    wqsF = cc(np.asarray(L1_attn_Wq, f32).T.reshape(D, 32, 128).transpose(1, 0, 2))
    kcF = np.asarray(L1_attn_k_cache, f32)
    vcF = np.asarray(L1_attn_v_cache, f32)
    bti = cc(np.asarray(L1_attn_block_table, np.int32).reshape(2, 128).T.copy())
    inv = 10000.0 ** (-np.arange(64) / 64.0)
    sc128 = f32(1.0 / np.sqrt(HD))
    mapsF = []
    for c in range(NCORES):
        pos = np.arange(1024 * (c % 4), 1024 * (c % 4) + 1024, dtype=np.float64)
        ang = pos[:, None] * inv[None, :]
        Ct = cc(np.concatenate([np.cos(ang), np.cos(ang)], 1).T.astype(f32)) * sc128
        St = cc(np.concatenate([-np.sin(ang), np.sin(ang)], 1).T.astype(f32)) * sc128
        mapsF.append(dict(xn1T=xn1T_c[c], wqs=wqsF, c1=Ct, s1=St,
                          kc=kcF, vc=vcF, bti=bti, eye=eye))
    resF = run_bass_kernel_spmd(build_attn1(), mapsF, core_ids=cores).results

    wosG = cc(np.asarray(L1_attn_Wo, f32).T.reshape(D, 32, 128).transpose(1, 0, 2))
    wbsG = cc(np.asarray(L1_ffn_W_base, f32).T.reshape(D, 32, 128).transpose(1, 0, 2))
    bbG = cc(np.asarray(L1_ffn_b_base, f32).reshape(32, 128).T)
    wg8G = cc(np.asarray(L1_ffn_W_gate, f32).T)
    atG = cc(np.asarray(L1_ffn_A, f32).transpose(2, 0, 1).reshape(D, 128))
    bt8G = cc(np.asarray(L1_ffn_B, f32).transpose(0, 2, 1).reshape(128, D))
    e16 = np.zeros((8, 128), f32)
    for e in range(8):
        e16[e, 16 * e:16 * (e + 1)] = 1.0
    mapsG = []
    for c in range(NCORES):
        arr = np.asarray(resF[c]["ctx1"])
        a6 = arr.reshape(8, 128, 4, 2, 4, 128)
        ctx1 = a6.transpose(3, 4, 1, 0, 2, 5).reshape(1024, D)
        mapsG.append(dict(xn1T=xn1T_c[c], cx1T=cc(ctx1.T), x1T=x1T_c[c],
                          wos=wosG, wbs=wbsG, bb=bbG, wg8=wg8G, at=atG,
                          bt8=bt8G, e16=e16, eye=eye))
    resG = run_bass_kernel_spmd(build_final(), mapsG, core_ids=cores).results

    out = np.empty((8192, D), dtype=f32)
    for c in range(NCORES):
        out[1024 * c:1024 * (c + 1)] = np.asarray(resG[c]["outT"]).T
    return out.reshape(2, 4096, 4096)



def build_mega(upto=99, dbg=()):
    nc = bass.Bass("TRN2", target_bir_lowering=False)
    _dbgd = {}
    def _mark_dbg(ap, nm):
        _dbgd[nm] = ap
        return ap
    XRO = nc.dram_tensor("xro", [S, D], F32, kind="ExternalInput").ap()
    XTP = nc.dram_tensor("xTp", [D, T], F32, kind="ExternalInput").ap()
    WQ = nc.dram_tensor("wq_t", [NH, D, HD], F32, kind="ExternalInput").ap()
    WK = nc.dram_tensor("wk_t", [NKV, D, HD], F32, kind="ExternalInput").ap()
    WV = nc.dram_tensor("wv", [D, KVD], F32, kind="ExternalInput").ap()
    CQ1 = nc.dram_tensor("cq1", [HD, T], F32, kind="ExternalInput").ap()
    SQ1 = nc.dram_tensor("sq1", [HD, T], F32, kind="ExternalInput").ap()
    CKF = nc.dram_tensor("ckf", [HD, S], F32, kind="ExternalInput").ap()
    SKF = nc.dram_tensor("skf", [HD, S], F32, kind="ExternalInput").ap()
    BIA = nc.dram_tensor("bia", [128, 64], F32, kind="ExternalInput").ap()
    M4 = nc.dram_tensor("m4", [128, 4, 512], F32, kind="ExternalInput").ap()
    EYE = nc.dram_tensor("eye", [128, 128], F32, kind="ExternalInput").ap()
    ONE = nc.dram_tensor("one", [1, 128], F32, kind="ExternalInput").ap()
    ONEC = nc.dram_tensor("onec", [128, 1], F32, kind="ExternalInput").ap()
    WOP = nc.dram_tensor("wop", [32, D, 128], F32, kind="ExternalInput").ap()
    GQ = nc.dram_tensor("gq", [D, 1376], I32, kind="ExternalInput").ap()
    GS = nc.dram_tensor("gs", [32, 11008], F32, kind="ExternalInput").ap()
    GZ = nc.dram_tensor("gz", [32, 1376], I32, kind="ExternalInput").ap()
    UQ = nc.dram_tensor("uq", [D, 1376], I32, kind="ExternalInput").ap()
    US = nc.dram_tensor("us", [32, 11008], F32, kind="ExternalInput").ap()
    UZ = nc.dram_tensor("uz", [32, 1376], I32, kind="ExternalInput").ap()
    DQ = nc.dram_tensor("dq", [11008, 512], I32, kind="ExternalInput").ap()
    DS = nc.dram_tensor("ds", [86, D], F32, kind="ExternalInput").ap()
    DZ = nc.dram_tensor("dz", [86, 512], I32, kind="ExternalInput").ap()
    WQ1 = nc.dram_tensor("wq1", [32, D, 128], F32, kind="ExternalInput").ap()
    C1T = nc.dram_tensor("c1t", [HD, T], F32, kind="ExternalInput").ap()
    S1T = nc.dram_tensor("s1t", [HD, T], F32, kind="ExternalInput").ap()
    KC = nc.dram_tensor("kc", [512, NKV, 16, HD], F32, kind="ExternalInput").ap()
    VC = nc.dram_tensor("vc", [512, NKV, 16, HD], F32, kind="ExternalInput").ap()
    BTI = nc.dram_tensor("bti", [128, 2], I32, kind="ExternalInput").ap()
    WO1 = nc.dram_tensor("wo1", [32, D, 128], F32, kind="ExternalInput").ap()
    WB1 = nc.dram_tensor("wb1", [32, D, 128], F32, kind="ExternalInput").ap()
    BB = nc.dram_tensor("bb", [128, 32], F32, kind="ExternalInput").ap()
    WG8 = nc.dram_tensor("wg8", [D, 8], F32, kind="ExternalInput").ap()
    AT = nc.dram_tensor("at", [D, 128], F32, kind="ExternalInput").ap()
    BT8 = nc.dram_tensor("bt8", [128, D], F32, kind="ExternalInput").ap()
    E16 = nc.dram_tensor("e16", [8, 128], F32, kind="ExternalInput").ap()
    CSEL = nc.dram_tensor("csel", [128, 256], BF16, kind="ExternalInput").ap()
    OUT = nc.dram_tensor("outT", [D, T], F32, kind="ExternalOutput").ap()
    def scr(nm, shape, dt):
        kw = dict(kind="ExternalOutput") if nm in dbg else {}
        return nc.dram_tensor(nm, shape, dt, **kw).ap()
    XNS = scr("xnT_s", [D, S], BF16)
    KTS = scr("kT_s", [KVD, S], BF16)
    VS = scr("v_s", [S, KVD], BF16)
    QTS = scr("qT_s", [D, T], BF16)
    WGSc = scr("Wg_s", [D, 11008], BF16)
    WUSc = scr("Wu_s", [D, 11008], BF16)
    WDSc = scr("Wd_s", [11008, D], BF16)
    HTS = scr("hT_s", [11008, T], BF16)
    X1S = scr("x1T_s", [D, T], F32)
    XN1S = scr("xn1T_s", [D, T], BF16)
    CX1S = scr("cx1T_s", [D, T], BF16)
    ZSG = scr("zsg_s", [32, 11008], BF16)
    ZSU = scr("zsu_s", [32, 11008], BF16)
    ZSD = scr("zsd_s", [86, 4096], BF16)
    CXS = scr("cx_s", [T, D], BF16)
    CTS = scr("ctxT_s", [D, T], BF16)

    outer = ExitStack()
    with outer:
        osb = lambda *a: outer.enter_context(nc.sbuf_tensor(*a))
        _cst = [outer]
        sem = lambda n: _cst[0].enter_context(nc.semaphore(n))
        sem2 = lambda n: [_cst[0].enter_context(nc.semaphore(f"{n}{i}")) for i in range(2)]

        def _phreset():
            nc.reset(previous_sbuf_base=nc.sbuf_base, previous_sbuf_top=nc.sbuf_top,
                     previous_psum_base=nc.psum_base, previous_psum_top=nc.psum_top,
                     previous_local_dram_base=nc.local_dram_base,
                     previous_shared_dram_base=nc.shared_dram_base)

        eyeb = osb("eyeb", [128, 128], BF16)
        oner = osb("oner", [1, 128], F32)
        onecs = osb("onecs", [128, 1], BF16)

        lcn = nc.alloc_semaphore(name="lcn")
        lcg = nc.alloc_semaphore(name="lcg")
        with nc.named_scope("C0"), nc.Block() as b:
            @b.gpsimd
            def _(g):
                g.dma_start(out=eyeb[:], in_=EYE).then_inc(lcg, 16)
                g.dma_start(out=onecs[:], in_=ONEC).then_inc(lcg, 16)
            @b.sync
            def _(sy):
                sy.dma_start(out=oner[:], in_=ONE).then_inc(lcn, 16)
                sy.wait_ge(lcn, 16)
                sy.wait_ge(lcg, 32)

        with ExitStack() as st0:
            sb = lambda *a: st0.enter_context(nc.sbuf_tensor(*a))
            _cst[0] = st0
            xt = [sb(f"p0xt{i}", [128, D], F32) for i in range(2)]
            junk = sb("p0junk", [128, D], BF16)
            stat = sb("p0stat", [128, 8], F32)
            xn = sb("p0xn", [128, D], BF16)
            xstg = [sb(f"p0stg{i}", [128, 32, 128], BF16) for i in range(2)]
            ptp = [st0.enter_context(nc.psum_tensor(f"p0ptp{i}", [128, 128], BF16)) for i in range(2)]
            lx = sem2("p0lx")
            s_ln = sem("p0s")
            v_ln = sem("p0v")
            tp = sem("p0tp")
            tc = sem("p0tc")
            stg = sem2("p0st")
            with nc.named_scope("P0"), nc.Block() as b:
                @b.sync
                def _(sy):
                    for i in range(32):
                        if i >= 2:
                            sy.wait_ge(s_ln, 4 * i - 4)
                            sy.wait_ge(v_ln, 7 * i - 13)
                        sy.dma_start(out=xt[i % 2][:], in_=XRO[128 * i:128 * (i + 1), :]).then_inc(lx[i % 2], 16)
                        if i >= 2:
                            sy.wait_ge(tc, 32 * (i - 1))
                            sy.dma_start(out=XNS.rearrange("(j p) t -> p j t", p=128)[:, :, 128 * (i - 2):128 * (i - 1)],
                                         in_=xstg[i % 2][:]).then_inc(stg[i % 2], 16)
                    for i in [30, 31]:
                        sy.wait_ge(tc, 32 * (i + 1))
                        sy.dma_start(out=XNS.rearrange("(j p) t -> p j t", p=128)[:, :, 128 * i:128 * (i + 1)],
                                     in_=xstg[i % 2][:]).then_inc(stg[i % 2], 16)
                    for par in range(2):
                        sy.wait_ge(stg[par], 16 * 16)
                @b.scalar
                def _(s):
                    for i in range(32):
                        s.wait_ge(lx[i % 2], 16 * (i // 2 + 1))
                        s.activation(junk[:], xt[i % 2][:], AF.Square,
                                     accum_out=stat[:, 1:2]).then_inc(s_ln, 1)
                        s.wait_ge(v_ln, 7 * i + 5)
                        s.activation(stat[:, 3:4], stat[:, 4:5], AF.Ln).then_inc(s_ln, 1)
                        s.wait_ge(v_ln, 7 * i + 6)
                        s.activation(stat[:, 5:6], stat[:, 4:5], AF.Exp).then_inc(s_ln, 1)
                        s.wait_ge(v_ln, 7 * i + 7)
                        s.wait_ge(s_ln, 4 * i + 3)
                        if i > 0:
                            s.wait_ge(tp, 32 * i)
                        s.activation(xn[:], xt[i % 2][:], AF.Identity, scale=stat[:, 5:6],
                                     bias=stat[:, 6:7]).then_inc(s_ln, 1)
                @b.vector
                def _(v):
                    for i in range(32):
                        v.wait_ge(s_ln, 4 * i + 1)
                        v.wait_ge(lx[i % 2], 16 * (i // 2 + 1))
                        v.reduce_sum(stat[:, 0:1], xt[i % 2][:], axis=AX.X).then_inc(v_ln, 1)
                        v.wait_ge(v_ln, 7 * i + 1)
                        v.tensor_scalar(stat[:, 2:3], stat[:, 0:1], 1.0 / D, None, ALU.mult).then_inc(v_ln, 1)
                        v.wait_ge(v_ln, 7 * i + 2)
                        v.tensor_tensor(stat[:, 7:8], stat[:, 2:3], stat[:, 2:3], ALU.mult).then_inc(v_ln, 1)
                        v.wait_ge(v_ln, 7 * i + 3)
                        v.scalar_tensor_tensor(stat[:, 3:4], stat[:, 1:2], 1.0 / D, stat[:, 7:8],
                                               ALU.mult, ALU.subtract).then_inc(v_ln, 1)
                        v.wait_ge(v_ln, 7 * i + 4)
                        v.tensor_scalar(stat[:, 4:5], stat[:, 3:4], EPS, None, ALU.add).then_inc(v_ln, 1)
                        v.wait_ge(s_ln, 4 * i + 2)
                        v.tensor_scalar(stat[:, 4:5], stat[:, 3:4], -0.5, None, ALU.mult).then_inc(v_ln, 1)
                        v.wait_ge(s_ln, 4 * i + 3)
                        v.scalar_tensor_tensor(stat[:, 6:7], stat[:, 2:3], -1.0, stat[:, 5:6],
                                               ALU.mult, ALU.mult).then_inc(v_ln, 1)
                        for j in range(32):
                            n = 32 * i + j
                            v.wait_ge(tp, n + 1)
                            if i >= 2 and j == 0:
                                v.wait_ge(stg[i % 2], 16 * (i // 2))
                            v.tensor_copy(xstg[i % 2][:, j, :], ptp[n % 2][:]).then_inc(tc, 1)
                @b.tensor
                def _(t):
                    for i in range(32):
                        t.wait_ge(s_ln, 4 * i + 4)
                        for j in range(32):
                            n = 32 * i + j
                            if n >= 2:
                                t.wait_ge(tc, n - 1)
                            t.transpose(ptp[n % 2][:], xn[:, 128 * j:128 * (j + 1)], eyeb[:]).then_inc(tp, 1)

        if upto <= 1:
            return nc

        with ExitStack() as st1:
            sb = lambda *a: st1.enter_context(nc.sbuf_tensor(*a))
            _cst[0] = st1
            wks = sb("p1wks", [128, 8, 32, 128], BF16)
            xc = [sb(f"p1xc{i}", [128, 32, 512], BF16) for i in range(2)]
            ckfs = sb("ckfs", [128, S], F32)
            skfs = sb("skfs", [128, S], F32)
            rt1 = sb("p1rt1", [128, 512], F32)
            rt2 = sb("p1rt2", [128, 512], F32)
            ko = [sb(f"p1ko{i}", [128, 512], BF16) for i in range(2)]
            pk = [st1.enter_context(nc.psum_tensor(f"p1pk{i}", [128, 512], F32)) for i in range(2)]
            lwk = sem("p1lwk")
            ltb = sem("p1ltb")
            lxc = sem2("p1lxc")
            mmk = sem("p1mmk")
            vvk = sem("p1vvk")
            evk = sem("p1evk")
            ssk = sem2("p1ssk")
            with nc.named_scope("P1k"), nc.Block() as b:
                @b.gpsimd
                def _(g):
                    for gg in range(8):
                        g.dma_start(out=wks[:, gg, :, :],
                                    in_=WK[gg].rearrange("(j p) n -> p j n", p=128)).then_inc(lwk, 16)
                    for tcx in range(8):
                        if tcx >= 2:
                            g.wait_ge(mmk, 8 * (tcx - 1))
                        g.dma_start(out=xc[tcx % 2][:],
                                    in_=XNS.rearrange("(j p) t -> p j t", p=128)[:, :, 512 * tcx:512 * (tcx + 1)]).then_inc(lxc[tcx % 2], 16)
                @b.sync
                def _(sy):
                    sy.dma_start(out=ckfs[:], in_=CKF).then_inc(ltb, 16)
                    sy.dma_start(out=skfs[:], in_=SKF).then_inc(ltb, 16)
                    for G in range(64):
                        tcx, gg = G // 8, G % 8
                        sy.wait_ge(evk, G + 1)
                        sy.dma_start(out=KTS[128 * gg:128 * (gg + 1), 512 * tcx:512 * (tcx + 1)],
                                     in_=ko[G % 2][:]).then_inc(ssk[G % 2], 16)
                    for par in range(2):
                        sy.wait_ge(ssk[par], 16 * 32)
                @b.tensor
                def _(t):
                    t.wait_ge(lwk, 16 * 8)
                    for G in range(64):
                        tcx, gg = G // 8, G % 8
                        if gg == 0:
                            t.wait_ge(lxc[tcx % 2], 16 * (tcx // 2 + 1))
                        if G >= 2:
                            t.wait_ge(evk, G - 1)
                        for j in range(32):
                            inst = t.matmul(pk[G % 2][:], wks[:, gg, j, :], xc[tcx % 2][:, j, :],
                                            start=(j == 0), stop=(j == 31))
                        inst.then_inc(mmk, 1)
                @b.vector
                def _(v):
                    v.wait_ge(ltb, 32)
                    for G in range(64):
                        tcx, gg = G // 8, G % 8
                        sl = slice(512 * tcx, 512 * (tcx + 1))
                        p = pk[G % 2]
                        v.wait_ge(mmk, G + 1)
                        if G >= 1:
                            v.wait_ge(evk, G)
                        if G >= 2:
                            v.wait_ge(ssk[G % 2], 16 * (G // 2))
                        vb = 3 * G
                        v.tensor_tensor(rt1[:], p[:], ckfs[:, sl], ALU.mult).then_inc(vvk, 1)
                        v.tensor_tensor(rt2[0:64, :], p[64:128, :], skfs[0:64, sl], ALU.mult).then_inc(vvk, 1)
                        v.tensor_tensor(rt2[64:128, :], p[0:64, :], skfs[64:128, sl], ALU.mult).then_inc(vvk, 1)
                        v.wait_ge(vvk, vb + 3)
                        v.tensor_tensor(ko[G % 2][:], rt1[:], rt2[:], ALU.add).then_inc(evk, 1)

        if upto <= 2:
            return nc

        with ExitStack() as st1v:
            sb = lambda *a: st1v.enter_context(nc.sbuf_tensor(*a))
            _cst[0] = st1v
            wvs = sb("p1wvs", [128, 32, 1024], BF16)
            xc = [sb(f"p1vxc{i}", [128, 32, 512], BF16) for i in range(2)]
            vo = [sb(f"p1vo{i}", [128, 512], BF16) for i in range(2)]
            pv = [st1v.enter_context(nc.psum_tensor(f"p1pv{i}", [128, 512], F32)) for i in range(2)]
            lwv = sem("p1lwv")
            lxv = sem2("p1lxv")
            mmv = sem("p1mmv")
            evv = sem("p1evv")
            ssv = sem2("p1ssv")
            with nc.named_scope("P1v"), nc.Block() as b:
                @b.gpsimd
                def _(g):
                    g.dma_start(out=wvs[:], in_=WV.rearrange("(j p) n -> p j n", p=128)).then_inc(lwv, 16)
                    for tcx in range(8):
                        if tcx >= 2:
                            g.wait_ge(mmv, 8 * (tcx - 1))
                        g.dma_start(out=xc[tcx % 2][:],
                                    in_=XNS.rearrange("(j p) t -> p j t", p=128)[:, :, 512 * tcx:512 * (tcx + 1)]).then_inc(lxv[tcx % 2], 16)
                @b.sync
                def _(sy):
                    for G in range(64):
                        tcx, tt, vh = G // 8, (G // 2) % 4, G % 2
                        sy.wait_ge(evv, G + 1)
                        sy.dma_start(out=VS[128 * (4 * tcx + tt):128 * (4 * tcx + tt + 1), 512 * vh:512 * (vh + 1)],
                                     in_=vo[G % 2][:]).then_inc(ssv[G % 2], 16)
                    for par in range(2):
                        sy.wait_ge(ssv[par], 16 * 32)
                @b.tensor
                def _(t):
                    t.wait_ge(lwv, 16)
                    for G in range(64):
                        tcx, tt, vh = G // 8, (G // 2) % 4, G % 2
                        if tt == 0 and vh == 0:
                            t.wait_ge(lxv[tcx % 2], 16 * (tcx // 2 + 1))
                        if G >= 2:
                            t.wait_ge(evv, G - 1)
                        for j in range(32):
                            inst = t.matmul(pv[G % 2][:], xc[tcx % 2][:, j, 128 * tt:128 * (tt + 1)],
                                            wvs[:, j, 512 * vh:512 * (vh + 1)],
                                            start=(j == 0), stop=(j == 31))
                        inst.then_inc(mmv, 1)
                @b.scalar
                def _(s):
                    for G in range(64):
                        s.wait_ge(mmv, G + 1)
                        if G >= 2:
                            s.wait_ge(ssv[G % 2], 16 * (G // 2))
                        s.activation(vo[G % 2][:], pv[G % 2][:], AF.Copy).then_inc(evv, 1)

        if upto <= 3:
            return nc

        with ExitStack() as st1q:
            sb = lambda *a: st1q.enter_context(nc.sbuf_tensor(*a))
            _cst[0] = st1q
            xq = sb("p1xq", [128, 32, 1024], BF16)
            wqb = [sb(f"p1wqb{i}", [128, 32, 128], BF16) for i in range(2)]
            cq1s = sb("cq1s", [128, T], F32)
            sq1s = sb("sq1s", [128, T], F32)
            rt1 = sb("p1qrt1", [128, 512], F32)
            rt2 = sb("p1qrt2", [128, 512], F32)
            qo = [sb(f"p1qo{i}", [128, 512], BF16) for i in range(2)]
            pq = [st1q.enter_context(nc.psum_tensor(f"p1pq{i}", [128, 512], F32)) for i in range(2)]
            lxq = sem("p1lxq")
            ltq = sem("p1ltq")
            lwq = sem2("p1lwq")
            mmq = sem("p1mmq")
            vvq = sem("p1vvq")
            evq = sem("p1evq")
            ssq = sem2("p1ssq")
            with nc.named_scope("P1q"), nc.Block() as b:
                @b.gpsimd
                def _(g):
                    for pi in range(32):
                        if pi >= 2:
                            g.wait_ge(mmq, 2 * pi - 2)
                        g.dma_start(out=wqb[pi % 2][:],
                                    in_=WQ[pi].rearrange("(j p) n -> p j n", p=128)).then_inc(lwq[pi % 2], 16)
                @b.sync
                def _(sy):
                    sy.dma_start(out=xq[:], in_=XNS.rearrange("(j p) t -> p j t", p=128)[:, :, 0:1024]).then_inc(lxq, 16)
                    sy.dma_start(out=cq1s[:], in_=CQ1).then_inc(ltq, 16)
                    sy.dma_start(out=sq1s[:], in_=SQ1).then_inc(ltq, 16)
                    for G in range(64):
                        pi, th = G // 2, G % 2
                        sy.wait_ge(evq, G + 1)
                        sy.dma_start(out=QTS[128 * pi:128 * (pi + 1), 512 * th:512 * (th + 1)],
                                     in_=qo[G % 2][:]).then_inc(ssq[G % 2], 16)
                    for par in range(2):
                        sy.wait_ge(ssq[par], 16 * 32)
                @b.tensor
                def _(t):
                    t.wait_ge(lxq, 16)
                    for G in range(64):
                        pi, th = G // 2, G % 2
                        if th == 0:
                            t.wait_ge(lwq[pi % 2], 16 * (pi // 2 + 1))
                        if G >= 2:
                            t.wait_ge(evq, G - 1)
                        for j in range(32):
                            inst = t.matmul(pq[G % 2][:], wqb[pi % 2][:, j, :],
                                            xq[:, j, 512 * th:512 * (th + 1)],
                                            start=(j == 0), stop=(j == 31))
                        inst.then_inc(mmq, 1)
                @b.vector
                def _(v):
                    v.wait_ge(ltq, 32)
                    for G in range(64):
                        pi, th = G // 2, G % 2
                        sl = slice(512 * th, 512 * (th + 1))
                        p = pq[G % 2]
                        v.wait_ge(mmq, G + 1)
                        if G >= 1:
                            v.wait_ge(evq, G)
                        if G >= 2:
                            v.wait_ge(ssq[G % 2], 16 * (G // 2))
                        vb = 3 * G
                        v.tensor_tensor(rt1[:], p[:], cq1s[:, sl], ALU.mult).then_inc(vvq, 1)
                        v.tensor_tensor(rt2[0:64, :], p[64:128, :], sq1s[0:64, sl], ALU.mult).then_inc(vvq, 1)
                        v.tensor_tensor(rt2[64:128, :], p[0:64, :], sq1s[64:128, sl], ALU.mult).then_inc(vvq, 1)
                        v.wait_ge(vvq, vb + 3)
                        v.tensor_tensor(qo[G % 2][:], rt1[:], rt2[:], ALU.add).then_inc(evq, 1)


        if upto <= 4:
            return nc
        _phreset()

        dqu = []
        for mi, (n_g, n_ch) in enumerate([(32, 11008), (32, 11008), (86, 4096)]):
            for g_ in range(n_g):
                off = 0
                while off < n_ch:
                    w = min(1024, n_ch - off)
                    dqu.append((mi, g_, off, w))
                    off += w
        NDQ = len(dqu)
        dq_sched = {}
        ui = 0
        for ag in range(64):
            take = (NDQ * (ag + 1)) // 64 - (NDQ * ag) // 64
            dq_sched[ag] = list(range(ui, ui + take))
            ui += take
        assert ui == NDQ
        dq_qw = [GQ, UQ, DQ]
        dq_ss = [GS, US, DS]
        dq_ws = [WGSc, WUSc, WDSc]

        with ExitStack() as st24:
            sb = lambda *a: st24.enter_context(nc.sbuf_tensor(*a))
            _cst[0] = st24
            ktg = [sb(f"p2kt{i}", [128, S], BF16) for i in range(2)]
            vg = [sb(f"p2vg{i}", [128, 32, 129], BF16) for i in range(2)]
            qth = [sb(f"p2qt{i}", [128, T], BF16) for i in range(2)]
            biat = sb("p2bia", [128, 64], F32)
            m4s = sb("p2m4", [128, 4, 512], F32)
            expb = [sb(f"p2ex{i}", [128, 512], BF16) for i in range(2)]
            sxp = [sb(f"p2sx{i}", [128, 512], F32) for i in range(2)]
            rcp = sb("p2rcp", [128, 4], F32)
            cbig = [sb(f"p2cb{i}", [128, 4, 128], BF16) for i in range(2)]
            onerb = sb("p4onerb", [1, 128], BF16)
            qwt = [sb(f"p4qw{i}", [128, 128], I32) for i in range(4)]
            sbr = [sb(f"p4sb{i}", [1, 1024], BF16) for i in range(2)]
            nibi = sb("p4nibi", [128, 1024], I32)
            wout = [sb(f"p4wo{i}", [128, 1024], BF16) for i in range(24)]
            scp = [st24.enter_context(nc.psum_tensor(f"p2scp{i}", [128, 512], F32)) for i in range(2)]
            ctp = [st24.enter_context(nc.psum_tensor(f"p2ctp{i}", [128, 129], F32)) for i in range(4)]
            psS = st24.enter_context(nc.psum_tensor("p4psS", [128, 1024], F32))
            l2k = sem2("p2lk")
            l2v = sem2("p2lv")
            l2q = sem2("p2lq")
            l2c = sem("p2lc")
            msv = sem("p2msv")
            msm = sem("p2msm")
            p2mm = sem("p2mm")
            p2vm = sem("p2vm")
            p2ex = sem("p2ex")
            p2av = sem("p2av")
            p2rc = sem("p2rc")
            p2ce = sem("p2ce")
            p2cs = sem2("p2cs")
            l4q = [_cst[0].enter_context(nc.semaphore(f"p4lq{i}")) for i in range(4)]
            l4s = sem2("p4ls")
            p4bc = sem("p4bc")
            p4un = sem("p4un")
            p4tt = sem("p4tt")
            p4st = [_cst[0].enter_context(nc.semaphore(f"p4st{i}")) for i in range(24)]

            def window(qc, kt):
                return 4 * qc <= kt < 4 * qc + 4

            wm_idx = {}
            wmc = 0
            for ag in range(64):
                qc = ag % 2
                for kt in range(32):
                    if window(qc, kt):
                        wm_idx[32 * ag + kt] = wmc
                        wmc += 1

            with nc.named_scope("P24"), nc.Block() as b:
                @b.sync
                def _(sy):
                    sy.dma_start(out=biat[:], in_=BIA).then_inc(l2c, 16)
                    sy.dma_start(out=m4s[:], in_=M4).then_inc(l2c, 16)
                    for ag in range(64):
                        g, qh, qc = ag // 8, (ag // 2) % 4, ag % 2
                        h = 4 * g + qh
                        if qh == 0 and qc == 0:
                            if g >= 2:
                                sy.wait_ge(p2av, 32 * 8 * (g - 1))
                            sy.dma_start(out=ktg[g % 2][:], in_=KTS[128 * g:128 * (g + 1), :]).then_inc(l2k[g % 2], 16)
                            sy.dma_start(out=vg[g % 2][:, :, 0:128],
                                         in_=VS.rearrange("(kt p) n -> p kt n", p=128)[:, :, 128 * g:128 * (g + 1)]).then_inc(l2v[g % 2], 16)
                        if qc == 0:
                            if h >= 2:
                                sy.wait_ge(p2av, 32 * 2 * (h - 1))
                            sy.dma_start(out=qth[h % 2][:], in_=QTS[128 * h:128 * (h + 1), :]).then_inc(l2q[h % 2], 16)
                        if ag >= 2:
                            ag2 = ag - 2
                            g2, qh2, qc2 = ag2 // 8, (ag2 // 2) % 4, ag2 % 2
                            sy.wait_ge(p2ce, 4 * ag2 + 4)
                            sy.dma_start(out=CXS.rearrange("(s p) d -> p s d", p=128)[:, 4 * qc2:4 * qc2 + 4, 128 * (4 * g2 + qh2):128 * (4 * g2 + qh2 + 1)],
                                         in_=cbig[ag2 % 2][:]).then_inc(p2cs[ag2 % 2], 16)
                        if ag >= 1:
                            for u2 in dq_sched[ag - 1]:
                                if u2 % 2 != 0:
                                    continue
                                mi2, g2_, off2, wdt2 = dqu[u2]
                                sy.wait_ge(p4tt, u2 + 1)
                                sy.dma_start(out=dq_ws[mi2][128 * g2_:128 * (g2_ + 1), off2:off2 + wdt2],
                                             in_=wout[u2 % 24][:, 0:wdt2]).then_inc(p4st[u2 % 24], 16)
                        for u in dq_sched[ag]:
                            mi, g_, off, wdt = dqu[u]
                            if u >= 4:
                                sy.wait_ge(p4un, u - 3)
                            sy.dma_start(out=qwt[u % 4][:, 0:wdt // 8],
                                         in_=dq_qw[mi].rearrange("(gg p) ccc -> p gg ccc", p=128)[:, g_, off // 8:(off + wdt) // 8]).then_inc(l4q[u % 4], 16)
                    for u2 in dq_sched[63]:
                        if u2 % 2 != 0:
                            continue
                        mi2, g2_, off2, wdt2 = dqu[u2]
                        sy.wait_ge(p4tt, u2 + 1)
                        sy.dma_start(out=dq_ws[mi2][128 * g2_:128 * (g2_ + 1), off2:off2 + wdt2],
                                     in_=wout[u2 % 24][:, 0:wdt2]).then_inc(p4st[u2 % 24], 16)
                    for ag2 in [62, 63]:
                        g2, qh2, qc2 = ag2 // 8, (ag2 // 2) % 4, ag2 % 2
                        sy.wait_ge(p2ce, 4 * ag2 + 4)
                        sy.dma_start(out=CXS.rearrange("(s p) d -> p s d", p=128)[:, 4 * qc2:4 * qc2 + 4, 128 * (4 * g2 + qh2):128 * (4 * g2 + qh2 + 1)],
                                     in_=cbig[ag2 % 2][:]).then_inc(p2cs[ag2 % 2], 16)
                    for par in range(2):
                        sy.wait_ge(p2cs[par], 16 * 32)
                    for par in range(24):
                        sy.wait_ge(p4st[par], 16 * len([u for u in range(NDQ) if u % 24 == par]))

                @b.gpsimd
                def _(g_e):
                    g_e.dma_start(out=onerb[:], in_=ONE).then_inc(msv, 16)
                    g_e.memset(vg[0][:, :, 128:129], 1.0).then_inc(msm, 1)
                    g_e.memset(vg[1][:, :, 128:129], 1.0).then_inc(msm, 1)
                    for u in range(NDQ):
                        mi, g_, off, wdt = dqu[u]
                        if u >= 2:
                            g_e.wait_ge(p4bc, u - 1)
                        g_e.dma_start(out=sbr[u % 2][:, 0:wdt],
                                      in_=dq_ss[mi][g_:g_ + 1, off:off + wdt]).then_inc(l4s[u % 2], 16)

                @b.tensor
                def _(t):
                    t.wait_ge(msv, 16)
                    t.wait_ge(msm, 2)
                    t.wait_ge(l2c, 32)
                    pend = None
                    for ag in range(64):
                        g, qh, qc = ag // 8, (ag // 2) % 4, ag % 2
                        h = 4 * g + qh
                        for kt in range(32):
                            n = 32 * ag + kt
                            if kt == 0:
                                t.wait_ge(l2k[g % 2], 16 * (g // 2 + 1))
                                t.wait_ge(l2v[g % 2], 16 * (g // 2 + 1))
                                t.wait_ge(l2q[h % 2], 16 * (h // 2 + 1))
                            if n >= 2:
                                t.wait_ge(p2ex, n - 1)
                            t.matmul(scp[n % 2][:], ktg[g % 2][:, 128 * kt:128 * (kt + 1)],
                                     qth[h % 2][:, 512 * qc:512 * (qc + 1)],
                                     start=True, stop=True).then_inc(p2mm, 1)
                            if pend is not None:
                                _mega_av(t, pend, expb, vg, ctp, p2ex, p2ce, p2av)
                            pend = n
                        _mega_av(t, pend, expb, vg, ctp, p2ex, p2ce, p2av)
                        pend = None
                        for u in dq_sched[ag]:
                            mi, g_, off, wdt = dqu[u]
                            t.wait_ge(l4s[u % 2], 16 * (u // 2 + 1))
                            if u >= 1:
                                t.wait_ge(p4tt, u)
                            nch = (wdt + 511) // 512
                            for ci in range(nch):
                                cw = min(512, wdt - 512 * ci)
                                inst = t.matmul(psS[:, 512 * ci:512 * ci + cw], onerb[:],
                                                sbr[u % 2][:, 512 * ci:512 * ci + cw], start=True, stop=True)
                            inst.then_inc(p4bc, 1)

                @b.scalar
                def _(s):
                    for ag in range(65):
                        if ag >= 1:
                            for u2 in dq_sched[ag - 1]:
                                if u2 % 2 != 1:
                                    continue
                                mi2, g2_, off2, wdt2 = dqu[u2]
                                s.wait_ge(p4tt, u2 + 1)
                                s.dma_start(out=dq_ws[mi2][128 * g2_:128 * (g2_ + 1), off2:off2 + wdt2],
                                            in_=wout[u2 % 24][:, 0:wdt2]).then_inc(p4st[u2 % 24], 16)
                        if ag == 64:
                            break
                        g, qh, qc = ag // 8, (ag // 2) % 4, ag % 2
                        for kt in range(32):
                            n = 32 * ag + kt
                            if n >= 2:
                                s.wait_ge(p2av, n - 1)
                            if window(qc, kt):
                                s.wait_ge(p2vm, wm_idx[n] + 1)
                                s.activation(expb[n % 2][:], sxp[n % 2][:], AF.Exp).then_inc(p2ex, 1)
                            else:
                                s.wait_ge(p2mm, n + 1)
                                s.activation(expb[n % 2][:], scp[n % 2][:], AF.Exp,
                                             bias=biat[:, 32 * qc + kt:32 * qc + kt + 1]).then_inc(p2ex, 1)
                        s.wait_ge(p2rc, 4 * (ag + 1))
                        if ag >= 2:
                            s.wait_ge(p2cs[ag % 2], 16 * (ag // 2))
                        for qs4 in range(4):
                            s.activation(cbig[ag % 2][:, qs4, :], ctp[qs4][:, 0:128], AF.Copy,
                                         scale=rcp[:, qs4:qs4 + 1]).then_inc(p2ce, 1)


                @b.vector
                def _(v):
                    v.wait_ge(l2c, 32)
                    for ag in range(64):
                        g, qh, qc = ag // 8, (ag // 2) % 4, ag % 2
                        for kt in range(32):
                            if not window(qc, kt):
                                continue
                            n = 32 * ag + kt
                            r = kt - 4 * qc
                            v.wait_ge(p2mm, n + 1)
                            if n >= 2:
                                v.wait_ge(p2ex, n - 1)
                            v.tensor_tensor(sxp[n % 2][:], scp[n % 2][:], m4s[:, r, :], ALU.add).then_inc(p2vm, 1)
                        for u in dq_sched[ag]:
                            mi, g_, off, wdt = dqu[u]
                            nb = wdt // 128
                            v.wait_ge(l4q[u % 4], 16 * (u // 4 + 1))
                            if u >= 1:
                                v.wait_ge(p4tt, u)
                            qview = qwt[u % 4][:, 0:wdt // 8].rearrange("p (nb c) -> p nb c", c=16)
                            nview = nibi[:, 0:wdt].rearrange("p (nb c) -> p nb c", c=128)
                            for pl in range(8):
                                inst = v.tensor_scalar(nview[:, :, 16 * pl:16 * (pl + 1)], qview,
                                                       4 * ARO[pl], 15,
                                                       ALU.logical_shift_right, ALU.bitwise_and)
                            inst.then_inc(p4un, 1)
                            v.wait_ge(p4un, u + 1)
                            v.wait_ge(p4bc, u + 1)
                            if u >= 24:
                                v.wait_ge(p4st[u % 24], 16 * (u // 24))
                            v.scalar_tensor_tensor(wout[u % 24][:, 0:wdt], nibi[:, 0:wdt], -8.0,
                                                   psS[:, 0:wdt], ALU.add, ALU.mult).then_inc(p4tt, 1)
                        v.wait_ge(p2av, 32 * (ag + 1))
                        if ag >= 1:
                            v.wait_ge(p2ce, 4 * ag)
                        for qs4 in range(4):
                            v.reciprocal(rcp[:, qs4:qs4 + 1], ctp[qs4][:, 128:129]).then_inc(p2rc, 1)

        if upto <= 5:
            return nc

        with ExitStack() as st3:
            sb = lambda *a: st3.enter_context(nc.sbuf_tensor(*a))
            _cst[0] = st3
            cxr = [sb(f"p3cx{i}", [128, D], BF16) for i in range(2)]
            cxstg = [sb(f"p3st{i}", [128, 32, 128], BF16) for i in range(2)]
            ptp3 = [st3.enter_context(nc.psum_tensor(f"p3tp{i}", [128, 128], BF16)) for i in range(2)]
            l3 = sem2("p3l")
            tp3 = sem("p3tp")
            tc3 = sem("p3tc")
            s3g = sem2("p3sg")
            with nc.named_scope("P3"), nc.Block() as b:
                @b.sync
                def _(sy):
                    for qt in range(8):
                        if qt >= 2:
                            sy.wait_ge(tp3, 32 * (qt - 1))
                        sy.dma_start(out=cxr[qt % 2][:], in_=CXS[128 * qt:128 * (qt + 1), :]).then_inc(l3[qt % 2], 16)
                        if qt >= 2:
                            sy.wait_ge(tc3, 32 * (qt - 1))
                            sy.dma_start(out=CTS.rearrange("(j p) t -> p j t", p=128)[:, :, 128 * (qt - 2):128 * (qt - 1)],
                                         in_=cxstg[qt % 2][:]).then_inc(s3g[qt % 2], 16)
                    for qt in [6, 7]:
                        sy.wait_ge(tc3, 32 * (qt + 1))
                        sy.dma_start(out=CTS.rearrange("(j p) t -> p j t", p=128)[:, :, 128 * qt:128 * (qt + 1)],
                                     in_=cxstg[qt % 2][:]).then_inc(s3g[qt % 2], 16)
                    for par in range(2):
                        sy.wait_ge(s3g[par], 16 * 4)
                @b.tensor
                def _(t):
                    for qt in range(8):
                        t.wait_ge(l3[qt % 2], 16 * (qt // 2 + 1))
                        for j in range(32):
                            n = 32 * qt + j
                            if n >= 2:
                                t.wait_ge(tc3, n - 1)
                            t.transpose(ptp3[n % 2][:], cxr[qt % 2][:, 128 * j:128 * (j + 1)], eyeb[:]).then_inc(tp3, 1)
                @b.vector
                def _(v):
                    for qt in range(8):
                        for j in range(32):
                            n = 32 * qt + j
                            v.wait_ge(tp3, n + 1)
                            if qt >= 2 and j == 0:
                                v.wait_ge(s3g[qt % 2], 16 * (qt // 2))
                            v.tensor_copy(cxstg[qt % 2][:, j, :], ptp3[n % 2][:]).then_inc(tc3, 1)

        if upto <= 6:
            return nc

        z_chunks = [(GZ, GS, ZSG, 32, 688, 5504, 0), (GZ, GS, ZSG, 32, 688, 5504, 5504),
                    (UZ, US, ZSU, 32, 688, 5504, 0), (UZ, US, ZSU, 32, 688, 5504, 5504),
                    (DZ, DS, ZSD, 86, 512, 4096, 0)]
        with ExitStack() as stz:
            sb = lambda *a: stz.enter_context(nc.sbuf_tensor(*a))
            _cst[0] = stz
            zqc = sb("pzq", [86, 688], I32)
            nzc = sb("pzn", [86, 5504], I32)
            ssc = sb("pzs", [86, 5504], F32)
            zoc = sb("pzo", [86, 5504], BF16)
            z3l = sem("pz3l")
            z3u = sem("pz3u")
            z3n = sem("pz3n")
            z3t = sem("pz3t")
            z3s = sem("pz3s")
            with nc.named_scope("P3z"), nc.Block() as b:
                @b.sync
                def _(sy):
                    for ci, (zsrc, ssrc, dst, P, NI, NC, off) in enumerate(z_chunks):
                        if ci >= 1:
                            sy.wait_ge(z3u, 8 * ci)
                            sy.wait_ge(z3t, ci)
                        sy.dma_start(out=zqc[0:P, 0:NI], in_=zsrc[:, off // 8:off // 8 + NI]).then_inc(z3l, 16)
                        sy.dma_start(out=ssc[0:P, 0:NC], in_=ssrc[:, off:off + NC]).then_inc(z3l, 16)
                        sy.wait_ge(z3t, ci + 1)
                        sy.dma_start(out=dst[:, off:off + NC], in_=zoc[0:P, 0:NC]).then_inc(z3s, 16)
                    sy.wait_ge(z3s, 16 * len(z_chunks))
                @b.vector
                def _(v):
                    for ci, (zsrc, ssrc, dst, P, NI, NC, off) in enumerate(z_chunks):
                        v.wait_ge(z3l, 32 * (ci + 1))
                        if ci >= 1:
                            v.wait_ge(z3t, ci)
                        qv = zqc[0:P, 0:NI].rearrange("p (nb c) -> p nb c", c=16)
                        nv = nzc[0:P, 0:NC].rearrange("p (nb c) -> p nb c", c=128)
                        for pl in range(8):
                            inst = v.tensor_scalar(nv[:, :, 16 * pl:16 * (pl + 1)], qv,
                                                   4 * ARO[pl], 15,
                                                   ALU.logical_shift_right, ALU.bitwise_and)
                        inst.then_inc(z3u, 8)
                        v.tensor_scalar(ssc[0:P, 0:NC], ssc[0:P, 0:NC], -1.0, None, ALU.mult).then_inc(z3n, 1)
                        v.wait_ge(z3u, 8 * ci + 8)
                        v.wait_ge(z3n, ci + 1)
                        if ci >= 1:
                            v.wait_ge(z3s, 16 * ci)
                        v.scalar_tensor_tensor(zoc[0:P, 0:NC], nzc[0:P, 0:NC], -8.0,
                                               ssc[0:P, 0:NC], ALU.add, ALU.mult).then_inc(z3t, 1)

        if upto <= 7:
            return nc
        _phreset()

        with ExitStack() as st5:
            sb = lambda *a: st5.enter_context(nc.sbuf_tensor(*a))
            _cst[0] = st5
            xq5 = sb("p5xq", [128, 32, T], BF16)
            cselb = sb("p5cs", [128, 256], BF16)
            xgs = sb("p5xg", [32, T], BF16)
            wg5 = [sb(f"p5wg{i}", [128, 32, 128], BF16) for i in range(2)]
            wu5 = [sb(f"p5wu{i}", [128, 32, 128], BF16) for i in range(2)]
            zg5 = [sb(f"p5zg{i}", [32, 128], BF16) for i in range(2)]
            zu5 = [sb(f"p5zu{i}", [32, 128], BF16) for i in range(2)]
            sg5 = [sb(f"p5sg{i}", [128, 512], F32) for i in range(2)]
            ht5 = [sb(f"p5ht{i}", [128, 512], BF16) for i in range(2)]
            psG = [st5.enter_context(nc.psum_tensor(f"p5pG{i}", [128, 512], F32)) for i in range(2)]
            psU = [st5.enter_context(nc.psum_tensor(f"p5pU{i}", [128, 512], F32)) for i in range(2)]
            pxg = st5.enter_context(nc.psum_tensor("p5px", [32, 512], F32))
            l5x = sem("p5lx")
            l5gw = sem2("p5lg")
            l5uw = sem2("p5lu")
            s5ht = sem2("p5sh")
            m5g = sem("p5mg")
            m5xg = sem("p5mx")
            x5e = sem("p5xe")
            s5gs = sem("p5sgs")
            h5v = sem("p5hv")
            with nc.named_scope("P5a"), nc.Block() as b:
                @b.sync
                def _(sy):
                    sy.dma_start(out=xq5[:], in_=XNS.rearrange("(j p) t -> p j t", p=128)[:, :, 0:T]).then_inc(l5x, 16)
                    sy.dma_start(out=cselb[:], in_=CSEL).then_inc(l5x, 16)
                    for gi in range(172):
                        nt, th = gi // 2, gi % 2
                        sy.wait_ge(h5v, gi + 1)
                        sy.dma_start(out=HTS[128 * nt:128 * (nt + 1), 512 * th:512 * (th + 1)],
                                     in_=ht5[gi % 2][:]).then_inc(s5ht[gi % 2], 16)
                    for par in range(2):
                        sy.wait_ge(s5ht[par], 16 * 86)
                @b.gpsimd
                def _(g):
                    for nt in range(86):
                        if nt >= 2:
                            g.wait_ge(m5g, 4 * (nt - 1))
                        g.dma_start(out=wg5[nt % 2][:],
                                    in_=WGSc.rearrange("(j p) n -> p j n", p=128)[:, :, 128 * nt:128 * (nt + 1)]).then_inc(l5gw[nt % 2], 16)
                        g.dma_start(out=zg5[nt % 2][:], in_=ZSG[:, 128 * nt:128 * (nt + 1)]).then_inc(l5gw[nt % 2], 16)
                        g.dma_start(out=wu5[nt % 2][:],
                                    in_=WUSc.rearrange("(j p) n -> p j n", p=128)[:, :, 128 * nt:128 * (nt + 1)]).then_inc(l5uw[nt % 2], 16)
                        g.dma_start(out=zu5[nt % 2][:], in_=ZSU[:, 128 * nt:128 * (nt + 1)]).then_inc(l5uw[nt % 2], 16)
                @b.tensor
                def _(t):
                    t.wait_ge(l5x, 32)
                    for th in range(2):
                        if th == 1:
                            t.wait_ge(x5e, 1)
                        for j in range(32):
                            inst = t.matmul(pxg[:], cselb[:, 127 - j:159 - j],
                                            xq5[:, j, 512 * th:512 * (th + 1)],
                                            start=(j == 0), stop=(j == 31))
                        inst.then_inc(m5xg, 1)
                    t.wait_ge(x5e, 2)
                    for gi in range(172):
                        nt, th = gi // 2, gi % 2
                        sl = slice(512 * th, 512 * (th + 1))
                        if th == 0:
                            t.wait_ge(l5gw[nt % 2], 32 * (nt // 2 + 1))
                            t.wait_ge(l5uw[nt % 2], 32 * (nt // 2 + 1))
                        if gi >= 2:
                            t.wait_ge(s5gs, gi - 1)
                            t.wait_ge(h5v, gi - 1)
                        for j in range(32):
                            t.matmul(psG[gi % 2][:], wg5[nt % 2][:, j, :], xq5[:, j, sl],
                                     start=(j == 0), stop=False)
                        t.matmul(psG[gi % 2][:], zg5[nt % 2][:], xgs[:, sl],
                                 start=False, stop=True).then_inc(m5g, 1)
                        for j in range(32):
                            t.matmul(psU[gi % 2][:], wu5[nt % 2][:, j, :], xq5[:, j, sl],
                                     start=(j == 0), stop=False)
                        t.matmul(psU[gi % 2][:], zu5[nt % 2][:], xgs[:, sl],
                                 start=False, stop=True).then_inc(m5g, 1)
                @b.scalar
                def _(s):
                    for th in range(2):
                        s.wait_ge(m5xg, th + 1)
                        s.activation(xgs[:, 512 * th:512 * (th + 1)], pxg[:], AF.Copy).then_inc(x5e, 1)
                    for gi in range(172):
                        s.wait_ge(m5g, 2 * gi + 1)
                        if gi >= 2:
                            s.wait_ge(h5v, gi - 1)
                        s.activation(sg5[gi % 2][:], psG[gi % 2][:], AF.Silu).then_inc(s5gs, 1)
                @b.vector
                def _(v):
                    for gi in range(172):
                        v.wait_ge(m5g, 2 * gi + 2)
                        v.wait_ge(s5gs, gi + 1)
                        if gi >= 2:
                            v.wait_ge(s5ht[gi % 2], 16 * (gi // 2))
                        v.tensor_tensor(ht5[gi % 2][:], sg5[gi % 2][:], psU[gi % 2][:], ALU.mult).then_inc(h5v, 1)

        if upto <= 8:
            return nc

        st5o = ExitStack()
        st5o.__enter__()
        g0h = st5o.enter_context(nc.sbuf_tensor("p5g0", [128, 32, T], BF16))
        with ExitStack() as st5e:
            sb = lambda *a: st5e.enter_context(nc.sbuf_tensor(*a))
            _cst[0] = st5e
            hth = sb("p5hth", [128, 86, 512], BF16)
            cse2 = sb("p5cs2", [128, 256], BF16)
            hgs = sb("p5hgs", [86, 512], BF16)
            wd = [sb(f"p5wd{i}", [128, 86, 128], BF16) for i in range(2)]
            zd5 = [sb(f"p5zd{i}", [86, 128], BF16) for i in range(2)]
            psD = st5e.enter_context(nc.psum_tensor("p5pD", [128, 512], F32))
            pshg = st5e.enter_context(nc.psum_tensor("p5phg", [86, 512], F32))
            l5h = sem("p5lh")
            l5c2 = sem("p5lc2")
            e5wd = sem2("p5ewd")
            m5d = sem("p5md")
            g5l = sem("p5gl")
            m5hg = sem("p5mhg")
            h5ge = sem("p5hge")
            with nc.named_scope("P5d"), nc.Block() as b:
                @b.sync
                def _(sy):
                    sy.dma_start(out=cse2[:], in_=CSEL[:, 0:256]).then_inc(l5c2, 16)
                    for hh in range(2):
                        if hh >= 1:
                            sy.wait_ge(m5d, 32 * hh)
                        sy.dma_start(out=hth[:], in_=HTS.rearrange("(j p) t -> p j t", p=128)[:, :, 512 * hh:512 * (hh + 1)]).then_inc(l5h, 16)
                    sy.wait_ge(g5l, 64)
                @b.gpsimd
                def _(g):
                    for hh in range(2):
                        for ot in range(32):
                            u = 32 * hh + ot
                            if u >= 2:
                                g.wait_ge(m5d, u - 1)
                            g.dma_start(out=wd[ot % 2][:],
                                        in_=WDSc.rearrange("(j p) n -> p j n", p=128)[:, :, 128 * ot:128 * (ot + 1)]).then_inc(e5wd[ot % 2], 16)
                            g.dma_start(out=zd5[ot % 2][:], in_=ZSD[:, 128 * ot:128 * (ot + 1)]).then_inc(e5wd[ot % 2], 16)
                @b.tensor
                def _(t):
                    for hh in range(2):
                        t.wait_ge(l5h, 16 * (hh + 1))
                        if hh == 0:
                            t.wait_ge(l5c2, 16)
                        else:
                            t.wait_ge(h5ge, hh)
                        for j in range(86):
                            inst = t.matmul(pshg[:], cse2[:, 127 - j:213 - j], hth[:, j, :],
                                            start=(j == 0), stop=(j == 85))
                        inst.then_inc(m5hg, 1)
                        t.wait_ge(h5ge, hh + 1)
                        for ot in range(32):
                            u = 32 * hh + ot
                            t.wait_ge(e5wd[ot % 2], 32 * (u // 2 + 1))
                            if u >= 1:
                                t.wait_ge(g5l, u)
                            for j in range(86):
                                t.matmul(psD[:], wd[ot % 2][:, j, :], hth[:, j, :],
                                         start=(j == 0), stop=False)
                            t.matmul(psD[:], zd5[ot % 2][:], hgs[:],
                                     start=False, stop=True).then_inc(m5d, 1)
                @b.scalar
                def _(s):
                    for hh in range(2):
                        s.wait_ge(m5hg, hh + 1)
                        if hh >= 1:
                            s.wait_ge(m5d, 32 * hh)
                        s.activation(hgs[:], pshg[:], AF.Copy).then_inc(h5ge, 1)
                        for ot in range(32):
                            u = 32 * hh + ot
                            s.wait_ge(m5d, u + 1)
                            s.activation(g0h[:, ot, 512 * hh:512 * (hh + 1)], psD[:], AF.Silu).then_inc(g5l, 1)

        with ExitStack() as st5w:
            sb = lambda *a: st5w.enter_context(nc.sbuf_tensor(*a))
            _cst[0] = st5w
            cxh5 = sb("p5cxh", [128, 32, 512], BF16)
            x1bh = sb("p5x1b", [128, 32, 512], BF16)
            wo5 = [sb(f"p5wo{i}", [128, 32, 128], BF16) for i in range(2)]
            xt5 = [sb(f"p5xt{i}", [128, 512], F32) for i in range(2)]
            s05 = [sb(f"p5s0{i}", [128, 512], F32) for i in range(2)]
            tsb = sb("p5tsb", [128, 512], F32)
            x1s = [sb(f"p5x1{i}", [128, 512], F32) for i in range(2)]
            x1q = [sb(f"p5x1q{i}", [128, 512], BF16) for i in range(2)]
            xn1 = [sb(f"p5xn{i}", [128, 512], BF16) for i in range(2)]
            mrow = sb("p5mr", [1, 512], F32)
            t1row = sb("p5t1", [1, 512], F32)
            vrow = sb("p5vr", [1, 512], F32)
            hrow = sb("p5hr", [1, 512], F32)
            rstdrow = sb("p5rs", [1, 512], F32)
            nbrow = sb("p5nbr", [1, 512], F32)
            psS2 = st5w.enter_context(nc.psum_tensor("p5pS2", [128, 512], F32))
            psSum = st5w.enter_context(nc.psum_tensor("p5pSm", [1, 512], F32))
            psSq = st5w.enter_context(nc.psum_tensor("p5pSq", [1, 512], F32))
            psR = st5w.enter_context(nc.psum_tensor("p5pR", [128, 512], F32))
            psN = st5w.enter_context(nc.psum_tensor("p5pN", [128, 512], F32))
            l5cx = sem("p5lcx")
            e5wo = sem2("p5ewo")
            e5xp = sem2("p5exp")
            e5x1 = sem2("p5ex1")
            e5xn = sem2("p5exn")
            m5s = sem("p5ms")
            m5q = sem("p5mq")
            s5c0 = sem("p5sc0")
            s5xc = sem("p5sxc")
            v5v = sem("p5vv")
            v5w = sem("p5vw")
            s5ln = sem("p5sln")
            b5cm = sem("p5bcm")
            v5n = sem("p5vn")
            with nc.named_scope("P5w"), nc.Block() as b:
                @b.sync
                def _(sy):
                    for h2 in range(2):
                        if h2 >= 1:
                            sy.wait_ge(m5s, 32 * h2)
                        sy.dma_start(out=cxh5[:], in_=CTS.rearrange("(j p) t -> p j t", p=128)[:, :, 512 * h2:512 * (h2 + 1)]).then_inc(l5cx, 16)
                        for ot in range(32):
                            u = 32 * h2 + ot
                            if u >= 2:
                                sy.wait_ge(v5v, 2 * (u - 2) + 2)
                            sy.dma_start(out=xt5[ot % 2][:], in_=XTP[128 * ot:128 * (ot + 1), 512 * h2:512 * (h2 + 1)]).then_inc(e5xp[u % 2], 16)
                            if ot >= 2:
                                sy.dma_start(out=X1S[128 * (ot - 2):128 * (ot - 1), 512 * h2:512 * (h2 + 1)],
                                             in_=x1s[ot % 2][:]).then_inc(e5x1[ot % 2], 16)
                        for ot in [30, 31]:
                            sy.wait_ge(v5v, 2 * (32 * h2 + ot) + 2)
                            sy.dma_start(out=X1S[128 * ot:128 * (ot + 1), 512 * h2:512 * (h2 + 1)],
                                         in_=x1s[ot % 2][:]).then_inc(e5x1[ot % 2], 16)
                        for ot in range(32):
                            sy.wait_ge(v5n, 2 * (32 * h2 + ot) + 2)
                            sy.dma_start(out=XN1S[128 * ot:128 * (ot + 1), 512 * h2:512 * (h2 + 1)],
                                         in_=xn1[ot % 2][:]).then_inc(e5xn[ot % 2], 16)
                    for par in range(2):
                        sy.wait_ge(e5x1[par], 16 * 32)
                        sy.wait_ge(e5xn[par], 16 * 32)
                @b.gpsimd
                def _(g):
                    for h2 in range(2):
                        for ot in range(32):
                            u = 32 * h2 + ot
                            if u >= 2:
                                g.wait_ge(m5s, u - 1)
                            g.dma_start(out=wo5[ot % 2][:], in_=WOP[ot].rearrange("(j p) n -> p j n", p=128)).then_inc(e5wo[ot % 2], 16)
                @b.tensor
                def _(t):
                    for h2 in range(2):
                        t.wait_ge(l5cx, 16 * (h2 + 1))
                        if h2 >= 1:
                            t.wait_ge(v5w, 6 * h2)
                            t.wait_ge(v5n, 64 * h2)
                        for ot in range(34):
                            if ot < 32:
                                u = 32 * h2 + ot
                                t.wait_ge(e5wo[ot % 2], 16 * (u // 2 + 1))
                                if u >= 1:
                                    t.wait_ge(s5c0, u)
                                for j in range(32):
                                    inst = t.matmul(psS2[:], wo5[ot % 2][:, j, :],
                                                    cxh5[:, j, :],
                                                    start=(j == 0), stop=(j == 31))
                                inst.then_inc(m5s, 1)
                            if ot >= 2:
                                o2 = ot - 2
                                u2 = 32 * h2 + o2
                                t.wait_ge(s5xc, 2 * u2 + 2)
                                t.matmul(psSum[:], onecs[:], x1bh[:, o2, :],
                                         start=(o2 == 0), stop=(o2 == 31))
                                inst = t.matmul(psSq[:], onecs[:], x1q[o2 % 2][:],
                                                start=(o2 == 0), stop=(o2 == 31))
                                inst.then_inc(m5q, 1)
                        t.wait_ge(s5ln, 2 * h2 + 2)
                        t.wait_ge(v5w, 6 * h2 + 6)
                        t.matmul(psR[:], oner[:], rstdrow[:], start=True, stop=True)
                        inst = t.matmul(psN[:], oner[:], nbrow[:], start=True, stop=True)
                        inst.then_inc(b5cm, 1)
                @b.scalar
                def _(s):
                    for h2 in range(2):
                        for ot in range(32):
                            u = 32 * h2 + ot
                            s.wait_ge(m5s, u + 1)
                            if u >= 2:
                                s.wait_ge(v5v, 2 * (u - 2) + 1)
                            s.activation(s05[ot % 2][:], psS2[:], AF.Copy).then_inc(s5c0, 1)
                            s.wait_ge(v5v, 2 * u + 2)
                            if h2 >= 1 and ot == 0:
                                s.wait_ge(v5n, 64 * h2)
                            if u >= 2:
                                s.wait_ge(m5q, u - 1)
                            s.activation(x1bh[:, ot, :], x1s[ot % 2][:], AF.Copy)
                            inst = s.activation(x1q[ot % 2][:], x1s[ot % 2][:], AF.Square)
                            inst.then_inc(s5xc, 2)
                        s.wait_ge(v5w, 6 * h2 + 4)
                        s.activation(vrow[:], hrow[:], AF.Ln).then_inc(s5ln, 1)
                        s.wait_ge(v5w, 6 * h2 + 5)
                        s.activation(rstdrow[:], hrow[:], AF.Exp).then_inc(s5ln, 1)
                @b.vector
                def _(v):
                    for h2 in range(2):
                        for ot in range(32):
                            u = 32 * h2 + ot
                            v.wait_ge(s5c0, u + 1)
                            v.wait_ge(e5xp[u % 2], 16 * (u // 2 + 1))
                            if h2 >= 1 and ot == 0:
                                v.wait_ge(v5n, 64 * h2)
                            if u >= 1:
                                v.wait_ge(v5v, 2 * u)
                            v.tensor_tensor(tsb[:], g0h[:, ot, 512 * h2:512 * (h2 + 1)],
                                            s05[ot % 2][:], ALU.mult).then_inc(v5v, 1)
                            v.wait_ge(v5v, 2 * u + 1)
                            if u >= 2:
                                v.wait_ge(e5x1[u % 2], 16 * (u // 2))
                            v.tensor_tensor(x1s[ot % 2][:], tsb[:], xt5[ot % 2][:], ALU.add).then_inc(v5v, 1)
                        v.wait_ge(m5q, 32 * h2 + 32)
                        vb = 6 * h2
                        v.tensor_scalar(mrow[:], psSum[:], 1.0 / D, None, ALU.mult).then_inc(v5w, 1)
                        v.wait_ge(v5w, vb + 1)
                        v.tensor_tensor(t1row[:], mrow[:], mrow[:], ALU.mult).then_inc(v5w, 1)
                        v.wait_ge(v5w, vb + 2)
                        v.scalar_tensor_tensor(vrow[:], psSq[:], 1.0 / D, t1row[:], ALU.mult, ALU.subtract).then_inc(v5w, 1)
                        v.wait_ge(v5w, vb + 3)
                        v.tensor_scalar(hrow[:], vrow[:], EPS, None, ALU.add).then_inc(v5w, 1)
                        v.wait_ge(s5ln, 2 * h2 + 1)
                        v.tensor_scalar(hrow[:], vrow[:], -0.5, None, ALU.mult).then_inc(v5w, 1)
                        v.wait_ge(s5ln, 2 * h2 + 2)
                        v.scalar_tensor_tensor(nbrow[:], mrow[:], -1.0, rstdrow[:], ALU.mult, ALU.mult).then_inc(v5w, 1)
                        v.wait_ge(v5v, 64 * h2 + 64)
                        v.wait_ge(b5cm, h2 + 1)
                        v.wait_ge(s5xc, 2 * (32 * h2 + 31) + 2)
                        for ot in range(32):
                            u = 32 * h2 + ot
                            if ot >= 1:
                                v.wait_ge(v5n, 2 * u)
                            v.tensor_tensor(tsb[:], x1bh[:, ot, :], psR[:], ALU.mult).then_inc(v5n, 1)
                            v.wait_ge(v5n, 2 * u + 1)
                            if u >= 2:
                                v.wait_ge(e5xn[u % 2], 16 * (u // 2))
                            v.tensor_tensor(xn1[ot % 2][:], tsb[:], psN[:], ALU.add).then_inc(v5n, 1)
        st5o.__exit__(None, None, None)

        if upto <= 9:
            return nc
        _phreset()

        with ExitStack() as st6:
            sb = lambda *a: st6.enter_context(nc.sbuf_tensor(*a))
            _cst[0] = st6
            ps = lambda *a: st6.enter_context(nc.psum_tensor(*a))
            xnT6 = sb("p6xn", [128, 32, T], BF16)
            q1T = sb("p6q1", [128, 32, T], BF16)
            wq6 = [sb(f"p6wq{i}", [128, 32, 128], BF16) for i in range(2)]
            c1s = sb("p6c1", [128, T], F32)
            s1s = sb("p6s1", [128, T], F32)
            rt1 = sb("p6r1", [128, 512], F32)
            rt2 = sb("p6r2", [128, 512], F32)
            idx6 = sb("p6idx", [128, 2], I32)
            kgB = sb("p6kg", [128, 2, 16, 128], BF16)
            vstage = sb("p6vs", [128, 2048], BF16)
            vgB = sb("p6vg", [128, 2, 16, 129], BF16)
            kT1 = sb("p6kt", [128, 32, 128], BF16)
            expb6 = [sb(f"p6ex{i}", [128, 512], BF16) for i in range(2)]
            rcp6 = sb("p6rc", [128, 4], F32)
            cxg = [sb(f"p6cx{i}", [128, 32, 128], BF16) for i in range(2)]
            cxs6 = sb("p6cs", [128, 2, T], BF16)
            pmm = [ps(f"p6pm{i}", [128, 512], F32) for i in range(2)]
            ptp = ps("p6tp", [128, 128], BF16)
            ctp6 = [ps(f"p6ct{i}", [128, 129], F32) for i in range(4)]
            ptp2 = ps("p6tp2", [128, 128], BF16)
            l6d = sem("p6ld")
            l6xn = sem("p6lxn")
            l6wq = sem2("p6lwq")
            kf = sem2("p6kf")
            vfp = sem2("p6vfp")
            s6f = sem("p6sf")
            msd = sem("p6msd")
            mmp = sem("p6mmp")
            evp = sem("p6evp")
            vvp = sem("p6vvp")
            vcp = sem("p6vcp")
            tpf = sem("p6tpf")
            tcf = sem("p6tcf")
            mmf = sem("p6mmf")
            exf = sem("p6exf")
            avf = sem("p6avf")
            vrc = sem("p6vrc")
            cte = sem("p6cte")
            tp6 = sem("p6tp6")
            tc6 = sem("p6tc6")

            def av6(t, n):
                g, ti = divmod(n, 256)
                kt = ti % 32
                half, j = kt // 16, kt % 16
                t.wait_ge(exf, n + 1)
                t.wait_ge(vcp, 2 * g + half + 1)
                if kt == 0:
                    t.wait_ge(msd, 1)
                    gidx = n // 32
                    if gidx >= 1:
                        t.wait_ge(cte, gidx)
                for qs4 in range(4):
                    inst = t.matmul(ctp6[qs4][:], expb6[n % 2][:, 128 * qs4:128 * (qs4 + 1)],
                                    vgB[:, half, j, :], start=(kt == 0), stop=(kt == 31))
                inst.then_inc(avf, 1)

            def ctx6_transposes(t, gg):
                for sl in range(32):
                    n6 = 32 * gg + sl
                    if sl == 0:
                        t.wait_ge(cte, 8 * (gg + 1))
                    if n6 >= 1:
                        t.wait_ge(tc6, n6)
                    t.transpose(ptp2[:], cxg[gg % 2][:, sl, :], eyeb[:]).then_inc(tp6, 1)

            def ctx6_copies(v, gg):
                for sl in range(32):
                    n6 = 32 * gg + sl
                    v.wait_ge(tp6, n6 + 1)
                    if sl == 0 and gg >= 1:
                        v.wait_ge(s6f, 16 * 2 * gg)
                    if sl == 16:
                        v.wait_ge(s6f, 16 * (2 * gg + 1))
                    qh6, th6, q46 = sl // 8, (sl // 4) % 2, sl % 4
                    o6 = 512 * th6 + 128 * q46
                    v.tensor_copy(cxs6[:, qh6 % 2, o6:o6 + 128], ptp2[:]).then_inc(tc6, 1)

            with nc.named_scope("P6"), nc.Block() as b:
                @b.sync
                def _(sy):
                    sy.dma_start(out=xnT6[:], in_=XN1S.rearrange("(j p) t -> p j t", p=128)).then_inc(l6xn, 16)
                    sy.dma_start(out=c1s[:], in_=C1T).then_inc(l6d, 16)
                    sy.dma_start(out=s1s[:], in_=S1T).then_inc(l6d, 16)
                    sy.dma_start(out=idx6[:], in_=BTI).then_inc(l6d, 16)
                    for k6 in range(2 * NKV):
                        gg, hh = k6 // 2, k6 % 2
                        sy.wait_ge(tc6, 32 * gg + 16 * (hh + 1))
                        sy.dma_start(out=CX1S.rearrange("(j p) t -> p j t", p=128)[:, 4 * gg + 2 * hh:4 * gg + 2 * hh + 2, :],
                                     in_=cxs6[:].rearrange("p a t -> p (a t)")).then_inc(s6f, 16)
                    sy.wait_ge(s6f, 16 * 2 * NKV)
                @b.gpsimd
                def _(g_):
                    g_.memset(vgB[:, :, :, 128:129], 1.0).then_inc(msd, 1)
                    for pi in range(32):
                        if pi >= 2:
                            g_.wait_ge(mmp, 2 * pi - 2)
                        g_.dma_start(out=wq6[pi % 2][:], in_=WQ1[pi].rearrange("(j p) n -> p j n", p=128)).then_inc(l6wq[pi % 2], 16)
                    g_.wait_ge(l6d, 48)
                    kcf = KC.rearrange("b h j d -> b (h j d)")
                    vcf = VC.rearrange("b h j d -> b (h j d)")
                    for g in range(NKV):
                        if g >= 1:
                            g_.wait_ge(tpf, 32 * g)
                        for half in range(2):
                            g_.indirect_dma_start(
                                out=kgB[:, half, :, :].rearrange("p a b -> p (a b)"), out_offset=None,
                                in_=kcf, in_offset=IndirectOffsetOnAxis(idx6[:, half:half + 1], axis=0),
                                element_offset=g * 2048).then_inc(kf[half], 16)
                            if g >= 1 or half >= 1:
                                g_.wait_ge(vcp, 2 * g + half)
                            g_.indirect_dma_start(
                                out=vstage[:], out_offset=None,
                                in_=vcf, in_offset=IndirectOffsetOnAxis(idx6[:, half:half + 1], axis=0),
                                element_offset=g * 2048).then_inc(vfp[half], 16)
                @b.tensor
                def _(t):
                    t.wait_ge(l6xn, 16)
                    for G in range(64):
                        pi, th = G // 2, G % 2
                        if th == 0:
                            t.wait_ge(l6wq[pi % 2], 16 * (pi // 2 + 1))
                        if G >= 2:
                            t.wait_ge(evp, G - 1)
                        for j in range(32):
                            inst = t.matmul(pmm[G % 2][:], wq6[pi % 2][:, j, :],
                                            xnT6[:, j, 512 * th:512 * (th + 1)],
                                            start=(j == 0), stop=(j == 31))
                        inst.then_inc(mmp, 1)
                    pending = None
                    for g in range(NKV):
                        if pending is not None:
                            av6(t, pending)
                            pending = None
                        if g >= 1:
                            ctx6_transposes(t, g - 1)
                        for kt in range(32):
                            n_t = 32 * g + kt
                            t.wait_ge(kf[kt // 16], 16 * (g + 1))
                            if n_t >= 1:
                                t.wait_ge(tcf, n_t)
                            t.transpose(ptp[:], kgB[:, kt // 16, kt % 16, :], eyeb[:]).then_inc(tpf, 1)
                        for ti in range(256):
                            n = 256 * g + ti
                            qh, th, kt = ti // 64, (ti // 32) % 2, ti % 32
                            h = 4 * g + qh
                            if ti == 0:
                                t.wait_ge(tcf, 32 * (g + 1))
                                t.wait_ge(evp, 64)
                            if n >= 2:
                                t.wait_ge(exf, n - 1)
                            t.matmul(pmm[n % 2][:], kT1[:, kt, :], q1T[:, h, 512 * th:512 * (th + 1)],
                                     start=True, stop=True).then_inc(mmf, 1)
                            if ti >= 1:
                                av6(t, n - 1)
                        pending = 256 * (g + 1) - 1
                    av6(t, pending)
                    ctx6_transposes(t, NKV - 1)
                @b.scalar
                def _(s):
                    for n in range(256 * NKV):
                        if n >= 2:
                            s.wait_ge(avf, n - 1)
                        s.wait_ge(mmf, n + 1)
                        s.activation(expb6[n % 2][:], pmm[n % 2][:], AF.Exp).then_inc(exf, 1)
                        ti = n % 256
                        if ti % 32 == 31:
                            gidx = n // 32
                            g, qh, th = n // 256, ti // 64, (ti // 32) % 2
                            s.wait_ge(vrc, 4 * (gidx + 1))
                            if qh == 0 and th == 0 and g >= 2:
                                s.wait_ge(tp6, 32 * (g - 1))
                            for qs4 in range(4):
                                inst = s.activation(cxg[g % 2][:, qh * 8 + th * 4 + qs4, :],
                                                    ctp6[qs4][:, 0:128], AF.Copy,
                                                    scale=rcp6[:, qs4:qs4 + 1])
                            inst.then_inc(cte, 1)
                @b.vector
                def _(v):
                    v.wait_ge(l6d, 48)
                    for G in range(64):
                        pi, th = G // 2, G % 2
                        sl = slice(512 * th, 512 * (th + 1))
                        p = pmm[G % 2]
                        v.wait_ge(mmp, G + 1)
                        if G >= 1:
                            v.wait_ge(evp, G)
                        vb = 3 * G
                        v.tensor_tensor(rt1[:], p[:], c1s[:, sl], ALU.mult).then_inc(vvp, 1)
                        v.tensor_tensor(rt2[0:64, :], p[64:128, :], s1s[0:64, sl], ALU.mult).then_inc(vvp, 1)
                        v.tensor_tensor(rt2[64:128, :], p[0:64, :], s1s[64:128, sl], ALU.mult).then_inc(vvp, 1)
                        v.wait_ge(vvp, vb + 3)
                        v.tensor_tensor(q1T[:, pi, sl], rt1[:], rt2[:], ALU.add).then_inc(evp, 1)
                    for g in range(NKV):
                        if g >= 1:
                            ctx6_copies(v, g - 1)
                        for kt in range(32):
                            n_t = 32 * g + kt
                            v.wait_ge(tpf, n_t + 1)
                            if kt == 0 and g >= 1:
                                v.wait_ge(mmf, 256 * g)
                            v.tensor_copy(kT1[:, kt, :], ptp[:]).then_inc(tcf, 1)
                        for half in range(2):
                            v.wait_ge(vfp[half], 16 * (g + 1))
                            if g >= 1:
                                v.wait_ge(avf, 256 * g)
                            v.tensor_copy(vgB[:, half, :, 0:128],
                                          vstage[:].rearrange("p (a b) -> p a b", a=16)).then_inc(vcp, 1)
                        for gidx in range(8 * g, 8 * (g + 1)):
                            v.wait_ge(avf, 32 * (gidx + 1))
                            if gidx >= 1:
                                v.wait_ge(cte, gidx)
                            for qs4 in range(4):
                                v.reciprocal(rcp6[:, qs4:qs4 + 1], ctp6[qs4][:, 128:129]).then_inc(vrc, 1)
                    ctx6_copies(v, NKV - 1)

        if upto <= 10:
            return nc
        _phreset()

        with ExitStack() as st7:
            sb = lambda *a: st7.enter_context(nc.sbuf_tensor(*a))
            _cst[0] = st7
            ps = lambda *a: st7.enter_context(nc.psum_tensor(*a))
            xnT7 = sb("p7xn", [128, 32, T], BF16)
            cxT7 = sb("p7cx", [128, 32, T], BF16)
            wo7 = [sb(f"p7wo{i}", [128, 32, 128], BF16) for i in range(2)]
            wb7 = [sb("p7wb0", [128, 32, 128], BF16)]
            x1t7 = [sb(f"p7x1{i}", [128, 512], F32) for i in range(2)]
            bb7 = sb("p7bb", [128, 32], F32)
            wg87 = sb("p7wg8", [128, 32, 8], BF16)
            at7 = sb("p7at", [128, 32, 128], BF16)
            bt87 = sb("p7bt8", [128, D], BF16)
            e167 = sb("p7e16", [8, 128], BF16)
            eyef = sb("p7eyf", [8, 8], F32)
            lg7 = sb("p7lg", [8, T], F32)
            lgT = sb("p7lgT", [128, 8, 8], F32)
            mc = sb("p7mc", [128, 8], F32)
            msk1 = sb("p7mk1", [128, 8], F32)
            msk2 = sb("p7mk2", [128, 8], F32)
            l2c7 = sb("p7l2c", [128, 8], F32)
            r1 = sb("p7r1", [128, 8], F32)
            r2 = sb("p7r2", [128, 8], F32)
            routeb = sb("p7rb", [128, 8], BF16)
            routeT = sb("p7rT", [8, T], BF16)
            routeX = sb("p7rX", [128, T], BF16)
            hw7 = sb("p7hw", [128, T], BF16)
            g17 = [sb(f"p7g1{i}", [128, 512], F32) for i in range(2)]
            s17 = [sb(f"p7s1{i}", [128, 512], F32) for i in range(2)]
            tt27 = sb("p7tt2", [128, 512], F32)
            ob7 = [sb(f"p7ob{i}", [128, 512], F32) for i in range(2)]
            psL = ps("p7pL", [8, T], F32)
            psT = ps("p7pT", [128, 8], F32)
            psTb = ps("p7pTb", [8, 128], BF16)
            psRX = ps("p7pRX", [128, 512], F32)
            psH = ps("p7pH", [128, 512], F32)
            psR2 = ps("p7pR2", [128, 512], F32)
            psS3 = ps("p7pS3", [128, 512], F32)
            l7d = sem("p7ld")
            l7g = sem("p7lg2")
            l7wb = sem("p7lwb")
            l7wo = sem2("p7lwo")
            l7x1 = sem2("p7lx1")
            sow7 = sem2("p7sow")
            mml = sem("p7mml")
            vga = sem("p7vga")
            sga = sem("p7sga")
            tra = sem("p7tra")
            mmx = sem("p7mmx")
            mmh = sem("p7mmh")
            vhw = sem("p7vhw")
            mmr = sem("p7mmr")
            mmo = sem("p7mmo")
            sgl = sem("p7sgl")
            ss1 = sem("p7ss1")
            vvo = sem("p7vvo")
            VGA_CH = lambda tc: 9 + 9 * tc

            with nc.named_scope("P7"), nc.Block() as b:
                @b.sync
                def _(sy):
                    sy.dma_start(out=xnT7[:], in_=XN1S.rearrange("(j p) t -> p j t", p=128)).then_inc(l7d, 16)
                    sy.dma_start(out=cxT7[:], in_=CX1S.rearrange("(j p) t -> p j t", p=128)).then_inc(l7d, 16)
                    sy.dma_start(out=bb7[:], in_=BB).then_inc(l7d, 16)
                    sy.dma_start(out=eyef[:], in_=EYE[0:8, 0:8]).then_inc(l7d, 16)
                    for u in range(64):
                        ot, th = divmod(u, 2)
                        if u >= 2:
                            sy.wait_ge(vvo, 2 * (u - 2) + 2)
                        sy.dma_start(out=x1t7[u % 2][:], in_=X1S[128 * ot:128 * (ot + 1), 512 * th:512 * (th + 1)]).then_inc(l7x1[u % 2], 16)
                        if u >= 2:
                            u2 = u - 2
                            ot2, th2 = divmod(u2, 2)
                            sy.wait_ge(vvo, 2 * u2 + 2)
                            sy.dma_start(out=OUT[128 * ot2:128 * (ot2 + 1), 512 * th2:512 * (th2 + 1)],
                                         in_=ob7[u2 % 2][:]).then_inc(sow7[u2 % 2], 16)
                    for u2 in [62, 63]:
                        ot2, th2 = divmod(u2, 2)
                        sy.wait_ge(vvo, 2 * u2 + 2)
                        sy.dma_start(out=OUT[128 * ot2:128 * (ot2 + 1), 512 * th2:512 * (th2 + 1)],
                                     in_=ob7[u2 % 2][:]).then_inc(sow7[u2 % 2], 16)
                    for i in range(2):
                        sy.wait_ge(sow7[i], 16 * 32)
                @b.gpsimd
                def _(g):
                    g.dma_start(out=wg87[:], in_=WG8.rearrange("(j p) e -> p j e", p=128)).then_inc(l7g, 16)
                    g.dma_start(out=at7[:], in_=AT.rearrange("(j p) n -> p j n", p=128)).then_inc(l7g, 16)
                    g.dma_start(out=bt87[:], in_=BT8).then_inc(l7g, 16)
                    g.dma_start(out=e167[:], in_=E16).then_inc(l7g, 16)
                    for ot in range(32):
                        if ot >= 1:
                            g.wait_ge(mmr, 2 * ot)
                        g.dma_start(out=wb7[0][:], in_=WB1[ot].rearrange("(j p) n -> p j n", p=128)).then_inc(l7wb, 16)
                        if ot >= 2:
                            g.wait_ge(mmo, 2 * ot - 2)
                        g.dma_start(out=wo7[ot % 2][:], in_=WO1[ot].rearrange("(j p) n -> p j n", p=128)).then_inc(l7wo[ot % 2], 16)
                @b.tensor
                def _(t):
                    t.wait_ge(l7d, 64)
                    t.wait_ge(l7g, 16 * 4)
                    for th in range(2):
                        for j in range(32):
                            inst = t.matmul(psL[:, 512 * th:512 * (th + 1)], wg87[:, j, :],
                                            xnT7[:, j, 512 * th:512 * (th + 1)],
                                            start=(j == 0), stop=(j == 31))
                        inst.then_inc(mml, 1)
                    for tc in range(8):
                        t.wait_ge(vga, 1)
                        if tc >= 1:
                            t.wait_ge(vga, 1 + tc)
                        t.transpose(psT[:, 0:8], lg7[:, 128 * tc:128 * (tc + 1)], eyef[:]).then_inc(tra, 1)
                    for tc in range(8):
                        t.wait_ge(vga, VGA_CH(tc) + 8)
                        if tc >= 1:
                            t.wait_ge(vga, VGA_CH(tc - 1) + 9)
                        t.transpose(psTb[:], routeb[:], eyeb[:]).then_inc(tra, 1)
                    for th in range(2):
                        t.wait_ge(vga, 81 + th)
                        t.matmul(psRX[:], e167[:], routeT[:, 512 * th:512 * (th + 1)],
                                 start=True, stop=True).then_inc(mmx, 1)
                    for th in range(2):
                        t.wait_ge(vga, 83)
                        if th >= 1:
                            t.wait_ge(vhw, th)
                        for j in range(32):
                            inst = t.matmul(psH[:], at7[:, j, :], xnT7[:, j, 512 * th:512 * (th + 1)],
                                            start=(j == 0), stop=(j == 31))
                        inst.then_inc(mmh, 1)
                    t.wait_ge(vhw, 2)
                    for ot in range(32):
                        t.wait_ge(l7wb, 16 * (ot + 1))
                        for th in range(2):
                            u = 2 * ot + th
                            if u >= 1:
                                t.wait_ge(sgl, u)
                            for j in range(32):
                                t.matmul(psR2[:], wb7[0][:, j, :], xnT7[:, j, 512 * th:512 * (th + 1)],
                                         start=(j == 0), stop=False)
                            inst = t.matmul(psR2[:], bt87[:, 128 * ot:128 * (ot + 1)],
                                            hw7[:, 512 * th:512 * (th + 1)], start=False, stop=True)
                            inst.then_inc(mmr, 1)
                        t.wait_ge(l7wo[ot % 2], 16 * (ot // 2 + 1))
                        for th in range(2):
                            u = 2 * ot + th
                            if u >= 1:
                                t.wait_ge(ss1, u)
                            for j in range(32):
                                inst = t.matmul(psS3[:], wo7[ot % 2][:, j, :], cxT7[:, j, 512 * th:512 * (th + 1)],
                                                start=(j == 0), stop=(j == 31))
                            inst.then_inc(mmo, 1)
                @b.scalar
                def _(s):
                    for tc in range(8):
                        vb = VGA_CH(tc)
                        s.wait_ge(vga, vb + 6)
                        s.activation(mc[:, 3:4], mc[:, 2:3], AF.Sigmoid).then_inc(sga, 1)
                        s.wait_ge(sga, 3 * tc + 1)
                        s.activation(r1[:], msk1[:], AF.Copy, scale=mc[:, 3:4]).then_inc(sga, 1)
                        s.wait_ge(vga, vb + 7)
                        s.activation(r2[:], msk2[:], AF.Copy, scale=mc[:, 4:5]).then_inc(sga, 1)
                    for ot in range(32):
                        for th in range(2):
                            u = 2 * ot + th
                            s.wait_ge(mmr, u + 1)
                            if u >= 2:
                                s.wait_ge(vvo, 2 * (u - 2) + 1)
                            s.activation(g17[th][:], psR2[:], AF.Silu, bias=bb7[:, ot:ot + 1]).then_inc(sgl, 1)
                            s.wait_ge(mmo, u + 1)
                            if u >= 2:
                                s.wait_ge(vvo, 2 * (u - 2) + 1)
                            s.activation(s17[th][:], psS3[:], AF.Copy).then_inc(ss1, 1)
                @b.vector
                def _(v):
                    v.wait_ge(mml, 2)
                    v.tensor_copy(lg7[:], psL[:]).then_inc(vga, 1)
                    for tc in range(8):
                        v.wait_ge(tra, tc + 1)
                        v.tensor_copy(lgT[:, tc, :], psT[:, 0:8]).then_inc(vga, 1)
                    for tc in range(8):
                        vb = VGA_CH(tc)
                        x = lgT[:, tc, :]
                        if tc >= 1:
                            v.wait_ge(sga, 3 * tc)
                        v.reduce_max(mc[:, 0:1], x, axis=AX.X).then_inc(vga, 1)
                        v.wait_ge(vga, vb + 1)
                        v.tensor_scalar(msk1[:], x, mc[:, 0:1], None, ALU.is_ge).then_inc(vga, 1)
                        v.wait_ge(vga, vb + 2)
                        v.scalar_tensor_tensor(l2c7[:], msk1[:], NEG, x, ALU.mult, ALU.add).then_inc(vga, 1)
                        v.wait_ge(vga, vb + 3)
                        v.reduce_max(mc[:, 1:2], l2c7[:], axis=AX.X).then_inc(vga, 1)
                        v.wait_ge(vga, vb + 4)
                        v.tensor_scalar(msk2[:], l2c7[:], mc[:, 1:2], None, ALU.is_ge).then_inc(vga, 1)
                        v.tensor_tensor(mc[:, 2:3], mc[:, 0:1], mc[:, 1:2], ALU.subtract).then_inc(vga, 1)
                        v.wait_ge(sga, 3 * tc + 1)
                        v.tensor_scalar(mc[:, 4:5], mc[:, 3:4], -1.0, 1.0, ALU.mult, ALU.add).then_inc(vga, 1)
                        v.wait_ge(sga, 3 * tc + 3)
                        if tc >= 1:
                            v.wait_ge(tra, 8 + tc)
                        v.tensor_tensor(routeb[:], r1[:], r2[:], ALU.add).then_inc(vga, 1)
                        v.wait_ge(tra, 8 + tc + 1)
                        v.tensor_copy(routeT[:, 128 * tc:128 * (tc + 1)], psTb[:]).then_inc(vga, 1)
                    for th in range(2):
                        v.wait_ge(mmx, th + 1)
                        v.tensor_copy(routeX[:, 512 * th:512 * (th + 1)], psRX[:]).then_inc(vga, 1)
                    for th in range(2):
                        v.wait_ge(mmh, th + 1)
                        v.tensor_tensor(hw7[:, 512 * th:512 * (th + 1)], psH[:],
                                        routeX[:, 512 * th:512 * (th + 1)], ALU.mult).then_inc(vhw, 1)
                    for u in range(64):
                        ot, th = divmod(u, 2)
                        v.wait_ge(sgl, u + 1)
                        v.wait_ge(ss1, u + 1)
                        v.wait_ge(l7x1[u % 2], 16 * (u // 2 + 1))
                        if u >= 1:
                            v.wait_ge(vvo, 2 * u)
                        v.tensor_tensor(tt27[:], g17[th][:], s17[th][:], ALU.mult).then_inc(vvo, 1)
                        v.wait_ge(vvo, 2 * u + 1)
                        if u >= 2:
                            v.wait_ge(sow7[u % 2], 16 * (u // 2))
                        v.tensor_tensor(ob7[u % 2][:], tt27[:], x1t7[u % 2][:],
                                        ALU.add).then_inc(vvo, 1)

    return nc


def _mega_av(t, n, expb, vg, ctp, p2ex, p2ce, p2av):
    ag, kt = n // 32, n % 32
    g = ag // 8
    t.wait_ge(p2ex, n + 1)
    if kt == 0 and ag >= 1:
        t.wait_ge(p2ce, 4 * ag)
    for qs4 in range(4):
        inst = t.matmul(ctp[qs4][:], expb[n % 2][:, 128 * qs4:128 * (qs4 + 1)],
                        vg[g % 2][:, kt, :], start=(kt == 0), stop=(kt == 31))
    inst.then_inc(p2av, 1)


def plane_perm(ncols):
    pp = np.zeros(ncols, dtype=np.int64)
    for nb in range(ncols // 128):
        base = 128 * nb
        for pl in range(8):
            for ci in range(16):
                pp[base + 16 * pl + ci] = base + 8 * ci + pl
    return pp


def mega_inputs(L0_attn_x, L0_attn_wq, L0_attn_wk, L0_attn_wv, L0_attn_wo,
                L0_ffn_gate_qweight, L0_ffn_gate_scales, L0_ffn_gate_qzeros,
                L0_ffn_up_qweight, L0_ffn_up_scales, L0_ffn_up_qzeros,
                L0_ffn_down_qweight, L0_ffn_down_scales, L0_ffn_down_qzeros,
                L1_attn_k_cache, L1_attn_v_cache, L1_attn_Wq, L1_attn_Wo,
                L1_attn_block_table, L1_ffn_W_base, L1_ffn_b_base, L1_ffn_W_gate,
                L1_ffn_A, L1_ffn_B):
    import ml_dtypes
    f32 = np.float32
    x2 = np.asarray(L0_attn_x, dtype=f32).reshape(8192, D)
    pp11 = plane_perm(11008)
    pp4 = plane_perm(4096)
    eye = np.eye(128, dtype=f32)
    one_r = np.ones((1, 128), f32)
    one_c = np.ones((128, 1), f32)
    csel = np.zeros((128, 256), ml_dtypes.bfloat16)
    csel[:, 127] = 1
    e16 = np.zeros((8, 128), f32)
    for e in range(8):
        e16[e, 16 * e:16 * (e + 1)] = 1.0
    perm_d = np.concatenate([np.arange(0, HD, 2), np.arange(1, HD, 2)])
    colp_q = np.concatenate([HD * h + perm_d for h in range(NH)])
    colp_k = np.concatenate([HD * g + perm_d for g in range(NKV)])
    wq0 = np.asarray(L0_attn_wq, f32)
    wk0 = np.asarray(L0_attn_wk, f32)
    wv0 = cc(np.asarray(L0_attn_wv, f32))
    wq_t = cc(wq0[:, colp_q].reshape(D, NH, HD).transpose(1, 0, 2))
    wk_t = cc(wk0[:, colp_k].reshape(D, NKV, HD).transpose(1, 0, 2))
    wop = cc(np.asarray(L0_attn_wo, f32)[:, pp4].reshape(D, 32, 128).transpose(1, 0, 2))
    klc = np.arange(128)[:, None]
    qlc = np.arange(512)[None, :]
    m4 = np.zeros((128, 4, 512), f32)
    for r in range(4):
        m4[:, r, :] = np.where(128 * r + klc <= qlc, 0.0, NEG)
    m4 = cc(m4)
    gq = cc(np.asarray(L0_ffn_gate_qweight, np.int32))
    gs = cc(np.asarray(L0_ffn_gate_scales, f32)[:, pp11])
    gz = cc(np.asarray(L0_ffn_gate_qzeros, np.int32))
    uq = cc(np.asarray(L0_ffn_up_qweight, np.int32))
    us = cc(np.asarray(L0_ffn_up_scales, f32)[:, pp11])
    uz = cc(np.asarray(L0_ffn_up_qzeros, np.int32))
    dq = cc(np.asarray(L0_ffn_down_qweight, np.int32)[pp11, :])
    ds = cc(np.asarray(L0_ffn_down_scales, f32)[:, pp4])
    dz = cc(np.asarray(L0_ffn_down_qzeros, np.int32))
    wq1 = cc(np.asarray(L1_attn_Wq, f32).T[pp4].reshape(D, 32, 128).transpose(1, 0, 2))
    wo1 = cc(np.asarray(L1_attn_Wo, f32).T[:, pp4].reshape(D, 32, 128).transpose(1, 0, 2))
    kc = cc(np.asarray(L1_attn_k_cache, f32))
    vc = cc(np.asarray(L1_attn_v_cache, f32))
    bti = cc(np.asarray(L1_attn_block_table, np.int32).reshape(2, 128).T.copy())
    wb1 = cc(np.asarray(L1_ffn_W_base, f32).T[pp4][:, pp4].reshape(D, 32, 128).transpose(1, 0, 2))
    bb = cc(np.asarray(L1_ffn_b_base, f32)[pp4].reshape(32, 128).T)
    wg8 = cc(np.asarray(L1_ffn_W_gate, f32).T[pp4])
    at = cc(np.asarray(L1_ffn_A, f32).transpose(2, 0, 1).reshape(D, 128)[pp4])
    bt8 = cc(np.asarray(L1_ffn_B, f32).transpose(0, 2, 1).reshape(128, D)[:, pp4])
    inv = 10000.0 ** (-np.arange(64) / 64.0)
    sc = f32(1.0 / np.sqrt(HD))

    def tables(pos):
        ang = pos.astype(np.float64)[:, None] * inv[None, :]
        Ct = cc(np.concatenate([np.cos(ang), np.cos(ang)], 1).T.astype(f32))
        St = cc(np.concatenate([-np.sin(ang), np.sin(ang)], 1).T.astype(f32))
        return Ct, St

    maps = []
    for c in range(NCORES):
        b, rho = c // 4, c % 4
        xro = cc(np.roll(x2[S * b:S * (b + 1)], -T * rho, axis=0))
        xTp = cc(x2[T * c:T * (c + 1)].T[pp4])
        posq = np.arange(T * rho, T * rho + T)
        cq1, sq1 = tables(posq)
        posk = (np.arange(S) + T * rho) % S
        ckf, skf = tables(posk)
        c1t, s1t = tables(posq)
        bia = np.zeros((128, 64), f32)
        for qc in range(2):
            for kt in range(32):
                keep = (kt < 4 * qc) or (kt >= 32 - 8 * rho)
                win = (4 * qc <= kt < 4 * qc + 4)
                bia[:, 32 * qc + kt] = 0.0 if (keep or win) else NEG
        maps.append(dict(
            xro=xro, xTp=xTp, wq_t=wq_t, wk_t=wk_t, wv=wv0,
            cq1=cq1 * sc, sq1=sq1 * sc, ckf=ckf, skf=skf,
            bia=cc(bia), m4=m4, eye=eye, one=one_r, onec=one_c,
            wop=wop, gq=gq, gs=gs, gz=gz, uq=uq, us=us, uz=uz,
            dq=dq, ds=ds, dz=dz,
            wq1=wq1, c1t=c1t * sc, s1t=s1t * sc, kc=kc, vc=vc, bti=bti,
            wo1=wo1, wb1=wb1, bb=bb, wg8=wg8, at=at, bt8=bt8, e16=e16,
            csel=csel,
        ))
    return maps


def kernel_mega(**inputs):
    maps = mega_inputs(**inputs)
    res = run_bass_kernel_spmd(build_mega(), maps, core_ids=list(range(NCORES))).results
    pp4 = plane_perm(4096)
    out = np.empty((8192, D), dtype=np.float32)
    for c in range(NCORES):
        out[T * c:T * (c + 1)][:, pp4] = np.asarray(res[c]["outT"]).T
    return out.reshape(2, 4096, 4096)


def kernel(**inputs):
    return kernel_mega(**inputs)
